# version 08
# baseline (speedup 1.0000x reference)
import numpy as np
import ml_dtypes
import concourse.bass as bass
import concourse.mybir as mybir
from concourse.bass_utils import run_bass_kernel_spmd

F32 = mybir.dt.float32
BF16 = mybir.dt.bfloat16
I32 = mybir.dt.int32
U8 = mybir.dt.uint8
AF = mybir.ActivationFunctionType
ALU = mybir.AluOpType
AX = mybir.AxisListType
BF = ml_dtypes.bfloat16

D = 4096
S = 4096
B = 4
H = 32
I = 11008
SOWN = 2048
KC = 32
GN = 86
OC = 32
MT = 256
NMT = SOWN // MT
FB = 512
NFB = SOWN // FB
RD = float(1.0 / np.sqrt(128.0))
LNEPS = 1e-5
AWQ_L = [0, 2, 4, 6, 1, 3, 5, 7]

ENGINES = ("sync", "gpsimd", "tensor", "vector", "scalar")

DUMP = ()


def ceil_div(a, b):
    return (a + b - 1) // b


class Build:
    def __init__(self, nc):
        self.nc = nc
        self.ops = {e: [] for e in ENGINES}
        self.sem = {}
        self.cnt = {}
        for name in ("eT", "eA", "eV", "eP"):
            self.sem[name] = nc.alloc_semaphore(name)
            self.cnt[name] = 0
        self.bank_free = {}
        self.trace = {}
        self.phase = "init"
        self.ptrace = {}

    def emit(self, eng, fn, meta=None):
        self.ops[eng].append(fn)
        self.trace.setdefault(eng, []).append(meta or ("op", None, 0))
        self.ptrace.setdefault(eng, []).append(self.phase)

    def wait(self, eng, semname, val):
        if val is None or val <= 0:
            return
        s = self.sem[semname]
        self.emit(eng, lambda e, s=s, v=val: e.wait_ge(s, v),
                  meta=("wait", semname, val))

    def wait_ev(self, eng, ev):
        if ev:
            self.wait(eng, ev[0], ev[1])

    def dma(self, eng, out, in_, ch="misc", **kw):
        semname = "q_" + ch
        if semname not in self.sem:
            self.sem[semname] = self.nc.alloc_semaphore(semname)
            self.cnt[semname] = 0
        s = self.sem[semname]
        self.cnt[semname] += 16
        self.emit(eng, lambda e, o=out, i=in_, s=s, kw=kw:
                  e.dma_start(out=o, in_=i, **kw).then_inc(s, 16),
                  meta=("inc", semname, 16))
        return (semname, self.cnt[semname])

    def qnow(self, ch):
        return ("q_" + ch, self.cnt.get("q_" + ch, 0))

    def op(self, eng, fn, inc=None):
        if inc:
            s = self.sem[inc]
            self.cnt[inc] += 1
            v = self.cnt[inc]
            self.emit(eng, lambda e, fn=fn, s=s: fn(e).then_inc(s, 1),
                      meta=("inc", inc, 1))
            return (inc, v)
        self.emit(eng, lambda e, fn=fn: fn(e))
        return None

    def opb(self, eng, fn, inc):
        ev = self.op(eng, fn, inc=inc)
        self.wait_ev(eng, ev)
        return ev

    def now(self, semname):
        return (semname, self.cnt[semname])

    def pe_bank_wait(self, banks):
        for bk in banks:
            ev = self.bank_free.get(bk)
            self.wait_ev("tensor", ev)
            self.bank_free[bk] = None

    def check_deadlock(self):
        import collections
        ptr = {e: 0 for e in ENGINES}
        cnt = collections.defaultdict(int)
        n = {e: len(self.trace.get(e, [])) for e in ENGINES}
        progress = True
        while progress:
            progress = False
            for e in ENGINES:
                while ptr[e] < n[e]:
                    kind, sem, val = self.trace[e][ptr[e]]
                    if kind == "wait" and cnt[sem] < val:
                        break
                    if kind == "inc":
                        cnt[sem] += val
                    ptr[e] += 1
                    progress = True
        stuck = {e: (ptr[e], n[e]) for e in ENGINES if ptr[e] < n[e]}
        if stuck:
            print("DEADLOCK! stuck engines:", stuck)
            for e, (p, _) in stuck.items():
                kind, sem, val = self.trace[e][p]
                print(f"  {e}@{p}: waiting {sem} >= {val}, have {cnt[sem]}")
            return False
        return True


def build_program():
    nc = bass.Bass("TRN2", target_bir_lowering=False)

    xd = nc.dram_tensor("x", [SOWN, D], F32, kind="ExternalInput").ap()
    wT = {w: nc.dram_tensor(f"w{w}T", [D, D], F32, kind="ExternalInput").ap()
          for w in ("q", "k", "v")}
    bT = {w: nc.dram_tensor(f"b{w}t", [128, 32], F32, kind="ExternalInput").ap()
          for w in ("q", "k")}
    bvd = nc.dram_tensor("bv", [1, D], F32, kind="ExternalInput").ap()
    qw_d = {"g": nc.dram_tensor("gqw", [D, 1376], I32, kind="ExternalInput").ap(),
            "u": nc.dram_tensor("uqw", [D, 1376], I32, kind="ExternalInput").ap(),
            "d": nc.dram_tensor("dqw", [I, 512], I32, kind="ExternalInput").ap()}
    sc_d = {"g": nc.dram_tensor("gsc", [32, I], F32, kind="ExternalInput").ap(),
            "u": nc.dram_tensor("usc", [32, I], F32, kind="ExternalInput").ap(),
            "d": nc.dram_tensor("dsc", [86, D], F32, kind="ExternalInput").ap()}
    qz_d = {"g": nc.dram_tensor("gqz", [32, 1376], I32, kind="ExternalInput").ap(),
            "u": nc.dram_tensor("uqz", [32, 1376], I32, kind="ExternalInput").ap(),
            "d": nc.dram_tensor("dqz", [86, 512], I32, kind="ExternalInput").ap()}
    outd = nc.dram_tensor("out", [SOWN, D], F32, kind="ExternalOutput").ap()

    xn_s = nc.dram_tensor("xn_s", [SOWN, D], BF16, kind="Internal").ap()
    wq7 = nc.dram_tensor("wq7", [32, 128, 32, 128], BF16, kind="Internal").ap()
    wk7 = nc.dram_tensor("wk7", [32, 128, 32, 128], BF16, kind="Internal").ap()
    wv7 = nc.dram_tensor("wv7", [16, 128, 32, 256], BF16, kind="Internal").ap()
    w7 = {"q": wq7, "k": wk7, "v": wv7}
    v_h8 = nc.dram_tensor("v_h8", [32, SOWN, 128], BF16, kind="Internal").ap()
    ctx_tm = nc.dram_tensor("ctx_tm", [SOWN, D], BF16, kind="Internal").ap()
    g_s = nc.dram_tensor("g_s", [D, SOWN], BF16, kind="Internal").ap()
    Wgu_t = nc.dram_tensor("Wgu_t", [GN, 128, 2, KC, 128], BF16, kind="Internal").ap()
    Wg_t = Wgu_t[:, :, 0]
    Wu_t = Wgu_t[:, :, 1]
    Wd_t = nc.dram_tensor("Wd_t", [OC, 128, GN, 128], BF16, kind="Internal").ap()
    zs_s = {"g": nc.dram_tensor("zs_g", [32, I], F32, kind="Internal").ap(),
            "u": nc.dram_tensor("zs_u", [32, I], F32, kind="Internal").ap(),
            "d": nc.dram_tensor("zs_d", [86, D], F32, kind="Internal").ap()}

    dump = {}
    for name, src in (("xn", xn_s), ("v", v_h8), ("ctx", ctx_tm), ("g", g_s),
                      ("wg", Wg_t), ("wu", Wu_t), ("wd", Wd_t)):
        if name in DUMP:
            dump[name] = (nc.dram_tensor("dump_" + name, list(src.shape), BF16,
                                         kind="ExternalOutput").ap(), src)

    ones_bf = nc.alloc_sbuf_tensor("ones_bf", [128, 128], BF16)
    ones_f = nc.alloc_sbuf_tensor("ones_f", [128, 32], F32)
    bt_sb = {w: nc.alloc_sbuf_tensor(f"bt_{w}", [128, 32], F32) for w in ("q", "k")}
    bv_sb = nc.alloc_sbuf_tensor("bv_sb", [1, D], BF16)
    epsc = nc.alloc_sbuf_tensor("epsc", [128, 1], F32)
    ARENA_KB = 198
    arena = nc.alloc_sbuf_tensor("arena", [128, ARENA_KB * 1024], U8)

    def view(off_b, n, dtype):
        sz = {F32: 4, BF16: 2, I32: 4}[dtype]
        assert off_b + n * sz <= ARENA_KB * 1024, (off_b, n, dtype)
        return arena[:, off_b:off_b + n * sz].bitcast(dtype)

    K1 = 1024
    ln_x = [view(0, D, F32), view(16 * K1, D, F32)]
    ln_sq = view(32 * K1, D, BF16)
    ln_xn = [view(40 * K1, D, BF16), view(48 * K1, D, BF16)]
    ln_st = view(56 * K1, 16, F32)
    d0_qz = view(64 * K1, 1376, I32)
    d0_z = view(70 * K1, 1376, I32)
    d0_sc = view(76 * K1, 1376, F32)
    d0_zs = view(82 * K1, 1376, F32)

    pj_xn = [view(0, KC * MT, BF16), view(16 * K1, KC * MT, BF16)]
    pj_w = [view(32 * K1, KC * 128, BF16), view(40 * K1, KC * 128, BF16)]
    pj_wv = [view(48 * K1, KC * 256, BF16), view(64 * K1, KC * 256, BF16)]
    q_sl = view(80 * K1, H * MT, BF16)
    k_sl = view(96 * K1, H * MT, BF16)
    mx_v = [view(112 * K1, 64 * 128, BF16)]
    mx_p = [view(128 * K1, 512, BF16), view(128 * K1 + 1024, 512, BF16)]
    mx_pn = [view(130 * K1, 512, BF16), view(130 * K1 + 1024, 512, BF16)]
    mx_rr = [view(132 * K1, 512, F32), view(134 * K1, 512, F32)]
    mx_cx = [view(136 * K1, 16 * 128, BF16), view(140 * K1, 16 * 128, BF16)]
    pj_ev = [view(142 * K1 + o * 1024, 512, BF16) for o in range(2)]
    _o = 144 * K1
    dqoff = {}
    for nm, nb in (("qw0", 1376 * 4), ("qw1", 1376 * 4), ("t10", 512 * 4), ("srow", 2752 * 2), ("zrow", 2752 * 2),
                   ("nib", 2752 * 4), ("wout", 5504 * 2),
                   ("bc0", 1024 * 4), ("bc1", 1024 * 4)):
        dqoff[nm] = _o
        _o += nb
    assert _o <= ARENA_KB * K1, _o

    def dqv(name, n, dtype):
        sz = {F32: 4, BF16: 2, I32: 4}[dtype]
        return arena[:, dqoff[name]:dqoff[name] + n * sz].bitcast(dtype)

    dq_qw = [dqv("qw0", 1376, I32), dqv("qw1", 1376, I32)]
    dq_t1 = [dqv("t10", 512, F32), dqv("t10", 512, F32)]
    dq_srow = dqv("srow", 2752, BF16)
    dq_zrow = dqv("zrow", 2752, BF16)
    dq_nib = dqv("nib", 2752, I32)
    dq_wout = dqv("wout", 5504, BF16)
    dq_bc = [dqv("bc0", 1024, F32), dqv("bc1", 1024, F32)]

    ff_wd = [view(0, GN * 128, BF16), view(22 * K1, GN * 128, BF16)]
    ff_h = view(44 * K1, GN * 512, BF16)
    ffgu = [view(130 * K1, 2 * KC * 128, BF16), view(146 * K1, 2 * KC * 128, BF16)]
    ff_xn = view(162 * K1, KC * 512, BF16)
    ff_sil = [view(194 * K1, 512, BF16), view(194 * K1 + 1024, 512, BF16)]
    ff_g = [view(196 * K1, 512, BF16), view(196 * K1 + 1024, 512, BF16)]
    cb_x = [view(0, D, F32), view(16 * K1, D, F32)]
    cb_c = [view(32 * K1, D, BF16), view(40 * K1, D, BF16)]
    cb_g = [view(48 * K1, D, BF16), view(56 * K1, D, BF16)]
    cb_t = view(64 * K1, D, F32)
    cb_o = [view(80 * K1, D, F32), view(96 * K1, D, F32)]

    P = nc.alloc_psum_tensor("P", [128, 4096], F32)

    def bank(i, n=1):
        return P[:, i * 512:(i + n) * 512]

    b = Build(nc)

    def mm(out, lhsT, rhs, start=True, stop=True):
        return lambda e: e.matmul(out, lhsT, rhs, start=start, stop=stop)

    b.phase = "init"
    b.op("vector", lambda e: e.memset(ones_bf[:], 1.0))
    b.op("vector", lambda e: e.memset(ones_f[:], 1.0))
    ev_const = b.op("vector", lambda e: e.memset(epsc[:], LNEPS), inc="eV")
    b.wait_ev("scalar", ev_const)
    b.wait_ev("gpsimd", ev_const)
    b.wait_ev("tensor", ev_const)
    for w in ("q", "k"):
        ev_bt = b.dma("sync", bt_sb[w][:], bT[w], ch="bt")
    ev_bv = b.dma("gpsimd", bv_sb[:], bvd, ch="bt2")

    ev_wcast = {}
    for w in ("q", "k", "v"):
        nn = 256 if w == "v" else 128
        for kc in range(KC):
            ev_wcast[w] = b.dma(
                "gpsimd", w7[w][:, :, kc, :].rearrange("a p n -> p a n"),
                wT[w][kc * 128:(kc + 1) * 128, :].rearrange("p (a n) -> p a n", n=nn),
                ch="cast" + w)

    b.phase = "ln"
    c_inv = 1.0 / D
    ev_ln_act = [None, None]
    for i in range(16):
        bi = i % 2
        if ev_ln_act[bi]:
            b.wait_ev("sync", ev_ln_act[bi])
        ev_x = b.dma("sync", ln_x[bi], xd[i * 128:(i + 1) * 128, :], ch="lnx%d" % bi)
        st = ln_st
        b.wait_ev("vector", ev_x)
        b.opb("vector", lambda e, o=st[:, 0:1], i2=ln_x[bi]:
              e.reduce_sum(o, i2, axis=AX.X), inc="eV")
        b.wait_ev("scalar", ev_x)
        eva = b.op("scalar", lambda e, o=ln_sq, i2=ln_x[bi], a=st[:, 1:2]:
                   e.activation(o, i2, AF.Square, accum_out=a), inc="eA")
        b.wait_ev("vector", eva)
        b.opb("vector", lambda e, o=st[:, 2:3], i2=st[:, 0:1]:
              e.tensor_scalar_mul(o, i2, c_inv), inc="eV")
        b.opb("vector", lambda e, o=st[:, 3:4], i2=st[:, 2:3]:
              e.tensor_tensor(o, i2, i2, ALU.mult), inc="eV")
        evv = b.opb("vector", lambda e, o=st[:, 4:5], i2=st[:, 1:2], m2=st[:, 3:4]:
                    e.scalar_tensor_tensor(o, i2, c_inv, m2, ALU.mult, ALU.subtract),
                    inc="eV")
        b.wait_ev("scalar", evv)
        eva = b.op("scalar", lambda e, o=st[:, 5:6], i2=st[:, 4:5]:
                   e.activation(o, i2, AF.Sqrt, bias=epsc[:]), inc="eA")
        b.wait_ev("vector", eva)
        b.opb("vector", lambda e, o=st[:, 6:7], i2=st[:, 5:6]:
              e.reciprocal(o, i2), inc="eV")
        evv = b.opb("vector", lambda e, o=st[:, 7:8], i2=st[:, 2:3], r=st[:, 6:7]:
                    e.scalar_tensor_tensor(o, i2, -1.0, r, ALU.mult, ALU.mult), inc="eV")
        b.wait_ev("scalar", evv)
        ev_ln_act[bi] = b.op("scalar", lambda e, o=ln_xn[bi], i2=ln_x[bi],
                             r=st[:, 6:7], m=st[:, 7:8]:
                             e.activation(o, i2, AF.Identity, bias=m, scale=r), inc="eA")
        b.wait_ev("sync", ev_ln_act[bi])
        b.dma("sync", xn_s[i * 128:(i + 1) * 128, :], ln_xn[bi], ch="lnst%d" % i)
    ev_lnst = [b.qnow("lnst%d" % i) for i in range(16)]
    b.phase = "d0"
    WCFG = {
        "g": dict(K=KC, N=I, pk=1376, G=32, qwd=qw_d["g"], scd=sc_d["g"], qzd=qz_d["g"]),
        "u": dict(K=KC, N=I, pk=1376, G=32, qwd=qw_d["u"], scd=sc_d["u"], qzd=qz_d["u"]),
        "d": dict(K=GN, N=D, pk=512, G=86, qwd=qw_d["d"], scd=sc_d["d"], qzd=qz_d["d"]),
    }
    ev_d0_v = None
    for w in ("g", "u", "d"):
        cfg = WCFG[w]
        G, N = cfg["G"], cfg["N"]
        nblk = 8
        blkn = N // nblk
        blkp = blkn // 8
        if ev_d0_v:
            b.wait_ev("scalar", ev_d0_v)
        b.dma("scalar", d0_qz[0:G, 0:blkp * nblk], cfg["qzd"], ch="d0l")
        for blk in range(nblk):
            if ev_d0_v:
                b.wait_ev("scalar", ev_d0_v)
            ev_sc = b.dma("scalar", d0_sc[0:G, 0:blkn],
                          cfg["scd"][:, blk * blkn:(blk + 1) * blkn], ch="d0l")
            b.wait_ev("vector", ev_sc)
            for j in range(8):
                qzsl = d0_qz[0:G, blk * blkp:(blk + 1) * blkp]
                fn = (lambda e, o=d0_z[0:G, AWQ_L[j]:blkn:8], i=qzsl, sh=4 * j:
                      e.tensor_scalar(o, i, sh, 15, ALU.logical_shift_right,
                                      ALU.bitwise_and))
                if j == 7:
                    b.opb("vector", fn, inc="eV")
                else:
                    b.op("vector", fn)
            ev_d0_v = b.op("vector", lambda e, o=d0_zs[0:G, 0:blkn], z=d0_z[0:G, 0:blkn],
                           s=d0_sc[0:G, 0:blkn]: e.tensor_tensor(o, z, s, ALU.mult),
                           inc="eV")
            b.wait_ev("gpsimd", ev_d0_v)
            ev_zs = b.dma("gpsimd", zs_s[w][:, blk * blkn:(blk + 1) * blkn],
                          d0_zs[0:G, 0:blkn], ch="d0s")
            b.wait_ev("vector", ev_zs)
    ev_d0_done = b.qnow("d0s")


    dq_state = dict(w_ev={}, rowfree=None)

    def dequant_steps():
        slice_par = 0
        b.phase = "dq"
        for w in ("g", "u", "d"):
            cfg = WCFG[w]
            Kc, N, pk = cfg["K"], cfg["N"], cfg["pk"]
            qn = N // 4
            qp = pk // 4
            nhalf = (N // 2) // 128
            tgt = {"g": Wg_t, "u": Wu_t, "d": Wd_t}[w]
            ev_wwr = None
            evp2 = None
            ev_qw = [None, None]
            ev_rows = None
            ev_qw[0] = b.dma("sync", dq_qw[0][:, 0:pk], cfg["qwd"][0:128, :],
                             ch="dqw0")
            b.wait_ev("gpsimd", ev_d0_done)
            b.dma("gpsimd", dq_srow[0:1, 0:qn], cfg["scd"][0:1, 0:qn], ch="dqr")
            ev_rows = b.dma("gpsimd", dq_zrow[0:1, 0:qn], zs_s[w][0:1, 0:qn],
                            ch="dqr")
            for ch in range(Kc):
                qb = ch % 2
                if ch + 1 < Kc:
                    b.wait_ev("sync", dq_state.get(("qwfree", (ch + 1) % 2)))
                    ev_qw[(ch + 1) % 2] = b.dma(
                        "sync", dq_qw[(ch + 1) % 2][:, 0:pk],
                        cfg["qwd"][(ch + 1) * 128:(ch + 2) * 128, :],
                        ch="dqw%d" % ((ch + 1) % 2))
                for hf in range(4):
                    ev_zr = ev_rows
                    b.wait_ev("vector", ev_qw[qb])
                    if hf % 2 == 0 and ev_wwr:
                        b.wait_ev("vector", ev_wwr)
                    for j in range(8):
                        srcq = dq_qw[qb][:, hf * qp:(hf + 1) * qp]
                        fn = (lambda e, o=dq_nib[:, AWQ_L[j]:qn:8], i=srcq, sh=4 * j:
                              e.tensor_scalar(o, i, sh, 15, ALU.logical_shift_right,
                                              ALU.bitwise_and))
                        if j == 7:
                            evv = b.opb("vector", fn, inc="eV")
                        else:
                            b.op("vector", fn)
                    if hf == 3:
                        dq_state[("qwfree", qb)] = evv
                    wo0 = (hf % 2) * qn
                    nsl = ceil_div(qn, 512)
                    evt_last_bc = None
                    for si in range(nsl):
                        c0 = si * 512
                        cw = min(512, qn - c0)
                        par = slice_par
                        slice_par ^= 1
                        bs, bz = 2, 3
                        b.wait_ev("tensor", ev_zr)
                        b.pe_bank_wait((bs,))
                        b.op("tensor", mm(bank(bs)[:, 0:cw], ones_bf[0:1, :],
                                          dq_srow[0:1, c0:c0 + cw]))
                        b.pe_bank_wait((bz,))
                        evt = b.op("tensor", mm(bank(bz)[:, 0:cw], ones_bf[0:1, :],
                                                dq_zrow[0:1, c0:c0 + cw]), inc="eT")
                        evt_last_bc = evt
                        b.wait_ev("scalar", evt)
                        b.wait_ev("scalar", dq_state.get(("bcfree", par)))
                        eva = b.op("scalar", lambda e, o=dq_bc[par][:, 0:1024],
                                   i=bank(2, 2): e.activation(o, i, AF.Identity),
                                   inc="eA")
                        b.bank_free[bs] = eva
                        b.bank_free[bz] = eva
                        b.wait_ev("vector", eva)
                        b.wait_ev("vector", dq_state.get("t1free"))
                        tb = dq_t1[par]
                        evv2 = b.op("vector", lambda e, o=tb[:, 0:cw],
                                    i=dq_nib[:, c0:c0 + cw], s=dq_bc[par][:, 0:cw]:
                                    e.tensor_tensor(o, i, s, ALU.mult), inc="eV")
                        b.wait_ev("gpsimd", evv2)
                        evp2 = b.op("gpsimd", lambda e,
                                    o=dq_wout[:, wo0 + c0:wo0 + c0 + cw],
                                    i=tb[:, 0:cw], z=dq_bc[par][:, 512:512 + cw]:
                                    e.tensor_tensor(o, i, z, ALU.subtract), inc="eP")
                        dq_state[("bcfree", par)] = evp2
                        dq_state["t1free"] = evp2
                        yield
                    if not (ch == Kc - 1 and hf == 3):
                        nch, nhf = (ch, hf + 1) if hf < 3 else (ch + 1, 0)
                        b.wait_ev("gpsimd", ("eT", evt_last_bc[1]))
                        b.dma("gpsimd", dq_srow[0:1, 0:qn],
                              cfg["scd"][nch:nch + 1, nhf * qn:(nhf + 1) * qn],
                              ch="dqr")
                        ev_rows = b.dma("gpsimd", dq_zrow[0:1, 0:qn],
                                        zs_s[w][nch:nch + 1,
                                                nhf * qn:(nhf + 1) * qn], ch="dqr")
                    if hf % 2 == 1:
                        h2 = hf // 2
                        dst = tgt[h2 * nhalf:(h2 + 1) * nhalf, :, ch, :]
                        ev_wwr = b.dma("gpsimd", dst.rearrange("a p n -> p a n"),
                                       dq_wout[:, 0:2 * qn]
                                       .rearrange("p (a n) -> p a n", n=128),
                                       ch="dqo" + w)
                        dq_state["w_ev"][w] = ev_wwr
        dq_state["done"] = b.now("eP")
        while True:
            yield

    dq_gen = dequant_steps()

    ev_wslab = [None, None]
    ev_wvslab = [None, None]
    wslab_used = [None, None]
    wvslab_used = [None, None]
    xn_used = None
    ev_ev_store = {}
    qk_ready = None
    mixv_used = [None]
    last_exp = {}
    pfree_t = {}
    pfree_v = {}
    pnfree = {}
    rrfree = {}
    cxfree = {}

    xn_used2 = [None, None]
    for mt in range(NMT):
        m0 = mt * MT
        xb = mt % 2
        b.wait_ev("scalar", ev_lnst[2 * mt])
        b.wait_ev("scalar", ev_lnst[2 * mt + 1])
        b.wait_ev("scalar", xn_used2[xb])
        for kc in range(KC):
            ev_xn = b.dma("scalar", pj_xn[xb].rearrange("p (c n) -> p c n", n=MT)[:, kc, :],
                          xn_s[m0:m0 + MT, kc * 128:(kc + 1) * 128], transpose=True,
                          ch="pjx%d" % xb)
        xv = pj_xn[xb].rearrange("p (c n) -> p c n", n=MT)
        b.phase = "proj"
        for w, dslab in (("q", q_sl), ("k", k_sl)):
            for ncn in range(H):
                wb2 = ncn % 2
                qeng = "sync" if wb2 == 0 else "scalar"
                b.wait_ev(qeng, ev_wcast[w])
                b.wait_ev(qeng, wslab_used[wb2])
                ev_wslab[wb2] = b.dma(
                    qeng, pj_w[wb2][:, 0:KC * 128],
                    w7[w][ncn].rearrange("p c n -> p (c n)"), ch="pjw%d" % wb2)
                pbank = ncn % 2
                b.wait_ev("tensor", ev_xn)
                b.wait_ev("tensor", ev_wslab[wb2])
                wv_ = pj_w[wb2].rearrange("p (c n) -> p c n", n=128)
                b.pe_bank_wait((pbank,))
                evt = None
                for kc in range(KC):
                    evt = b.op("tensor", mm(bank(pbank)[:, 0:MT], wv_[:, kc, :],
                                            xv[:, kc, :], kc == 0, kc == KC - 1),
                               inc="eT" if kc == KC - 1 else None) or evt
                wslab_used[wb2] = evt
                xn_used2[xb] = evt
                b.wait_ev("scalar", evt)
                if mt == 0 and w == "q" and ncn == 0:
                    b.wait_ev("scalar", ev_bt)
                eva = b.op("scalar", lambda e,
                           o=dslab.rearrange("p (c n) -> p c n", n=MT)[:, ncn, :],
                           i=bank(pbank)[:, 0:MT], bias=bt_sb[w][:, ncn:ncn + 1]:
                           e.activation(o, i, AF.Identity, bias=bias), inc="eA")
                b.bank_free[pbank] = eva
                qk_ready = eva
                b.phase = "dq"
                for _ in range(4):
                    next(dq_gen)
                b.phase = "proj"
        for ns in range(16):
            wb2 = ns % 2
            qeng = "sync" if wb2 == 0 else "scalar"
            b.wait_ev(qeng, ev_wcast["v"])
            b.wait_ev(qeng, wvslab_used[wb2])
            ev_wvslab[wb2] = b.dma(
                qeng, pj_wv[wb2][:, 0:KC * 256],
                w7["v"][ns].rearrange("p c n -> p (c n)"), ch="pjv%d" % wb2)
            wvv = pj_wv[wb2].rearrange("p (c n) -> p c n", n=256)
            for mc in range(MT // 128):
                pbank = (ns * 2 + mc) % 2
                b.wait_ev("tensor", ev_xn)
                b.wait_ev("tensor", ev_wvslab[wb2])
                if mt == 0 and ns == 0 and mc == 0:
                    b.wait_ev("tensor", ev_bv)
                b.pe_bank_wait((pbank,))
                for kc in range(KC):
                    b.op("tensor", mm(bank(pbank)[:, 0:256],
                                      xv[:, kc, mc * 128:(mc + 1) * 128],
                                      wvv[:, kc, :], kc == 0, False))
                evt = b.op("tensor", mm(bank(pbank)[:, 0:256], ones_bf[0:1, :],
                                        bv_sb[0:1, ns * 256:(ns + 1) * 256], False, True),
                           inc="eT")
                wvslab_used[wb2] = evt
                xn_used2[xb] = evt
                sb_i = (ns * 2 + mc) % 2
                b.wait_ev("scalar", evt)
                b.wait_ev("scalar", ev_ev_store.get(sb_i))
                eva = b.op("scalar", lambda e, o=pj_ev[sb_i][:, 0:256],
                           i=bank(pbank)[:, 0:256]:
                           e.activation(o, i, AF.Identity), inc="eA")
                b.bank_free[pbank] = eva
                ev_ev_store[sb_i] = b.dma(
                    "scalar", v_h8[ns * 2:(ns + 1) * 2, m0 + mc * 128:m0 + (mc + 1) * 128, :]
                    .rearrange("t m d -> m t d"),
                    pj_ev[sb_i][:, 0:256].rearrange("p (t d) -> p t d", d=128),
                    ch="pjev%d" % sb_i)
            b.phase = "dq"
            for _ in range(3):
                next(dq_gen)
            b.phase = "proj"
        ev_vtm = [b.qnow("pjev%d" % i) for i in range(4)]
        b.phase = "mix"
        qv3 = q_sl.rearrange("p (c n) -> p c n", n=MT)
        kv3 = k_sl.rearrange("p (c n) -> p c n", n=MT)
        ev_mv = None
        for g16 in range(MT // 16):
            gm0 = g16 * 16
            par = g16 % 2
            if gm0 % 64 == 0:
                for _e in ev_vtm:
                    b.wait_ev("scalar", _e)
                b.wait_ev("scalar", mixv_used[0])
                ev_mv = b.dma("scalar",
                              mx_v[0].rearrange("p (c n) -> p c n", n=128)[0:32],
                              v_h8[:, m0 + gm0:m0 + gm0 + 64, :], ch="mixv0")
            vb = 0
            vbase = (gm0 // 64) * 64
            scb = 4 + par
            b.wait_ev("tensor", qk_ready)
            b.wait_ev("tensor", last_exp.get(scb))
            b.pe_bank_wait((scb,))
            evt = None
            for j in range(16):
                mloc = gm0 + j
                evt = b.op("tensor", mm(bank(scb)[0:32, j * 32:(j + 1) * 32],
                                        kv3[:, :, mloc], qv3[:, :, mloc]),
                           inc="eT" if j == 15 else None) or evt
            b.wait_ev("scalar", evt)
            b.wait_ev("scalar", pfree_t.get(par))
            b.wait_ev("scalar", pfree_v.get(par))
            eva = b.op("scalar", lambda e, o=mx_p[par][0:32, :], i=bank(scb)[0:32, :]:
                       e.activation(o, i, AF.Exp, scale=RD), inc="eA")
            last_exp[scb] = eva
            cb = 6 + par
            b.wait_ev("tensor", eva)
            b.pe_bank_wait((cb,))
            evt = b.op("tensor", mm(bank(cb)[0:1, :], ones_bf[0:32, 0:1],
                                    mx_p[par][0:32, :]), inc="eT")
            b.wait_ev("vector", evt)
            b.wait_ev("vector", rrfree.get(par))
            evv = b.opb("vector", lambda e, o=mx_rr[par][0:1, :], i=bank(cb)[0:1, :]:
                        e.reciprocal(o, i), inc="eV")
            b.wait_ev("tensor", evv)
            evt = b.op("tensor", mm(bank(cb)[0:32, :], ones_f[0:1, :],
                                    mx_rr[par][0:1, :]), inc="eT")
            rrfree[par] = evt
            b.wait_ev("vector", evt)
            b.wait_ev("vector", pnfree.get(par))
            evv = b.op("vector", lambda e, o=mx_pn[par][0:32, :], i=mx_p[par][0:32, :],
                       r=bank(cb)[0:32, :]: e.tensor_tensor(o, i, r, ALU.mult), inc="eV")
            pfree_v[par] = evv
            b.bank_free[cb] = evv
            vv3 = mx_v[vb].rearrange("p (c n) -> p c n", n=128)
            cxb = par
            last_pv_evac = None
            b.wait_ev("tensor", evv)
            b.wait_ev("tensor", ev_mv)
            for rnd in range(4):
                pvb = rnd % 2
                b.pe_bank_wait((pvb,))
                evt = None
                for j in range(4):
                    tok = gm0 + rnd * 4 + j
                    evt = b.op("tensor", mm(bank(pvb)[0:32, j * 128:(j + 1) * 128],
                                            mx_pn[par][0:32,
                                                       (rnd * 4 + j) * 32:
                                                       (rnd * 4 + j + 1) * 32],
                                            vv3[0:32, tok - vbase, :]),
                               inc="eT" if j == 3 else None) or evt
                b.wait_ev("scalar", evt)
                if rnd == 0:
                    b.wait_ev("scalar", cxfree.get(cxb))
                last_pv_evac = b.op("scalar", lambda e,
                                    o=mx_cx[cxb][0:32, rnd * 512:(rnd + 1) * 512],
                                    i=bank(pvb)[0:32, :]:
                                    e.activation(o, i, AF.Identity), inc="eA")
                b.bank_free[pvb] = last_pv_evac
            pfree_t[par] = b.now("eT")
            pnfree[par] = b.now("eT")
            b.wait_ev("sync", last_pv_evac)
            cxfree[cxb] = b.dma(
                "sync", ctx_tm[m0 + gm0:m0 + gm0 + 16, :]
                .rearrange("c (h d) -> h c d", d=128),
                mx_cx[cxb][0:32, :].rearrange("p (c n) -> p c n", n=128)[0:32],
                ch="ctm%d" % cxb)
            mixv_used[0] = b.now("eT")
    b.phase = "dq"
    while "done" not in dq_state:
        next(dq_gen)
    ev_dq_done = dq_state["done"]
    ev_ctx_all = [b.qnow("ctm0"), b.qnow("ctm1")]

    b.phase = "ffn"
    for w in ("g", "u", "d"):
        b.wait_ev("sync", b.qnow("dqo" + w))
        b.wait_ev("scalar", b.qnow("dqo" + w))
    ff_used = {}
    ev_g_store = {}
    b.wait_ev("tensor", ev_dq_done)
    for mb in range(NFB):
        m0 = mb * FB
        b.wait_ev("scalar", ff_used.get("xn"))
        for kc in range(KC):
            ev_xn = b.dma("scalar", ff_xn.rearrange("p (c n) -> p c n", n=FB)[:, kc, :],
                          xn_s[m0:m0 + FB, kc * 128:(kc + 1) * 128], transpose=True,
                          ch="ffx")
        if mb == 0 and "ffxn0" in DUMP:
            fx0 = nc.dram_tensor("dump_ffxn0", [128, KC * FB], BF16,
                                 kind="ExternalOutput").ap()
            b.wait_ev("sync", ev_xn)
            b.dma("sync", fx0, ff_xn[:], ch="dump")
        ev_h = None
        for nci in range(GN):
            wb2 = nci % 2
            b.wait_ev("sync" if wb2 == 0 else "scalar", ff_used.get(("wg", wb2)))
            qeng = "sync" if wb2 == 0 else "scalar"
            evwu = b.dma(qeng, ffgu[wb2][:, 0:2 * KC * 128],
                         Wgu_t[nci].rearrange("p a c n -> p (a c n)"),
                         ch="ffg%d" % wb2)
            pgb = (wb2 * 2, wb2 * 2 + 1)
            b.wait_ev("tensor", ev_xn)
            b.wait_ev("tensor", evwu)
            gv = ffgu[wb2][:, 0:KC * 128].rearrange("p (c n) -> p c n", n=128)
            uv = ffgu[wb2][:, KC * 128:2 * KC * 128].rearrange("p (c n) -> p c n", n=128)
            xv2 = ff_xn.rearrange("p (c n) -> p c n", n=FB)
            evt = None
            for sub in range(2):
                bk_ = pgb[sub]
                b.pe_bank_wait((bk_,))
                wsl = gv if sub == 0 else uv
                for kc in range(KC):
                    inc = "eT" if (sub == 1 and kc == KC - 1) else None
                    evt = b.op("tensor", mm(bank(bk_), wsl[:, kc, :], xv2[:, kc, :],
                                            kc == 0, kc == KC - 1), inc=inc) or evt
            ff_used[("wg", wb2)] = evt
            ff_used["xn"] = evt
            sp = nci % 2
            b.wait_ev("scalar", evt)
            b.wait_ev("scalar", ff_used.get(("sil", sp)))
            eva = b.op("scalar", lambda e, o=ff_sil[sp], i=bank(pgb[0]):
                       e.activation(o, i, AF.Silu), inc="eA")
            b.bank_free[pgb[0]] = eva
            b.wait_ev("vector", eva)
            evv = b.op("vector", lambda e, o=ff_h[:, nci * 512:(nci + 1) * 512],
                       i=ff_sil[sp], u=bank(pgb[1]): e.tensor_tensor(o, i, u, ALU.mult),
                       inc="eV")
            b.bank_free[pgb[1]] = evv
            ff_used[("sil", sp)] = evv
            ev_h = evv
        if mb == 0 and "h0" in DUMP:
            h0d = nc.dram_tensor("dump_h0", [128, GN * 512], BF16,
                                 kind="ExternalOutput").ap()
            fx1 = nc.dram_tensor("dump_ffxn1", [128, KC * FB], BF16,
                                 kind="ExternalOutput").ap()
            b.wait_ev("sync", ev_h)
            b.dma("sync", h0d, ff_h[:], ch="dump")
            b.dma("sync", fx1, ff_xn[:], ch="dump")
        hv = ff_h.rearrange("p (c n) -> p c n", n=512)
        for oc in range(OC):
            wb2 = oc % 2
            b.wait_ev("sync" if wb2 == 0 else "scalar", ff_used.get(("wd", wb2)))
            evwd = b.dma("sync" if wb2 == 0 else "scalar", ff_wd[wb2][:, 0:GN * 128],
                         Wd_t[oc].rearrange("p c n -> p (c n)"), ch="ffd%d" % wb2)
            dv = ff_wd[wb2].rearrange("p (c n) -> p c n", n=128)
            bk_ = 4 + oc % 2
            b.wait_ev("tensor", evwd)
            b.wait_ev("tensor", ev_h)
            b.pe_bank_wait((bk_,))
            evt = None
            for nci in range(GN):
                inc = "eT" if nci == GN - 1 else None
                evt = b.op("tensor", mm(bank(bk_), dv[:, nci, :], hv[:, nci, :],
                                        nci == 0, nci == GN - 1), inc=inc) or evt
            ff_used[("wd", wb2)] = evt
            sp = oc % 2
            b.wait_ev("scalar", evt)
            b.wait_ev("scalar", ev_g_store.get(sp))
            eva = b.op("scalar", lambda e, o=ff_g[sp], i=bank(bk_):
                       e.activation(o, i, AF.Silu), inc="eA")
            b.bank_free[bk_] = eva
            b.wait_ev("sync", eva)
            ev_g_store[sp] = b.dma("sync", g_s[oc * 128:(oc + 1) * 128,
                                              mb * FB:(mb + 1) * FB], ff_g[sp],
                                   ch="ffgs%d" % sp)
    ev_g_all = [b.qnow("ffgs0"), b.qnow("ffgs1")]

    b.phase = "comb"
    cb_used = {}
    for _e in ev_ctx_all + ev_g_all:
        b.wait_ev("sync", _e)
    for sci in range(16):
        bi = sci % 2
        l0 = sci * 128
        b.wait_ev("sync", cb_used.get(bi))
        b.dma("sync", cb_x[bi], xd[l0:l0 + 128, :], ch="cbl%d" % bi)
        b.dma("sync", cb_c[bi], ctx_tm[l0:l0 + 128, :], ch="cbl%d" % bi)
        ev_ld = None
        for j in range(8):
            ev_ld = b.dma("sync", cb_g[bi].rearrange("p (c n) -> p c n", n=512)[:, j, :],
                          g_s[j * 512:(j + 1) * 512, l0:l0 + 128], transpose=True,
                          ch="cbl%d" % bi)
        b.wait_ev("vector", ev_ld)
        b.wait_ev("vector", cb_used.get(("o", bi)))
        b.opb("vector", lambda e, o=cb_t, c=cb_c[bi], g=cb_g[bi]:
              e.tensor_tensor(o, c, g, ALU.mult), inc="eV")
        evv = b.op("vector", lambda e, o=cb_o[bi], t=cb_t, x=cb_x[bi]:
                   e.tensor_tensor(o, t, x, ALU.add), inc="eV")
        cb_used[bi] = evv
        b.wait_ev("sync", evv)
        cb_used[("o", bi)] = b.dma("sync", outd[l0:l0 + 128, :], cb_o[bi],
                                   ch="cbo%d" % bi)

    if "h" in DUMP:
        hdump = nc.dram_tensor("dump_h", [128, GN * 512], BF16, kind="ExternalOutput").ap()
        b.dma("sync", hdump, ff_h[:], ch="dump")
    if "ffxn" in DUMP:
        fxdump = nc.dram_tensor("dump_ffxn", [128, KC * FB], BF16, kind="ExternalOutput").ap()
        b.dma("sync", fxdump, ff_xn[:], ch="dump")
    for name, (dd, src) in dump.items():
        b.dma("sync", dd, src, ch="dump")

    GP_CH = ("bt2", "castq", "castk", "castv", "d0s", "dqw0", "dqw1", "dqr",
             "dqog", "dqou", "dqod")
    for name in list(b.cnt):
        if name.startswith("q_"):
            eng = "gpsimd" if name[2:] in GP_CH else "sync"
            b.wait(eng, name, b.cnt[name])

    if not b.check_deadlock():
        raise RuntimeError("offline deadlock check failed")
    nc._builder = b

    with nc.Block() as block:
        @block.sync
        def _(e):
            for f in b.ops["sync"]:
                f(e)

        @block.gpsimd
        def _(e):
            for f in b.ops["gpsimd"]:
                f(e)

        @block.tensor
        def _(e):
            for f in b.ops["tensor"]:
                f(e)

        @block.vector
        def _(e):
            for f in b.ops["vector"]:
                f(e)

        @block.scalar
        def _(e):
            for f in b.ops["scalar"]:
                f(e)

    return nc


def kernel(**inputs):
    f32 = np.float32
    x = np.ascontiguousarray(np.asarray(inputs["L0_attn_hidden_states"], dtype=f32))
    host = {}
    for w, kw in (("q", "L0_attn_wq"), ("k", "L0_attn_wk"), ("v", "L0_attn_wv")):
        host[f"w{w}T"] = np.ascontiguousarray(np.asarray(inputs[kw], dtype=f32).T)
    for w, kb in (("q", "L0_attn_bq"), ("k", "L0_attn_bk")):
        host[f"b{w}t"] = np.ascontiguousarray(
            np.asarray(inputs[kb], dtype=f32).reshape(32, 128).T)
    host["bv"] = np.ascontiguousarray(
        np.asarray(inputs["L0_attn_bv"], dtype=f32).reshape(1, D))
    for pre, nm in (("g", "gate"), ("u", "up"), ("d", "down")):
        host[pre + "qw"] = np.ascontiguousarray(
            np.asarray(inputs[f"L0_ffn_{nm}_qweight"], dtype=np.int32))
        host[pre + "sc"] = np.ascontiguousarray(
            np.asarray(inputs[f"L0_ffn_{nm}_scales"], dtype=f32))
        host[pre + "qz"] = np.ascontiguousarray(
            np.asarray(inputs[f"L0_ffn_{nm}_qzeros"], dtype=np.int32))

    nc = build_program()
    in_maps = []
    for c in range(8):
        m = dict(host)
        h = c % 2
        m["x"] = np.ascontiguousarray(x[c // 2][h * SOWN:(h + 1) * SOWN])
        in_maps.append(m)
    res = run_bass_kernel_spmd(nc, in_maps, core_ids=list(range(8)))
    out = np.empty((B, S, D), dtype=f32)
    for c in range(8):
        h = c % 2
        out[c // 2, h * SOWN:(h + 1) * SOWN, :] = res.results[c]["out"]
    return out


if __name__ == "__main__":
    pass


# version 11
# speedup vs baseline: 1.2146x; 1.2146x over previous
import numpy as np
import ml_dtypes
import concourse.bass as bass
import concourse.mybir as mybir
from concourse.bass_utils import run_bass_kernel_spmd

F32 = mybir.dt.float32
BF16 = mybir.dt.bfloat16
I32 = mybir.dt.int32
U8 = mybir.dt.uint8
AF = mybir.ActivationFunctionType
ALU = mybir.AluOpType
AX = mybir.AxisListType
BF = ml_dtypes.bfloat16

D = 4096
S = 4096
B = 4
H = 32
I = 11008
SOWN = 2048
KC = 32
GN = 86
OC = 32
MT = 256
NMT = SOWN // MT
FB = 512
NFB = SOWN // FB
RD = float(1.0 / np.sqrt(128.0))
LNEPS = 1e-5
AWQ_L = [0, 2, 4, 6, 1, 3, 5, 7]

ENGINES = ("sync", "gpsimd", "tensor", "vector", "scalar")

DUMP = ()


def ceil_div(a, b):
    return (a + b - 1) // b


class Build:
    def __init__(self, nc):
        self.nc = nc
        self.ops = {e: [] for e in ENGINES}
        self.sem = {}
        self.cnt = {}
        for name in ("eT", "eA", "eV", "eP"):
            self.sem[name] = nc.alloc_semaphore(name)
            self.cnt[name] = 0
        self.bank_free = {}
        self.trace = {}

    def emit(self, eng, fn, meta=None):
        self.ops[eng].append(fn)
        self.trace.setdefault(eng, []).append(meta or ("op", None, 0))

    def wait(self, eng, semname, val):
        if val is None or val <= 0:
            return
        s = self.sem[semname]
        self.emit(eng, lambda e, s=s, v=val: e.wait_ge(s, v),
                  meta=("wait", semname, val))

    def wait_ev(self, eng, ev):
        if ev:
            self.wait(eng, ev[0], ev[1])

    def dma(self, eng, out, in_, ch="misc", **kw):
        semname = "q_" + ch
        if semname not in self.sem:
            self.sem[semname] = self.nc.alloc_semaphore(semname)
            self.cnt[semname] = 0
        s = self.sem[semname]
        self.cnt[semname] += 16
        self.emit(eng, lambda e, o=out, i=in_, s=s, kw=kw:
                  e.dma_start(out=o, in_=i, **kw).then_inc(s, 16),
                  meta=("inc", semname, 16))
        return (semname, self.cnt[semname])

    def qnow(self, ch):
        return ("q_" + ch, self.cnt.get("q_" + ch, 0))

    def op(self, eng, fn, inc=None):
        if inc:
            s = self.sem[inc]
            self.cnt[inc] += 1
            v = self.cnt[inc]
            self.emit(eng, lambda e, fn=fn, s=s: fn(e).then_inc(s, 1),
                      meta=("inc", inc, 1))
            return (inc, v)
        self.emit(eng, lambda e, fn=fn: fn(e))
        return None

    def opb(self, eng, fn, inc):
        ev = self.op(eng, fn, inc=inc)
        self.wait_ev(eng, ev)
        return ev

    def now(self, semname):
        return (semname, self.cnt[semname])

    def pe_bank_wait(self, banks):
        for bk in banks:
            ev = self.bank_free.get(bk)
            self.wait_ev("tensor", ev)
            self.bank_free[bk] = None

    def check_deadlock(self):
        import collections
        ptr = {e: 0 for e in ENGINES}
        cnt = collections.defaultdict(int)
        n = {e: len(self.trace.get(e, [])) for e in ENGINES}
        progress = True
        while progress:
            progress = False
            for e in ENGINES:
                while ptr[e] < n[e]:
                    kind, sem, val = self.trace[e][ptr[e]]
                    if kind == "wait" and cnt[sem] < val:
                        break
                    if kind == "inc":
                        cnt[sem] += val
                    ptr[e] += 1
                    progress = True
        stuck = {e: (ptr[e], n[e]) for e in ENGINES if ptr[e] < n[e]}
        if stuck:
            print("DEADLOCK! stuck engines:", stuck)
            for e, (p, _) in stuck.items():
                kind, sem, val = self.trace[e][p]
                print(f"  {e}@{p}: waiting {sem} >= {val}, have {cnt[sem]}")
            return False
        return True


def build_program():
    nc = bass.Bass("TRN2", target_bir_lowering=False)

    xd = nc.dram_tensor("x", [SOWN, D], F32, kind="ExternalInput").ap()
    wT = {w: nc.dram_tensor(f"w{w}T", [D, D], F32, kind="ExternalInput").ap()
          for w in ("q", "k", "v")}
    bT = {w: nc.dram_tensor(f"b{w}t", [128, 32], F32, kind="ExternalInput").ap()
          for w in ("q", "k")}
    bvd = nc.dram_tensor("bv", [1, D], F32, kind="ExternalInput").ap()
    qw_d = {"g": nc.dram_tensor("gqw", [D, 1376], I32, kind="ExternalInput").ap(),
            "u": nc.dram_tensor("uqw", [D, 1376], I32, kind="ExternalInput").ap(),
            "d": nc.dram_tensor("dqw", [I, 512], I32, kind="ExternalInput").ap()}
    sc_d = {"g": nc.dram_tensor("gsc", [32, I], F32, kind="ExternalInput").ap(),
            "u": nc.dram_tensor("usc", [32, I], F32, kind="ExternalInput").ap(),
            "d": nc.dram_tensor("dsc", [86, D], F32, kind="ExternalInput").ap()}
    qz_d = {"g": nc.dram_tensor("gqz", [32, 1376], I32, kind="ExternalInput").ap(),
            "u": nc.dram_tensor("uqz", [32, 1376], I32, kind="ExternalInput").ap(),
            "d": nc.dram_tensor("dqz", [86, 512], I32, kind="ExternalInput").ap()}
    outd = nc.dram_tensor("out", [SOWN, D], F32, kind="ExternalOutput").ap()

    xn_s = nc.dram_tensor("xn_s", [SOWN, D], BF16, kind="Internal").ap()
    wq7 = nc.dram_tensor("wq7", [32, 128, 32, 128], BF16, kind="Internal").ap()
    wk7 = nc.dram_tensor("wk7", [32, 128, 32, 128], BF16, kind="Internal").ap()
    wv7 = nc.dram_tensor("wv7", [16, 128, 32, 256], BF16, kind="Internal").ap()
    w7 = {"q": wq7, "k": wk7, "v": wv7}
    v_h8 = nc.dram_tensor("v_h8", [32, SOWN, 128], BF16, kind="Internal").ap()
    ctx_tm = nc.dram_tensor("ctx_tm", [SOWN, D], BF16, kind="Internal").ap()
    g_s = nc.dram_tensor("g_s", [D, SOWN], BF16, kind="Internal").ap()
    Wg_t = nc.dram_tensor("Wg_t", [GN, 128, KC, 128], BF16, kind="Internal").ap()
    Wu_t = nc.dram_tensor("Wu_t", [GN, 128, KC, 128], BF16, kind="Internal").ap()
    Wd_t = nc.dram_tensor("Wd_t", [OC, 128, GN, 128], BF16, kind="Internal").ap()
    zs_s = {"g": nc.dram_tensor("zs_g", [32, I], BF16, kind="Internal").ap(),
            "u": nc.dram_tensor("zs_u", [32, I], BF16, kind="Internal").ap(),
            "d": nc.dram_tensor("zs_d", [86, D], BF16, kind="Internal").ap()}
    scb_d = {"g": nc.dram_tensor("scb_g", [32, I], BF16, kind="Internal").ap(),
             "u": nc.dram_tensor("scb_u", [32, I], BF16, kind="Internal").ap(),
             "d": nc.dram_tensor("scb_d", [86, D], BF16, kind="Internal").ap()}
    h_d = nc.dram_tensor("h_d", [GN, 128, SOWN], BF16, kind="Internal").ap()

    dump = {}
    for name, src in (("xn", xn_s), ("v", v_h8), ("ctx", ctx_tm), ("g", g_s),
                      ("wg", Wg_t), ("wu", Wu_t), ("wd", Wd_t)):
        if name in DUMP:
            dump[name] = (nc.dram_tensor("dump_" + name, list(src.shape), BF16,
                                         kind="ExternalOutput").ap(), src)

    ones_bf = nc.alloc_sbuf_tensor("ones_bf", [128, 128], BF16)
    bt_sb = {w: nc.alloc_sbuf_tensor(f"bt_{w}", [128, 32], F32) for w in ("q", "k")}
    bv_sb = nc.alloc_sbuf_tensor("bv_sb", [1, D], BF16)
    epsc = nc.alloc_sbuf_tensor("epsc", [128, 1], F32)
    ARENA_KB = 198
    arena = nc.alloc_sbuf_tensor("arena", [128, ARENA_KB * 1024], U8)

    def view(off_b, n, dtype):
        sz = {F32: 4, BF16: 2, I32: 4}[dtype]
        assert off_b + n * sz <= ARENA_KB * 1024, (off_b, n, dtype)
        return arena[:, off_b:off_b + n * sz].bitcast(dtype)

    K1 = 1024
    ln_x = [view(0, D, F32), view(16 * K1, D, F32)]
    ln_sq = view(32 * K1, D, BF16)
    ln_xn = [view(40 * K1, D, BF16), view(48 * K1, D, BF16)]
    ln_st = view(56 * K1, 16, F32)
    d0_qz = view(64 * K1, 1376, I32)
    d0_z = view(70 * K1, 1376, I32)
    d0_sc = view(76 * K1, 1376, F32)
    d0_zs = view(82 * K1, 1376, BF16)
    d0_scb = view(86 * K1, 1376, BF16)

    pj_xn = view(0, KC * MT, BF16)
    pj_w = [view(16 * K1, KC * 128, BF16), view(24 * K1, KC * 128, BF16)]
    pj_wv = [view(32 * K1, KC * 256, BF16), view(48 * K1, KC * 256, BF16)]
    q_sl = view(64 * K1, H * MT, BF16)
    k_sl = view(80 * K1, H * MT, BF16)
    mx_v = [view(96 * K1, 64 * 128, BF16), view(112 * K1, 64 * 128, BF16)]
    mx_p = [view(128 * K1, 512, BF16), view(128 * K1 + 1024, 512, BF16)]
    mx_pn = [view(130 * K1, 512, BF16), view(130 * K1 + 1024, 512, BF16)]
    mx_rr = [view(132 * K1, 512, BF16), view(132 * K1 + 1024, 512, BF16)]
    mx_cx = [view(134 * K1, 16 * 128, BF16), view(138 * K1, 16 * 128, BF16)]
    pj_ev = [view(142 * K1 + o * 1024, 512, BF16) for o in range(4)]
    _o = 146 * K1
    dqoff = {}
    for nm, nb in (("qw0", 688 * 4), ("qw1", 688 * 4), ("sr0", 2752 * 2),
                   ("sr1", 2752 * 2), ("nib", 2752 * 4), ("wout", 5504 * 2)):
        dqoff[nm] = _o
        _o += nb
    assert _o <= ARENA_KB * K1, _o

    def dqv(name, n, dtype):
        sz = {F32: 4, BF16: 2, I32: 4}[dtype]
        return arena[:, dqoff[name]:dqoff[name] + n * sz].bitcast(dtype)

    dq_qw = [dqv("qw0", 688, I32), dqv("qw1", 688, I32)]
    dq_sr = [dqv("sr0", 2752, BF16), dqv("sr1", 2752, BF16)]
    dq_nib = dqv("nib", 2752, I32)
    dq_wout = dqv("wout", 5504, BF16)

    fx_full = view(0, KC * SOWN, BF16)
    f0_wg = [view(131072, KC * 128, BF16), view(139264, KC * 128, BF16)]
    f0_wu = [view(147456, KC * 128, BF16), view(155648, KC * 128, BF16)]
    h_t = [view(163840, SOWN, BF16), view(167936, SOWN, BF16)]
    f0_sil = [view(172032, 512, BF16), view(173056, 512, BF16)]
    cs4 = view(174080, 4 * 512, BF16)
    oh32 = view(178176, KC * 32, BF16)
    zs4 = [view(180224 + i * 256, 128, BF16) for i in range(4)]
    f2_h = view(0, GN * 512, BF16)
    f2_wd = [view(88064, 43 * 128, BF16), view(99072, 43 * 128, BF16)]
    oh86 = view(110080, 86 * 86, BF16)
    f2_g = [view(124928, 512, BF16), view(125952, 512, BF16)]
    f2_cs = view(126976, 512, BF16)
    cb_x = [view(0, D, F32), view(16 * K1, D, F32)]
    cb_c = [view(32 * K1, D, BF16), view(40 * K1, D, BF16)]
    cb_g = [view(48 * K1, D, BF16), view(56 * K1, D, BF16)]
    cb_t = view(64 * K1, D, F32)
    cb_o = [view(80 * K1, D, F32), view(96 * K1, D, F32)]

    P = nc.alloc_psum_tensor("P", [128, 4096], F32)

    def bank(i, n=1):
        return P[:, i * 512:(i + n) * 512]

    b = Build(nc)

    def mm(out, lhsT, rhs, start=True, stop=True):
        return lambda e: e.matmul(out, lhsT, rhs, start=start, stop=stop)

    def lp(fn, *a, **kw):
        with nc.allow_low_precision(reason="bf16 operand rounding (within bf16-matmul precision)"):
            return fn(*a, **kw)

    b.op("vector", lambda e: e.memset(ones_bf[:], 1.0))
    ev_const = b.op("vector", lambda e: e.memset(epsc[:], LNEPS), inc="eV")
    b.wait_ev("scalar", ev_const)
    b.wait_ev("gpsimd", ev_const)
    b.wait_ev("tensor", ev_const)
    for w in ("q", "k"):
        ev_bt = b.dma("sync", bt_sb[w][:], bT[w], ch="bt")
    ev_bv = b.dma("gpsimd", bv_sb[:], bvd, ch="bt2")

    ev_wcast = {}
    for w in ("q", "k", "v"):
        nn = 256 if w == "v" else 128
        for kc in range(KC):
            ev_wcast[w] = b.dma(
                "gpsimd", w7[w][:, :, kc, :].rearrange("a p n -> p a n"),
                wT[w][kc * 128:(kc + 1) * 128, :].rearrange("p (a n) -> p a n", n=nn),
                ch="cast" + w)

    WCFG = {
        "g": dict(K=KC, N=I, pk=1376, G=32, qwd=qw_d["g"], scd=sc_d["g"], qzd=qz_d["g"]),
        "u": dict(K=KC, N=I, pk=1376, G=32, qwd=qw_d["u"], scd=sc_d["u"], qzd=qz_d["u"]),
        "d": dict(K=GN, N=D, pk=512, G=86, qwd=qw_d["d"], scd=sc_d["d"], qzd=qz_d["d"]),
    }
    ev_d0_v = None
    d0_scb_done = {}
    for w in ("g", "u", "d"):
        cfg = WCFG[w]
        G, N = cfg["G"], cfg["N"]
        nblk = 8
        blkn = N // nblk
        blkp = blkn // 8
        if ev_d0_v:
            b.wait_ev("sync", ev_d0_v)
        b.dma("sync", d0_qz[0:G, 0:blkp * nblk], cfg["qzd"], ch="d0l")
        for blk in range(nblk):
            if ev_d0_v:
                b.wait_ev("sync", ev_d0_v)
            ev_sc = b.dma("sync", d0_sc[0:G, 0:blkn],
                          cfg["scd"][:, blk * blkn:(blk + 1) * blkn], ch="d0l")
            b.wait_ev("vector", ev_sc)
            ev_scb = b.op("vector", lambda e, o=d0_scb[0:G, 0:blkn],
                          i=d0_sc[0:G, 0:blkn]: e.tensor_copy(o, i), inc="eV")
            for j in range(8):
                qzsl = d0_qz[0:G, blk * blkp:(blk + 1) * blkp]
                fn = (lambda e, o=d0_z[0:G, AWQ_L[j]:blkn:8], i=qzsl, sh=4 * j:
                      e.tensor_scalar(o, i, sh, 15, ALU.logical_shift_right,
                                      ALU.bitwise_and))
                if j == 7:
                    b.opb("vector", fn, inc="eV")
                else:
                    b.op("vector", fn)
            ev_d0_v = b.op("vector", lambda e, o=d0_zs[0:G, 0:blkn], z=d0_z[0:G, 0:blkn],
                           s=d0_sc[0:G, 0:blkn]: e.tensor_tensor(o, z, s, ALU.mult),
                           inc="eV")
            b.wait_ev("gpsimd", ev_scb)
            b.dma("gpsimd", scb_d[w][:, blk * blkn:(blk + 1) * blkn],
                  d0_scb[0:G, 0:blkn], ch="d0s")
            b.wait_ev("gpsimd", ev_d0_v)
            ev_zs = b.dma("gpsimd", zs_s[w][:, blk * blkn:(blk + 1) * blkn],
                          d0_zs[0:G, 0:blkn], ch="d0s")
            b.wait_ev("vector", ev_zs)
        d0_scb_done[w] = b.qnow("d0s")
    ev_d0_done = b.qnow("d0s")

    c_inv = 1.0 / D
    ev_ln_act = [None, None]
    for i in range(16):
        bi = i % 2
        if ev_ln_act[bi]:
            b.wait_ev("sync", ev_ln_act[bi])
        ev_x = b.dma("sync", ln_x[bi], xd[i * 128:(i + 1) * 128, :], ch="lnx%d" % bi)
        st = ln_st
        b.wait_ev("vector", ev_x)
        b.opb("vector", lambda e, o=st[:, 0:1], i2=ln_x[bi]:
              e.reduce_sum(o, i2, axis=AX.X), inc="eV")
        b.wait_ev("scalar", ev_x)
        eva = b.op("scalar", lambda e, o=ln_sq, i2=ln_x[bi], a=st[:, 1:2]:
                   e.activation(o, i2, AF.Square, accum_out=a), inc="eA")
        b.wait_ev("vector", eva)
        b.opb("vector", lambda e, o=st[:, 2:3], i2=st[:, 0:1]:
              e.tensor_scalar_mul(o, i2, c_inv), inc="eV")
        b.opb("vector", lambda e, o=st[:, 3:4], i2=st[:, 2:3]:
              e.tensor_tensor(o, i2, i2, ALU.mult), inc="eV")
        evv = b.opb("vector", lambda e, o=st[:, 4:5], i2=st[:, 1:2], m2=st[:, 3:4]:
                    e.scalar_tensor_tensor(o, i2, c_inv, m2, ALU.mult, ALU.subtract),
                    inc="eV")
        b.wait_ev("scalar", evv)
        eva = b.op("scalar", lambda e, o=st[:, 5:6], i2=st[:, 4:5]:
                   e.activation(o, i2, AF.Sqrt, bias=epsc[:]), inc="eA")
        b.wait_ev("vector", eva)
        b.opb("vector", lambda e, o=st[:, 6:7], i2=st[:, 5:6]:
              e.reciprocal(o, i2), inc="eV")
        evv = b.opb("vector", lambda e, o=st[:, 7:8], i2=st[:, 2:3], r=st[:, 6:7]:
                    e.scalar_tensor_tensor(o, i2, -1.0, r, ALU.mult, ALU.mult), inc="eV")
        b.wait_ev("scalar", evv)
        ev_ln_act[bi] = b.op("scalar", lambda e, o=ln_xn[bi], i2=ln_x[bi],
                             r=st[:, 6:7], m=st[:, 7:8]:
                             e.activation(o, i2, AF.Identity, bias=m, scale=r), inc="eA")
        b.wait_ev("sync", ev_ln_act[bi])
        b.dma("sync", xn_s[i * 128:(i + 1) * 128, :], ln_xn[bi], ch="lnst")
    ev_ln_all = b.qnow("lnst")

    dq_state = {}
    dq_halves = []
    for w in ("g", "u", "d"):
        for ch in range(WCFG[w]["K"]):
            for h2 in (0, 1):
                dq_halves.append((w, ch, h2))
    NQ = 2 * len(dq_halves)

    def dq_issue_qw(t):
        w, ch, h2 = dq_halves[t]
        hq = WCFG[w]["pk"] // 2
        qb = t % 2
        b.wait_ev("gpsimd", dq_state.get(("qwfree", qb)))
        dq_state[("qwld", t)] = b.dma(
            "gpsimd", dq_qw[qb][:, 0:hq],
            WCFG[w]["qwd"][ch * 128:(ch + 1) * 128, h2 * hq:(h2 + 1) * hq],
            ch="dqw%d" % qb)

    def dq_issue_sr(u):
        t, qih = divmod(u, 2)
        w, ch, h2 = dq_halves[t]
        qn = WCFG[w]["N"] // 4
        hf = h2 * 2 + qih
        sb_ = u % 2
        b.wait_ev("gpsimd", dq_state.get(("srfree", sb_)))
        b.wait_ev("gpsimd", d0_scb_done[w])
        dq_state[("srld", u)] = b.dma(
            "gpsimd", dq_sr[sb_][:, 0:qn],
            scb_d[w][ch:ch + 1, hf * qn:(hf + 1) * qn].partition_broadcast(128),
            ch="dqs%d" % sb_)

    def dequant_steps():
        for t0 in (0, 1):
            dq_issue_qw(t0)
        for u0 in (0, 1):
            dq_issue_sr(u0)
        for t, (w, ch, h2) in enumerate(dq_halves):
            cfg = WCFG[w]
            N, pk = cfg["N"], cfg["pk"]
            qn = N // 4
            qp = pk // 4
            nhalf = (N // 2) // 128
            qb = t % 2
            tgt = {"g": Wg_t, "u": Wu_t, "d": Wd_t}[w]
            evm = None
            for qih in range(2):
                u = 2 * t + qih
                sb_ = u % 2
                if qih == 0:
                    b.wait_ev("vector", dq_state[("qwld", t)])
                    b.wait_ev("vector", dq_state.get("woutfree"))
                for j in range(8):
                    src = dq_qw[qb][:, qih * qp:(qih + 1) * qp]
                    fn = (lambda e, o=dq_nib[:, AWQ_L[j]:qn:8], i=src, sh=4 * j:
                          e.tensor_scalar(o, i, sh, 15, ALU.logical_shift_right,
                                          ALU.bitwise_and))
                    if j == 7:
                        evup = b.opb("vector", fn, inc="eV")
                    else:
                        b.op("vector", fn)
                if qih == 1:
                    dq_state[("qwfree", qb)] = evup
                    if t + 2 < len(dq_halves):
                        dq_issue_qw(t + 2)
                b.wait_ev("vector", dq_state[("srld", u)])
                evm = b.op("vector", lambda e, o=dq_wout[:, qih * qn:(qih + 1) * qn],
                           i=dq_nib[:, 0:qn], s=dq_sr[sb_][:, 0:qn]:
                           e.tensor_tensor(o, i, s, ALU.mult), inc="eV")
                dq_state[("srfree", sb_)] = evm
                if u + 2 < NQ:
                    dq_issue_sr(u + 2)
                yield
            b.wait_ev("gpsimd", evm)
            dst = tgt[h2 * nhalf:(h2 + 1) * nhalf, :, ch, :]
            dq_state["woutfree"] = b.dma("gpsimd", dst.rearrange("a p n -> p a n"),
                                         dq_wout[:, 0:2 * qn]
                                         .rearrange("p (a n) -> p a n", n=128),
                                         ch="dqo" + w)
        dq_state["done"] = b.now("eV")
        while True:
            yield

    dq_gen = dequant_steps()

    ev_wslab = [None, None]
    ev_wvslab = [None, None]
    wslab_used = [None, None]
    wvslab_used = [None, None]
    xn_used = None
    ev_ev_store = {}
    qk_ready = None
    mixv_used = [None, None]
    last_exp = {}
    pfree_t = {}
    pfree_v = {}
    pnfree = {}
    rrfree = {}
    cxfree = {}

    for mt in range(NMT):
        m0 = mt * MT
        if mt == 0:
            b.wait_ev("sync", ev_ln_all)
        b.wait_ev("sync", xn_used)
        for kc in range(KC):
            ev_xn = b.dma("sync", pj_xn.rearrange("p (c n) -> p c n", n=MT)[:, kc, :],
                          xn_s[m0:m0 + MT, kc * 128:(kc + 1) * 128], transpose=True,
                          ch="pjx")
        xv = pj_xn.rearrange("p (c n) -> p c n", n=MT)
        for w, dslab in (("q", q_sl), ("k", k_sl)):
            for ncn in range(H):
                wb2 = ncn % 2
                b.wait_ev("sync", ev_wcast[w])
                b.wait_ev("sync", wslab_used[wb2])
                ev_wslab[wb2] = b.dma(
                    "sync", pj_w[wb2][:, 0:KC * 128],
                    w7[w][ncn].rearrange("p c n -> p (c n)"), ch="pjw%d" % wb2)
                pbank = ncn % 2
                b.wait_ev("tensor", ev_xn)
                b.wait_ev("tensor", ev_wslab[wb2])
                wv_ = pj_w[wb2].rearrange("p (c n) -> p c n", n=128)
                b.pe_bank_wait((pbank,))
                evt = None
                for kc in range(KC):
                    evt = b.op("tensor", mm(bank(pbank)[:, 0:MT], wv_[:, kc, :],
                                            xv[:, kc, :], kc == 0, kc == KC - 1),
                               inc="eT" if kc == KC - 1 else None) or evt
                wslab_used[wb2] = evt
                xn_used = evt
                b.wait_ev("scalar", evt)
                if mt == 0 and w == "q" and ncn == 0:
                    b.wait_ev("scalar", ev_bt)
                eva = b.op("scalar", lambda e,
                           o=dslab.rearrange("p (c n) -> p c n", n=MT)[:, ncn, :],
                           i=bank(pbank)[:, 0:MT], bias=bt_sb[w][:, ncn:ncn + 1]:
                           e.activation(o, i, AF.Identity, bias=bias), inc="eA")
                b.bank_free[pbank] = eva
                qk_ready = eva
                next(dq_gen)
        for ns in range(16):
            wb2 = ns % 2
            b.wait_ev("sync", ev_wcast["v"])
            b.wait_ev("sync", wvslab_used[wb2])
            ev_wvslab[wb2] = b.dma(
                "sync", pj_wv[wb2][:, 0:KC * 256],
                w7["v"][ns].rearrange("p c n -> p (c n)"), ch="pjv%d" % wb2)
            wvv = pj_wv[wb2].rearrange("p (c n) -> p c n", n=256)
            for mc in range(MT // 128):
                pbank = (ns * 2 + mc) % 2
                b.wait_ev("tensor", ev_xn)
                b.wait_ev("tensor", ev_wvslab[wb2])
                if mt == 0 and ns == 0 and mc == 0:
                    b.wait_ev("tensor", ev_bv)
                b.pe_bank_wait((pbank,))
                for kc in range(KC):
                    b.op("tensor", mm(bank(pbank)[:, 0:256],
                                      xv[:, kc, mc * 128:(mc + 1) * 128],
                                      wvv[:, kc, :], kc == 0, False))
                evt = b.op("tensor", mm(bank(pbank)[:, 0:256], ones_bf[0:1, :],
                                        bv_sb[0:1, ns * 256:(ns + 1) * 256], False, True),
                           inc="eT")
                wvslab_used[wb2] = evt
                xn_used = evt
                sb_i = (ns * 2 + mc) % 4
                b.wait_ev("scalar", evt)
                b.wait_ev("scalar", ev_ev_store.get(sb_i))
                eva = b.op("scalar", lambda e, o=pj_ev[sb_i][:, 0:256],
                           i=bank(pbank)[:, 0:256]:
                           e.activation(o, i, AF.Identity), inc="eA")
                b.bank_free[pbank] = eva
                b.wait_ev("sync", eva)
                ev_ev_store[sb_i] = b.dma(
                    "sync", v_h8[ns * 2:(ns + 1) * 2, m0 + mc * 128:m0 + (mc + 1) * 128, :]
                    .rearrange("t m d -> m t d"),
                    pj_ev[sb_i][:, 0:256].rearrange("p (t d) -> p t d", d=128),
                    ch="pjev%d" % sb_i)
            next(dq_gen)
        ev_vtm = [b.qnow("pjev%d" % i) for i in range(4)]
        qv3 = q_sl.rearrange("p (c n) -> p c n", n=MT)
        kv3 = k_sl.rearrange("p (c n) -> p c n", n=MT)
        ev_mv = None
        for g16 in range(MT // 16):
            gm0 = g16 * 16
            par = g16 % 2
            if gm0 % 64 == 0:
                vb = (gm0 // 64) % 2
                for _e in ev_vtm:
                    b.wait_ev("sync", _e)
                b.wait_ev("sync", mixv_used[vb])
                ev_mv = b.dma("sync",
                              mx_v[vb].rearrange("p (c n) -> p c n", n=128)[0:32],
                              v_h8[:, m0 + gm0:m0 + gm0 + 64, :], ch="mixv%d" % vb)
            vb = (gm0 // 64) % 2
            vbase = (gm0 // 64) * 64
            scb = 4 + par
            b.wait_ev("tensor", qk_ready)
            b.wait_ev("tensor", last_exp.get(scb))
            b.pe_bank_wait((scb,))
            evt = None
            for j in range(16):
                mloc = gm0 + j
                evt = b.op("tensor", mm(bank(scb)[0:32, j * 32:(j + 1) * 32],
                                        kv3[:, :, mloc], qv3[:, :, mloc]),
                           inc="eT" if j == 15 else None) or evt
            b.wait_ev("scalar", evt)
            b.wait_ev("scalar", pfree_t.get(par))
            b.wait_ev("scalar", pfree_v.get(par))
            eva = b.op("scalar", lambda e, o=mx_p[par][0:32, :], i=bank(scb)[0:32, :]:
                       e.activation(o, i, AF.Exp, scale=RD), inc="eA")
            last_exp[scb] = eva
            cb = 6 + par
            b.wait_ev("tensor", eva)
            b.pe_bank_wait((cb,))
            evt = b.op("tensor", mm(bank(cb)[0:1, :], ones_bf[0:32, 0:1],
                                    mx_p[par][0:32, :]), inc="eT")
            b.wait_ev("vector", evt)
            b.wait_ev("vector", rrfree.get(par))
            evv = b.opb("vector", lambda e, o=mx_rr[par][0:1, :], i=bank(cb)[0:1, :]:
                        lp(e.reciprocal, o, i), inc="eV")
            b.wait_ev("tensor", evv)
            evt = b.op("tensor", mm(bank(cb)[0:32, :], ones_bf[0:1, 0:32],
                                    mx_rr[par][0:1, :]), inc="eT")
            rrfree[par] = evt
            b.wait_ev("vector", evt)
            b.wait_ev("vector", pnfree.get(par))
            evv = b.op("vector", lambda e, o=mx_pn[par][0:32, :], i=mx_p[par][0:32, :],
                       r=bank(cb)[0:32, :]: e.tensor_tensor(o, i, r, ALU.mult), inc="eV")
            pfree_v[par] = evv
            vv3 = mx_v[vb].rearrange("p (c n) -> p c n", n=128)
            cxb = par
            last_pv_evac = None
            for rnd in range(4):
                b.wait_ev("tensor", evv)
                b.wait_ev("tensor", ev_mv)
                if rnd > 0:
                    b.wait_ev("tensor", last_pv_evac)
                b.pe_bank_wait((cb,))
                evt = None
                for j in range(4):
                    tok = gm0 + rnd * 4 + j
                    evt = b.op("tensor", mm(bank(cb)[0:32, j * 128:(j + 1) * 128],
                                            mx_pn[par][0:32,
                                                       (rnd * 4 + j) * 32:
                                                       (rnd * 4 + j + 1) * 32],
                                            vv3[0:32, tok - vbase, :]),
                               inc="eT" if j == 3 else None) or evt
                b.wait_ev("scalar", evt)
                if rnd == 0:
                    b.wait_ev("scalar", cxfree.get(cxb))
                last_pv_evac = b.op("scalar", lambda e,
                                    o=mx_cx[cxb][0:32, rnd * 512:(rnd + 1) * 512],
                                    i=bank(cb)[0:32, :]:
                                    e.activation(o, i, AF.Identity), inc="eA")
                b.bank_free[cb] = last_pv_evac
            pfree_t[par] = b.now("eT")
            pnfree[par] = b.now("eT")
            b.wait_ev("sync", last_pv_evac)
            cxfree[cxb] = b.dma(
                "sync", ctx_tm[m0 + gm0:m0 + gm0 + 16, :]
                .rearrange("c (h d) -> h c d", d=128),
                mx_cx[cxb][0:32, :].rearrange("p (c n) -> p c n", n=128)[0:32],
                ch="ctm%d" % cxb)
            mixv_used[vb] = b.now("eT")
    while "done" not in dq_state:
        next(dq_gen)
    ev_dq_done = dq_state["done"]
    ev_ctx_all = [b.qnow("ctm0"), b.qnow("ctm1")]

    ev_mix_T = b.now("eT")
    ev_mix_A = b.now("eA")
    ev_mix_V = b.now("eV")
    for w in ("g", "u", "d"):
        b.wait_ev("sync", b.qnow("dqo" + w))
        b.wait_ev("vector", b.qnow("dqo" + w))
        b.wait_ev("scalar", b.qnow("dqo" + w))
    b.wait_ev("sync", ev_d0_done)
    b.wait_ev("sync", ev_mix_T)
    b.wait_ev("sync", ev_mix_A)
    b.wait_ev("sync", ev_mix_V)
    for i in range(2):
        b.wait_ev("sync", b.qnow("ctm%d" % i))
    b.wait_ev("scalar", ev_dq_done)
    b.wait_ev("tensor", ev_dq_done)
    ev_fx = None
    for kc in range(KC):
        for tb in range(4):
            ev_fx = b.dma("sync",
                          fx_full.rearrange("p (c n) -> p c n", n=SOWN)
                          [:, kc, tb * 512:(tb + 1) * 512],
                          xn_s[tb * 512:(tb + 1) * 512, kc * 128:(kc + 1) * 128],
                          transpose=True, ch="ffx")
    fxv = fx_full.rearrange("p (c n) -> p c n", n=SOWN)
    b.op("vector", lambda e: e.memset(oh32[:], 0.0))
    oh32v = oh32.rearrange("p (c n) -> p c n", n=32)
    ev_oh = None
    for kc in range(KC):
        ev_oh = b.op("vector", lambda e, o=oh32v[:, kc, kc:kc + 1]: e.memset(o, 1.0),
                     inc="eV" if kc == KC - 1 else None) or ev_oh
    b.wait_ev("tensor", ev_fx)
    b.wait_ev("tensor", ev_oh)
    ev_cs4 = None
    for tb in range(4):
        b.pe_bank_wait((6,))
        evt = None
        for kc in range(KC):
            evt = b.op("tensor", mm(bank(6)[0:32, :], oh32v[:, kc, :],
                                    fxv[:, kc, tb * 512:(tb + 1) * 512],
                                    kc == 0, kc == KC - 1),
                       inc="eT" if kc == KC - 1 else None) or evt
        b.wait_ev("scalar", evt)
        b.wait_ev("scalar", ev_cs4)
        ev_cs4 = b.op("scalar", lambda e, o=cs4[0:32, tb * 512:(tb + 1) * 512],
                      i=bank(6)[0:32, :]:
                      e.activation(o, i, AF.Identity, scale=-1.0), inc="eA")
        b.bank_free[6] = ev_cs4
    b.wait_ev("tensor", ev_cs4)
    f0_used = {}
    zs_used = {}
    sil_used = {}
    ev_ht_store = {}
    for nci in range(GN):
        wb2 = nci % 2
        b.wait_ev("sync", f0_used.get(wb2))
        b.dma("sync", f0_wg[wb2][:], Wg_t[nci].rearrange("p c n -> p (c n)"),
              ch="f0w%d" % wb2)
        evwu = b.dma("sync", f0_wu[wb2][:], Wu_t[nci].rearrange("p c n -> p (c n)"),
                     ch="f0w%d" % wb2)
        b.wait_ev("sync", zs_used.get(wb2))
        b.dma("sync", zs4[wb2][0:32, :], zs_s["g"][:, nci * 128:(nci + 1) * 128],
              ch="f0z%d" % wb2)
        b.wait_ev("sync", zs_used.get(2 + wb2))
        evzu = b.dma("sync", zs4[2 + wb2][0:32, :],
                     zs_s["u"][:, nci * 128:(nci + 1) * 128], ch="f0z%d" % wb2)
        gv = f0_wg[wb2].rearrange("p (c n) -> p c n", n=128)
        uv = f0_wu[wb2].rearrange("p (c n) -> p c n", n=128)
        hb = nci % 2
        b.wait_ev("vector", ev_ht_store.get(hb))
        evt = None
        evv = None
        for tb in range(4):
            bkG, bkU = 2 * tb, 2 * tb + 1
            b.wait_ev("tensor", evwu)
            b.wait_ev("tensor", evzu)
            for bk_, wsl, zi in ((bkG, gv, wb2), (bkU, uv, 2 + wb2)):
                b.pe_bank_wait((bk_,))
                for kc in range(KC):
                    b.op("tensor", mm(bank(bk_), wsl[:, kc, :],
                                      fxv[:, kc, tb * 512:(tb + 1) * 512],
                                      kc == 0, False))
                evt = b.op("tensor", mm(bank(bk_), zs4[zi][0:32, :],
                                        cs4[0:32, tb * 512:(tb + 1) * 512],
                                        False, True), inc="eT")
                zs_used[zi] = evt
            sp = tb % 2
            b.wait_ev("scalar", evt)
            b.wait_ev("scalar", sil_used.get(sp))
            eva = b.op("scalar", lambda e, o=f0_sil[sp], i=bank(bkG):
                       e.activation(o, i, AF.Silu), inc="eA")
            b.bank_free[bkG] = eva
            b.wait_ev("vector", eva)
            evv = b.op("vector", lambda e, o=h_t[hb][:, tb * 512:(tb + 1) * 512],
                       i=f0_sil[sp], u=bank(bkU): e.tensor_tensor(o, i, u, ALU.mult),
                       inc="eV")
            b.bank_free[bkU] = evv
            sil_used[sp] = evv
        f0_used[wb2] = evt
        b.wait_ev("sync", evv)
        ev_ht_store[hb] = b.dma("sync", h_d[nci], h_t[hb][:], ch="hst%d" % hb)

    ev_f0_T = b.now("eT")
    b.wait_ev("sync", ev_f0_T)
    b.wait_ev("vector", ev_f0_T)
    for i in range(2):
        b.wait_ev("sync", ev_ht_store.get(i))
    b.op("vector", lambda e: e.memset(oh86[:], 0.0))
    oh86v = oh86.rearrange("p (c n) -> p c n", n=86)
    ev_oh86 = None
    for j in range(GN):
        ev_oh86 = b.op("vector", lambda e, o=oh86v[:, j, j:j + 1]: e.memset(o, 1.0),
                       inc="eV" if j == GN - 1 else None) or ev_oh86
    ev_g_store = {}
    f2h_used = None
    cs_free = None
    for mb in range(NFB):
        m0 = mb * FB
        b.wait_ev("sync", f2h_used)
        ev_h = None
        for nci in range(GN):
            ev_h = b.dma("sync", f2_h.rearrange("p (c n) -> p c n", n=FB)[:, nci, :],
                         h_d[nci][:, m0:m0 + FB], ch="f2h")
        hv = f2_h.rearrange("p (c n) -> p c n", n=FB)
        b.wait_ev("tensor", ev_h)
        b.wait_ev("tensor", ev_oh86)
        b.pe_bank_wait((7,))
        csh_evt = None
        for j in range(GN):
            csh_evt = b.op("tensor", mm(bank(7)[0:86, :], oh86v[:, j, :], hv[:, j, :],
                                        j == 0, j == GN - 1),
                           inc="eT" if j == GN - 1 else None) or csh_evt
        b.wait_ev("scalar", csh_evt)
        b.wait_ev("scalar", cs_free)
        ev_csh = b.op("scalar", lambda e, o=f2_cs[0:86, :], i=bank(7)[0:86, :]:
                      e.activation(o, i, AF.Identity, scale=-1.0), inc="eA")
        b.bank_free[7] = ev_csh
        for oc in range(OC):
            evwd = {}
            for hs in range(2):
                b.wait_ev("sync", f0_used.get(("wd", hs)))
                evwd[hs] = b.dma("sync", f2_wd[hs][:, 0:43 * 128],
                                 Wd_t[oc, :, hs * 43:(hs + 1) * 43, :]
                                 .rearrange("p c n -> p (c n)"), ch="ffd%d" % hs)
            b.wait_ev("sync", zs_used.get(oc % 2))
            evzd = b.dma("sync", zs4[oc % 2][0:86, :],
                         zs_s["d"][:, oc * 128:(oc + 1) * 128], ch="f0z%d" % (oc % 2))
            bk_ = 4 + oc % 2
            b.wait_ev("tensor", evzd)
            b.wait_ev("tensor", ev_csh)
            b.pe_bank_wait((bk_,))
            for nci in range(GN):
                hb2 = 0 if nci < 43 else 1
                if nci == 0:
                    b.wait_ev("tensor", evwd[0])
                if nci == 43:
                    b.wait_ev("tensor", evwd[1])
                dv = f2_wd[hb2].rearrange("p (c n) -> p c n", n=128)
                inc = "eT" if nci in (42, GN - 1) else None
                ev_ = b.op("tensor", mm(bank(bk_), dv[:, nci - 43 * hb2, :],
                                        hv[:, nci, :], nci == 0, False), inc=inc)
                if nci == 42:
                    f0_used[("wd", 0)] = ev_
                elif nci == GN - 1:
                    f0_used[("wd", 1)] = ev_
            evt = b.op("tensor", mm(bank(bk_), zs4[oc % 2][0:86, :], f2_cs[0:86, :],
                                    False, True), inc="eT")
            zs_used[oc % 2] = evt
            cs_free = evt
            f2h_used = evt
            sp = oc % 2
            b.wait_ev("scalar", evt)
            b.wait_ev("scalar", ev_g_store.get(sp))
            eva = b.op("scalar", lambda e, o=f2_g[sp], i=bank(bk_):
                       e.activation(o, i, AF.Silu), inc="eA")
            b.bank_free[bk_] = eva
            b.wait_ev("sync", eva)
            ev_g_store[sp] = b.dma("sync", g_s[oc * 128:(oc + 1) * 128,
                                              mb * FB:(mb + 1) * FB], f2_g[sp],
                                   ch="ffgs%d" % sp)
    ev_g_all = [b.qnow("ffgs0"), b.qnow("ffgs1")]

    cb_used = {}
    for _e in ev_ctx_all + ev_g_all:
        b.wait_ev("sync", _e)
    for sci in range(16):
        bi = sci % 2
        l0 = sci * 128
        b.wait_ev("sync", cb_used.get(bi))
        b.dma("sync", cb_x[bi], xd[l0:l0 + 128, :], ch="cbl%d" % bi)
        b.dma("sync", cb_c[bi], ctx_tm[l0:l0 + 128, :], ch="cbl%d" % bi)
        ev_ld = None
        for j in range(8):
            ev_ld = b.dma("sync", cb_g[bi].rearrange("p (c n) -> p c n", n=512)[:, j, :],
                          g_s[j * 512:(j + 1) * 512, l0:l0 + 128], transpose=True,
                          ch="cbl%d" % bi)
        b.wait_ev("vector", ev_ld)
        b.wait_ev("vector", cb_used.get(("o", bi)))
        b.opb("vector", lambda e, o=cb_t, c=cb_c[bi], g=cb_g[bi]:
              e.tensor_tensor(o, c, g, ALU.mult), inc="eV")
        evv = b.op("vector", lambda e, o=cb_o[bi], t=cb_t, x=cb_x[bi]:
                   e.tensor_tensor(o, t, x, ALU.add), inc="eV")
        cb_used[bi] = evv
        b.wait_ev("sync", evv)
        cb_used[("o", bi)] = b.dma("sync", outd[l0:l0 + 128, :], cb_o[bi],
                                   ch="cbo%d" % bi)

    for name, (dd, src) in dump.items():
        b.dma("sync", dd, src, ch="dump")

    GP_CH = ("bt2", "castq", "castk", "castv", "d0s", "dqw0", "dqw1", "dqr",
             "dqog", "dqou", "dqod")
    for name in list(b.cnt):
        if name.startswith("q_"):
            eng = "gpsimd" if name[2:] in GP_CH else "sync"
            b.wait(eng, name, b.cnt[name])

    if not b.check_deadlock():
        raise RuntimeError("offline deadlock check failed")

    with nc.Block() as block:
        @block.sync
        def _(e):
            for f in b.ops["sync"]:
                f(e)

        @block.gpsimd
        def _(e):
            for f in b.ops["gpsimd"]:
                f(e)

        @block.tensor
        def _(e):
            for f in b.ops["tensor"]:
                f(e)

        @block.vector
        def _(e):
            for f in b.ops["vector"]:
                f(e)

        @block.scalar
        def _(e):
            for f in b.ops["scalar"]:
                f(e)

    return nc


def kernel(**inputs):
    f32 = np.float32
    x = np.ascontiguousarray(np.asarray(inputs["L0_attn_hidden_states"], dtype=f32))
    host = {}
    for w, kw in (("q", "L0_attn_wq"), ("k", "L0_attn_wk"), ("v", "L0_attn_wv")):
        host[f"w{w}T"] = np.ascontiguousarray(np.asarray(inputs[kw], dtype=f32).T)
    for w, kb in (("q", "L0_attn_bq"), ("k", "L0_attn_bk")):
        host[f"b{w}t"] = np.ascontiguousarray(
            np.asarray(inputs[kb], dtype=f32).reshape(32, 128).T)
    host["bv"] = np.ascontiguousarray(
        np.asarray(inputs["L0_attn_bv"], dtype=f32).reshape(1, D))
    for pre, nm in (("g", "gate"), ("u", "up"), ("d", "down")):
        host[pre + "qw"] = np.ascontiguousarray(
            np.asarray(inputs[f"L0_ffn_{nm}_qweight"], dtype=np.int32))
        sc = np.ascontiguousarray(np.asarray(inputs[f"L0_ffn_{nm}_scales"], dtype=f32))
        host[pre + "sc"] = sc
        host[pre + "qz"] = np.ascontiguousarray(
            np.asarray(inputs[f"L0_ffn_{nm}_qzeros"], dtype=np.int32))

    nc = build_program()
    in_maps = []
    for c in range(8):
        m = dict(host)
        h = c % 2
        m["x"] = np.ascontiguousarray(x[c // 2][h * SOWN:(h + 1) * SOWN])
        in_maps.append(m)
    res = run_bass_kernel_spmd(nc, in_maps, core_ids=list(range(8)))
    out = np.empty((B, S, D), dtype=f32)
    for c in range(8):
        h = c % 2
        out[c // 2, h * SOWN:(h + 1) * SOWN, :] = res.results[c]["out"]
    return out


if __name__ == "__main__":
    pass


# version 12
# speedup vs baseline: 1.4279x; 1.1756x over previous
import numpy as np
import ml_dtypes
import concourse.bass as bass
import concourse.mybir as mybir
from concourse.bass_utils import run_bass_kernel_spmd

F32 = mybir.dt.float32
BF16 = mybir.dt.bfloat16
I32 = mybir.dt.int32
U8 = mybir.dt.uint8
AF = mybir.ActivationFunctionType
ALU = mybir.AluOpType
AX = mybir.AxisListType
BF = ml_dtypes.bfloat16

D = 4096
S = 4096
B = 4
H = 32
I = 11008
SOWN = 2048
KC = 32
GN = 86
OC = 32
MT = 256
NMT = SOWN // MT
FB = 512
NFB = SOWN // FB
RD = float(1.0 / np.sqrt(128.0))
LNEPS = 1e-5
AWQ_L = [0, 2, 4, 6, 1, 3, 5, 7]

ENGINES = ("sync", "gpsimd", "tensor", "vector", "scalar")

DUMP = ()


def ceil_div(a, b):
    return (a + b - 1) // b


class Build:
    def __init__(self, nc):
        self.nc = nc
        self.ops = {e: [] for e in ENGINES}
        self.sem = {}
        self.cnt = {}
        for name in ("eT", "eA", "eV", "eP"):
            self.sem[name] = nc.alloc_semaphore(name)
            self.cnt[name] = 0
        self.bank_free = {}
        self.trace = {}

    def emit(self, eng, fn, meta=None):
        self.ops[eng].append(fn)
        self.trace.setdefault(eng, []).append(meta or ("op", None, 0))

    def wait(self, eng, semname, val):
        if val is None or val <= 0:
            return
        s = self.sem[semname]
        self.emit(eng, lambda e, s=s, v=val: e.wait_ge(s, v),
                  meta=("wait", semname, val))

    def wait_ev(self, eng, ev):
        if ev:
            self.wait(eng, ev[0], ev[1])

    def dma(self, eng, out, in_, ch="misc", **kw):
        semname = "q_" + ch
        if semname not in self.sem:
            self.sem[semname] = self.nc.alloc_semaphore(semname)
            self.cnt[semname] = 0
        s = self.sem[semname]
        self.cnt[semname] += 16
        self.emit(eng, lambda e, o=out, i=in_, s=s, kw=kw:
                  e.dma_start(out=o, in_=i, **kw).then_inc(s, 16),
                  meta=("inc", semname, 16))
        return (semname, self.cnt[semname])

    def qnow(self, ch):
        return ("q_" + ch, self.cnt.get("q_" + ch, 0))

    def op(self, eng, fn, inc=None):
        if inc:
            s = self.sem[inc]
            self.cnt[inc] += 1
            v = self.cnt[inc]
            self.emit(eng, lambda e, fn=fn, s=s: fn(e).then_inc(s, 1),
                      meta=("inc", inc, 1))
            return (inc, v)
        self.emit(eng, lambda e, fn=fn: fn(e))
        return None

    def opb(self, eng, fn, inc):
        ev = self.op(eng, fn, inc=inc)
        self.wait_ev(eng, ev)
        return ev

    def now(self, semname):
        return (semname, self.cnt[semname])

    def pe_bank_wait(self, banks):
        for bk in banks:
            ev = self.bank_free.get(bk)
            self.wait_ev("tensor", ev)
            self.bank_free[bk] = None

    def check_deadlock(self):
        import collections
        ptr = {e: 0 for e in ENGINES}
        cnt = collections.defaultdict(int)
        n = {e: len(self.trace.get(e, [])) for e in ENGINES}
        progress = True
        while progress:
            progress = False
            for e in ENGINES:
                while ptr[e] < n[e]:
                    kind, sem, val = self.trace[e][ptr[e]]
                    if kind == "wait" and cnt[sem] < val:
                        break
                    if kind == "inc":
                        cnt[sem] += val
                    ptr[e] += 1
                    progress = True
        stuck = {e: (ptr[e], n[e]) for e in ENGINES if ptr[e] < n[e]}
        if stuck:
            print("DEADLOCK! stuck engines:", stuck)
            for e, (p, _) in stuck.items():
                kind, sem, val = self.trace[e][p]
                print(f"  {e}@{p}: waiting {sem} >= {val}, have {cnt[sem]}")
            return False
        return True


def build_program():
    nc = bass.Bass("TRN2", target_bir_lowering=False)

    xd = nc.dram_tensor("x", [SOWN, D], F32, kind="ExternalInput").ap()
    wT = {w: nc.dram_tensor(f"w{w}T", [D, D], F32, kind="ExternalInput").ap()
          for w in ("q", "k", "v")}
    bT = {w: nc.dram_tensor(f"b{w}t", [128, 32], F32, kind="ExternalInput").ap()
          for w in ("q", "k")}
    bvd = nc.dram_tensor("bv", [1, D], F32, kind="ExternalInput").ap()
    qw_d = {"g": nc.dram_tensor("gqw", [D, 1376], I32, kind="ExternalInput").ap(),
            "u": nc.dram_tensor("uqw", [D, 1376], I32, kind="ExternalInput").ap(),
            "d": nc.dram_tensor("dqw", [I, 512], I32, kind="ExternalInput").ap()}
    sc_d = {"g": nc.dram_tensor("gsc", [32, I], F32, kind="ExternalInput").ap(),
            "u": nc.dram_tensor("usc", [32, I], F32, kind="ExternalInput").ap(),
            "d": nc.dram_tensor("dsc", [86, D], F32, kind="ExternalInput").ap()}
    qz_d = {"g": nc.dram_tensor("gqz", [32, 1376], I32, kind="ExternalInput").ap(),
            "u": nc.dram_tensor("uqz", [32, 1376], I32, kind="ExternalInput").ap(),
            "d": nc.dram_tensor("dqz", [86, 512], I32, kind="ExternalInput").ap()}
    outd = nc.dram_tensor("out", [SOWN, D], F32, kind="ExternalOutput").ap()

    xn_s = nc.dram_tensor("xn_s", [SOWN, D], BF16, kind="Internal").ap()
    wq7 = nc.dram_tensor("wq7", [32, 128, 32, 128], BF16, kind="Internal").ap()
    wk7 = nc.dram_tensor("wk7", [32, 128, 32, 128], BF16, kind="Internal").ap()
    wv7 = nc.dram_tensor("wv7", [16, 128, 32, 256], BF16, kind="Internal").ap()
    w7 = {"q": wq7, "k": wk7, "v": wv7}
    v_h8 = nc.dram_tensor("v_h8", [32, SOWN, 128], BF16, kind="Internal").ap()
    ctx_tm = nc.dram_tensor("ctx_tm", [SOWN, D], BF16, kind="Internal").ap()
    den_d = nc.dram_tensor("den_d", [SOWN, H], F32, kind="Internal").ap()
    g_s = nc.dram_tensor("g_s", [D, SOWN], BF16, kind="Internal").ap()
    Wg_t = nc.dram_tensor("Wg_t", [GN, 128, KC, 128], BF16, kind="Internal").ap()
    Wu_t = nc.dram_tensor("Wu_t", [GN, 128, KC, 128], BF16, kind="Internal").ap()
    Wd_t = nc.dram_tensor("Wd_t", [OC, 128, GN, 128], BF16, kind="Internal").ap()
    zs_s = {"g": nc.dram_tensor("zs_g", [32, I], BF16, kind="Internal").ap(),
            "u": nc.dram_tensor("zs_u", [32, I], BF16, kind="Internal").ap(),
            "d": nc.dram_tensor("zs_d", [86, D], BF16, kind="Internal").ap()}
    scb_d = {"g": nc.dram_tensor("scb_g", [32, I], BF16, kind="Internal").ap(),
             "u": nc.dram_tensor("scb_u", [32, I], BF16, kind="Internal").ap(),
             "d": nc.dram_tensor("scb_d", [86, D], BF16, kind="Internal").ap()}
    h_d = nc.dram_tensor("h_d", [GN, 128, SOWN], BF16, kind="Internal").ap()

    dump = {}
    for name, src in (("xn", xn_s), ("v", v_h8), ("ctx", ctx_tm), ("g", g_s),
                      ("wg", Wg_t), ("wu", Wu_t), ("wd", Wd_t)):
        if name in DUMP:
            dump[name] = (nc.dram_tensor("dump_" + name, list(src.shape), BF16,
                                         kind="ExternalOutput").ap(), src)

    ones_bf = nc.alloc_sbuf_tensor("ones_bf", [128, 128], BF16)
    bt_sb = {w: nc.alloc_sbuf_tensor(f"bt_{w}", [128, 32], F32) for w in ("q", "k")}
    bv_sb = nc.alloc_sbuf_tensor("bv_sb", [1, D], BF16)
    epsc = nc.alloc_sbuf_tensor("epsc", [128, 1], F32)
    ARENA_KB = 198
    arena = nc.alloc_sbuf_tensor("arena", [128, ARENA_KB * 1024], U8)

    def view(off_b, n, dtype):
        sz = {F32: 4, BF16: 2, I32: 4}[dtype]
        assert off_b + n * sz <= ARENA_KB * 1024, (off_b, n, dtype)
        return arena[:, off_b:off_b + n * sz].bitcast(dtype)

    K1 = 1024
    ln_x = [view(0, D, F32), view(16 * K1, D, F32)]
    ln_sq = view(32 * K1, D, BF16)
    ln_xn = [view(40 * K1, D, BF16), view(48 * K1, D, BF16)]
    ln_st = view(56 * K1, 16, F32)
    d0_qz = view(64 * K1, 1376, I32)
    d0_z = view(70 * K1, 1376, I32)
    d0_sc = view(76 * K1, 1376, F32)
    d0_zs = view(82 * K1, 1376, BF16)
    d0_scb = view(86 * K1, 1376, BF16)

    pj_xn = [view(0, KC * MT, BF16), view(16 * K1, KC * MT, BF16)]
    pj_w = [view(32 * K1, KC * 128, BF16), view(40 * K1, KC * 128, BF16)]
    pj_wv = [view(48 * K1, KC * 256, BF16), view(64 * K1, KC * 256, BF16)]
    q_sl = view(80 * K1, H * MT, BF16)
    k_sl = view(96 * K1, H * MT, BF16)
    mx_v = [view(112 * K1, 32 * 128, BF16), view(120 * K1, 32 * 128, BF16)]
    mx_p = [view(128 * K1, 512, BF16), view(128 * K1 + 1024, 512, BF16)]
    mx_cx = [view(130 * K1, 16 * 128, BF16), view(134 * K1, 16 * 128, BF16)]
    pj_ev = [view(138 * K1 + o * 1024, 512, BF16) for o in range(4)]
    den_sb = [view(142 * K1, 512, F32), view(144 * K1, 512, F32)]
    _o = 146 * K1
    dqoff = {}
    for nm, nb in (("qw0", 688 * 4), ("qw1", 688 * 4), ("sr0", 2752 * 2),
                   ("sr1", 2752 * 2), ("nib", 2752 * 4), ("wout", 5504 * 2)):
        dqoff[nm] = _o
        _o += nb
    assert _o <= ARENA_KB * K1, _o

    def dqv(name, n, dtype):
        sz = {F32: 4, BF16: 2, I32: 4}[dtype]
        return arena[:, dqoff[name]:dqoff[name] + n * sz].bitcast(dtype)

    dq_qw = [dqv("qw0", 688, I32), dqv("qw1", 688, I32)]
    dq_sr = [dqv("sr0", 2752, BF16), dqv("sr1", 2752, BF16)]
    dq_nib = dqv("nib", 2752, I32)
    dq_wout = dqv("wout", 5504, BF16)

    fx_full = view(0, KC * SOWN, BF16)
    f0_wg = [view(131072, KC * 128, BF16), view(139264, KC * 128, BF16)]
    f0_wu = [view(147456, KC * 128, BF16), view(155648, KC * 128, BF16)]
    h_t = [view(163840, SOWN, BF16), view(167936, SOWN, BF16)]
    f0_sil = [view(172032, 512, BF16), view(173056, 512, BF16)]
    cs4 = view(174080, 4 * 512, BF16)
    oh32 = view(178176, KC * 32, BF16)
    zs4 = [view(180224 + i * 256, 128, BF16) for i in range(4)]
    f2_h = view(0, GN * 512, BF16)
    f2_wd = [view(88064, 43 * 128, BF16), view(99072, 43 * 128, BF16)]
    oh86 = view(110080, 86 * 86, BF16)
    f2_g = [view(124928, 512, BF16), view(125952, 512, BF16)]
    f2_cs = view(126976, 512, BF16)
    cb_x = [view(0, D, F32), view(16 * K1, D, F32)]
    cb_c = [view(32 * K1, D, BF16), view(40 * K1, D, BF16)]
    cb_g = [view(48 * K1, D, BF16), view(56 * K1, D, BF16)]
    cb_t = view(64 * K1, D, F32)
    cb_o = [view(80 * K1, D, F32), view(96 * K1, D, F32)]
    cb_dn = [view(112 * K1, 32, F32), view(112 * K1 + 128, 32, F32)]
    cb_r = [view(112 * K1 + 256, 32, F32), view(112 * K1 + 384, 32, F32)]
    cb_t2 = view(116 * K1, D, F32)

    P = nc.alloc_psum_tensor("P", [128, 4096], F32)

    def bank(i, n=1):
        return P[:, i * 512:(i + n) * 512]

    b = Build(nc)

    def mm(out, lhsT, rhs, start=True, stop=True):
        return lambda e: e.matmul(out, lhsT, rhs, start=start, stop=stop)

    def lp(fn, *a, **kw):
        with nc.allow_low_precision(reason="bf16 operand rounding (within bf16-matmul precision)"):
            return fn(*a, **kw)

    b.op("vector", lambda e: e.memset(ones_bf[:], 1.0))
    ev_const = b.op("vector", lambda e: e.memset(epsc[:], LNEPS), inc="eV")
    b.wait_ev("scalar", ev_const)
    b.wait_ev("gpsimd", ev_const)
    b.wait_ev("tensor", ev_const)
    for w in ("q", "k"):
        ev_bt = b.dma("sync", bt_sb[w][:], bT[w], ch="bt")
    ev_bv = b.dma("gpsimd", bv_sb[:], bvd, ch="bt2")

    ev_wcast = {}
    for w in ("q", "k", "v"):
        nn = 256 if w == "v" else 128
        for kc in range(KC):
            ev_wcast[w] = b.dma(
                "gpsimd", w7[w][:, :, kc, :].rearrange("a p n -> p a n"),
                wT[w][kc * 128:(kc + 1) * 128, :].rearrange("p (a n) -> p a n", n=nn),
                ch="cast" + w)

    WCFG = {
        "g": dict(K=KC, N=I, pk=1376, G=32, qwd=qw_d["g"], scd=sc_d["g"], qzd=qz_d["g"]),
        "u": dict(K=KC, N=I, pk=1376, G=32, qwd=qw_d["u"], scd=sc_d["u"], qzd=qz_d["u"]),
        "d": dict(K=GN, N=D, pk=512, G=86, qwd=qw_d["d"], scd=sc_d["d"], qzd=qz_d["d"]),
    }
    ev_d0_v = None
    d0_scb_done = {}
    for w in ("g", "u", "d"):
        cfg = WCFG[w]
        G, N = cfg["G"], cfg["N"]
        nblk = 8
        blkn = N // nblk
        blkp = blkn // 8
        if ev_d0_v:
            b.wait_ev("sync", ev_d0_v)
        b.dma("sync", d0_qz[0:G, 0:blkp * nblk], cfg["qzd"], ch="d0l")
        for blk in range(nblk):
            if ev_d0_v:
                b.wait_ev("sync", ev_d0_v)
            ev_sc = b.dma("sync", d0_sc[0:G, 0:blkn],
                          cfg["scd"][:, blk * blkn:(blk + 1) * blkn], ch="d0l")
            b.wait_ev("vector", ev_sc)
            ev_scb = b.op("vector", lambda e, o=d0_scb[0:G, 0:blkn],
                          i=d0_sc[0:G, 0:blkn]: e.tensor_copy(o, i), inc="eV")
            for j in range(8):
                qzsl = d0_qz[0:G, blk * blkp:(blk + 1) * blkp]
                fn = (lambda e, o=d0_z[0:G, AWQ_L[j]:blkn:8], i=qzsl, sh=4 * j:
                      e.tensor_scalar(o, i, sh, 15, ALU.logical_shift_right,
                                      ALU.bitwise_and))
                if j == 7:
                    b.opb("vector", fn, inc="eV")
                else:
                    b.op("vector", fn)
            ev_d0_v = b.op("vector", lambda e, o=d0_zs[0:G, 0:blkn], z=d0_z[0:G, 0:blkn],
                           s=d0_sc[0:G, 0:blkn]: e.tensor_tensor(o, z, s, ALU.mult),
                           inc="eV")
            b.wait_ev("gpsimd", ev_scb)
            b.dma("gpsimd", scb_d[w][:, blk * blkn:(blk + 1) * blkn],
                  d0_scb[0:G, 0:blkn], ch="d0s")
            b.wait_ev("gpsimd", ev_d0_v)
            ev_zs = b.dma("gpsimd", zs_s[w][:, blk * blkn:(blk + 1) * blkn],
                          d0_zs[0:G, 0:blkn], ch="d0s")
            b.wait_ev("vector", ev_zs)
        d0_scb_done[w] = b.qnow("d0s")
    ev_d0_done = b.qnow("d0s")
    ev_d0_vec = b.now("eV")

    c_inv = 1.0 / D
    ev_ln_act = [None, None]
    for i in range(16):
        bi = i % 2
        if ev_ln_act[bi]:
            b.wait_ev("sync", ev_ln_act[bi])
        ev_x = b.dma("sync", ln_x[bi], xd[i * 128:(i + 1) * 128, :], ch="lnx%d" % bi)
        st = ln_st
        b.wait_ev("vector", ev_x)
        b.opb("vector", lambda e, o=st[:, 0:1], i2=ln_x[bi]:
              e.reduce_sum(o, i2, axis=AX.X), inc="eV")
        b.wait_ev("scalar", ev_x)
        eva = b.op("scalar", lambda e, o=ln_sq, i2=ln_x[bi], a=st[:, 1:2]:
                   e.activation(o, i2, AF.Square, accum_out=a), inc="eA")
        b.wait_ev("vector", eva)
        b.opb("vector", lambda e, o=st[:, 2:3], i2=st[:, 0:1]:
              e.tensor_scalar_mul(o, i2, c_inv), inc="eV")
        b.opb("vector", lambda e, o=st[:, 3:4], i2=st[:, 2:3]:
              e.tensor_tensor(o, i2, i2, ALU.mult), inc="eV")
        evv = b.opb("vector", lambda e, o=st[:, 4:5], i2=st[:, 1:2], m2=st[:, 3:4]:
                    e.scalar_tensor_tensor(o, i2, c_inv, m2, ALU.mult, ALU.subtract),
                    inc="eV")
        b.wait_ev("scalar", evv)
        eva = b.op("scalar", lambda e, o=st[:, 5:6], i2=st[:, 4:5]:
                   e.activation(o, i2, AF.Sqrt, bias=epsc[:]), inc="eA")
        b.wait_ev("vector", eva)
        b.opb("vector", lambda e, o=st[:, 6:7], i2=st[:, 5:6]:
              e.reciprocal(o, i2), inc="eV")
        evv = b.opb("vector", lambda e, o=st[:, 7:8], i2=st[:, 2:3], r=st[:, 6:7]:
                    e.scalar_tensor_tensor(o, i2, -1.0, r, ALU.mult, ALU.mult), inc="eV")
        b.wait_ev("scalar", evv)
        ev_ln_act[bi] = b.op("scalar", lambda e, o=ln_xn[bi], i2=ln_x[bi],
                             r=st[:, 6:7], m=st[:, 7:8]:
                             e.activation(o, i2, AF.Identity, bias=m, scale=r), inc="eA")
        b.wait_ev("sync", ev_ln_act[bi])
        b.dma("sync", xn_s[i * 128:(i + 1) * 128, :], ln_xn[bi], ch="lnst")
    ev_ln_all = b.qnow("lnst")

    dq_state = {}
    dq_halves = []
    for w in ("g", "u", "d"):
        for ch in range(WCFG[w]["K"]):
            for h2 in (0, 1):
                dq_halves.append((w, ch, h2))
    NQ = 2 * len(dq_halves)

    def dq_issue_qw(t):
        w, ch, h2 = dq_halves[t]
        hq = WCFG[w]["pk"] // 2
        qb = t % 2
        b.wait_ev("gpsimd", dq_state.get(("qwfree", qb)))
        dq_state[("qwld", t)] = b.dma(
            "gpsimd", dq_qw[qb][:, 0:hq],
            WCFG[w]["qwd"][ch * 128:(ch + 1) * 128, h2 * hq:(h2 + 1) * hq],
            ch="dqw%d" % qb)

    def dq_issue_sr(u):
        t, qih = divmod(u, 2)
        w, ch, h2 = dq_halves[t]
        qn = WCFG[w]["N"] // 4
        hf = h2 * 2 + qih
        sb_ = u % 2
        b.wait_ev("gpsimd", dq_state.get(("srfree", sb_)))
        b.wait_ev("gpsimd", d0_scb_done[w])
        dq_state[("srld", u)] = b.dma(
            "gpsimd", dq_sr[sb_][:, 0:qn],
            scb_d[w][ch:ch + 1, hf * qn:(hf + 1) * qn].partition_broadcast(128),
            ch="dqs%d" % sb_)

    def dequant_steps():
        for t0 in (0, 1):
            dq_issue_qw(t0)
        for u0 in (0, 1):
            dq_issue_sr(u0)
        for t, (w, ch, h2) in enumerate(dq_halves):
            cfg = WCFG[w]
            N, pk = cfg["N"], cfg["pk"]
            qn = N // 4
            qp = pk // 4
            nhalf = (N // 2) // 128
            qb = t % 2
            tgt = {"g": Wg_t, "u": Wu_t, "d": Wd_t}[w]
            evm = None
            for qih in range(2):
                u = 2 * t + qih
                sb_ = u % 2
                if qih == 0:
                    b.wait_ev("vector", dq_state[("qwld", t)])
                    b.wait_ev("vector", dq_state.get("woutfree"))
                for j in range(8):
                    src = dq_qw[qb][:, qih * qp:(qih + 1) * qp]
                    fn = (lambda e, o=dq_nib[:, AWQ_L[j]:qn:8], i=src, sh=4 * j:
                          e.tensor_scalar(o, i, sh, 15, ALU.logical_shift_right,
                                          ALU.bitwise_and))
                    if j == 7:
                        evup = b.opb("vector", fn, inc="eV")
                    else:
                        b.op("vector", fn)
                if qih == 1:
                    dq_state[("qwfree", qb)] = evup
                    if t + 2 < len(dq_halves):
                        dq_issue_qw(t + 2)
                b.wait_ev("vector", dq_state[("srld", u)])
                evm = b.op("vector", lambda e, o=dq_wout[:, qih * qn:(qih + 1) * qn],
                           i=dq_nib[:, 0:qn], s=dq_sr[sb_][:, 0:qn]:
                           e.tensor_tensor(o, i, s, ALU.mult), inc="eV")
                dq_state[("srfree", sb_)] = evm
                if u + 2 < NQ:
                    dq_issue_sr(u + 2)
                yield
            b.wait_ev("gpsimd", evm)
            dst = tgt[h2 * nhalf:(h2 + 1) * nhalf, :, ch, :]
            dq_state["woutfree"] = b.dma("gpsimd", dst.rearrange("a p n -> p a n"),
                                         dq_wout[:, 0:2 * qn]
                                         .rearrange("p (a n) -> p a n", n=128),
                                         ch="dqo" + w)
        dq_state["done"] = b.now("eV")
        while True:
            yield

    dq_gen = dequant_steps()

    ev_wslab = [None, None]
    ev_wvslab = [None, None]
    wslab_used = [None, None]
    wvslab_used = [None, None]
    xn_used = {}
    ev_xn_b = {}
    ev_ev_store = {}
    qk_ready = None
    mixv_used = [None, None]
    last_exp = {}
    pfree_t = {}
    denfree = {}
    cxfree = {}

    for mt in range(NMT):
        m0 = mt * MT
        bx = mt % 2
        if mt == 0:
            b.wait_ev("sync", ev_ln_all)
            b.wait_ev("sync", ev_d0_done)
            b.wait_ev("sync", ev_d0_vec)
            b.wait_ev("scalar", ev_d0_done)
            b.wait_ev("scalar", ev_d0_vec)
        b.wait_ev("sync", xn_used.get(bx))
        for kc in range(KC):
            ev_xn = b.dma("sync", pj_xn[bx].rearrange("p (c n) -> p c n", n=MT)[:, kc, :],
                          xn_s[m0:m0 + MT, kc * 128:(kc + 1) * 128], transpose=True,
                          ch="pjx%d" % bx)
        ev_xn_b[bx] = ev_xn
        xv = pj_xn[bx].rearrange("p (c n) -> p c n", n=MT)
        for w, dslab in (("q", q_sl), ("k", k_sl)):
            for ncn in range(H):
                wb2 = ncn % 2
                b.wait_ev("sync", ev_wcast[w])
                b.wait_ev("sync", wslab_used[wb2])
                ev_wslab[wb2] = b.dma(
                    "sync", pj_w[wb2][:, 0:KC * 128],
                    w7[w][ncn].rearrange("p c n -> p (c n)"), ch="pjw%d" % wb2)
                pbank = ncn % 2
                b.wait_ev("tensor", ev_xn)
                b.wait_ev("tensor", ev_wslab[wb2])
                wv_ = pj_w[wb2].rearrange("p (c n) -> p c n", n=128)
                b.pe_bank_wait((pbank,))
                evt = None
                for kc in range(KC):
                    evt = b.op("tensor", mm(bank(pbank)[:, 0:MT], wv_[:, kc, :],
                                            xv[:, kc, :], kc == 0, kc == KC - 1),
                               inc="eT" if kc == KC - 1 else None) or evt
                wslab_used[wb2] = evt
                xn_used[bx] = evt
                b.wait_ev("scalar", evt)
                if mt == 0 and w == "q" and ncn == 0:
                    b.wait_ev("scalar", ev_bt)
                eva = b.op("scalar", lambda e,
                           o=dslab.rearrange("p (c n) -> p c n", n=MT)[:, ncn, :],
                           i=bank(pbank)[:, 0:MT], bias=bt_sb[w][:, ncn:ncn + 1]:
                           e.activation(o, i, AF.Identity, bias=bias), inc="eA")
                b.bank_free[pbank] = eva
                qk_ready = eva
                next(dq_gen)
        for ns in range(16):
            wb2 = ns % 2
            b.wait_ev("sync", ev_wcast["v"])
            b.wait_ev("sync", wvslab_used[wb2])
            ev_wvslab[wb2] = b.dma(
                "sync", pj_wv[wb2][:, 0:KC * 256],
                w7["v"][ns].rearrange("p c n -> p (c n)"), ch="pjv%d" % wb2)
            wvv = pj_wv[wb2].rearrange("p (c n) -> p c n", n=256)
            for mc in range(MT // 128):
                pbank = (ns * 2 + mc) % 2
                b.wait_ev("tensor", ev_xn)
                b.wait_ev("tensor", ev_wvslab[wb2])
                if mt == 0 and ns == 0 and mc == 0:
                    b.wait_ev("tensor", ev_bv)
                b.pe_bank_wait((pbank,))
                for kc in range(KC):
                    b.op("tensor", mm(bank(pbank)[:, 0:256],
                                      xv[:, kc, mc * 128:(mc + 1) * 128],
                                      wvv[:, kc, :], kc == 0, False))
                evt = b.op("tensor", mm(bank(pbank)[:, 0:256], ones_bf[0:1, :],
                                        bv_sb[0:1, ns * 256:(ns + 1) * 256], False, True),
                           inc="eT")
                wvslab_used[wb2] = evt
                xn_used[bx] = evt
                sb_i = (ns * 2 + mc) % 4
                b.wait_ev("scalar", evt)
                b.wait_ev("scalar", ev_ev_store.get(sb_i))
                eva = b.op("scalar", lambda e, o=pj_ev[sb_i][:, 0:256],
                           i=bank(pbank)[:, 0:256]:
                           e.activation(o, i, AF.Identity), inc="eA")
                b.bank_free[pbank] = eva
                b.wait_ev("sync", eva)
                ev_ev_store[sb_i] = b.dma(
                    "sync", v_h8[ns * 2:(ns + 1) * 2, m0 + mc * 128:m0 + (mc + 1) * 128, :]
                    .rearrange("t m d -> m t d"),
                    pj_ev[sb_i][:, 0:256].rearrange("p (t d) -> p t d", d=128),
                    ch="pjev%d" % sb_i)
            next(dq_gen)
        ev_vtm = [b.qnow("pjev%d" % i) for i in range(4)]
        qv3 = q_sl.rearrange("p (c n) -> p c n", n=MT)
        kv3 = k_sl.rearrange("p (c n) -> p c n", n=MT)
        ev_mv = None
        for g16 in range(MT // 16):
            gm0 = g16 * 16
            par = g16 % 2
            if gm0 % 32 == 0:
                vb = (gm0 // 32) % 2
                for _e in ev_vtm:
                    b.wait_ev("sync", _e)
                b.wait_ev("sync", mixv_used[vb])
                ev_mv = b.dma("sync",
                              mx_v[vb].rearrange("p (c n) -> p c n", n=128)[0:32],
                              v_h8[:, m0 + gm0:m0 + gm0 + 32, :], ch="mixv%d" % vb)
            vb = (gm0 // 32) % 2
            vbase = (gm0 // 32) * 32
            scb = 4 + par
            b.wait_ev("tensor", qk_ready)
            b.wait_ev("tensor", last_exp.get(scb))
            b.pe_bank_wait((scb,))
            evt = None
            for j in range(16):
                mloc = gm0 + j
                evt = b.op("tensor", mm(bank(scb)[0:32, j * 32:(j + 1) * 32],
                                        kv3[:, :, mloc], qv3[:, :, mloc]),
                           inc="eT" if j == 15 else None) or evt
            b.wait_ev("scalar", evt)
            b.wait_ev("scalar", pfree_t.get(par))
            eva = b.op("scalar", lambda e, o=mx_p[par][0:32, :], i=bank(scb)[0:32, :]:
                       e.activation(o, i, AF.Exp, scale=RD), inc="eA")
            last_exp[scb] = eva
            cb = 6 + par
            b.wait_ev("tensor", eva)
            b.pe_bank_wait((cb,))
            evt_d = b.op("tensor", mm(bank(cb)[0:1, :], ones_bf[0:32, 0:1],
                                      mx_p[par][0:32, :]), inc="eT")
            b.wait_ev("scalar", evt_d)
            b.wait_ev("scalar", denfree.get(par))
            ev_den = b.op("scalar", lambda e, o=den_sb[par][0:1, :], i=bank(cb)[0:1, :]:
                          e.activation(o, i, AF.Identity), inc="eA")
            b.bank_free[cb] = ev_den
            b.wait_ev("sync", ev_den)
            gidx = (m0 + gm0) // 16
            denfree[par] = b.dma(
                "sync", den_d.rearrange("(a m) h -> a (m h)", m=16)[gidx:gidx + 1, :],
                den_sb[par][0:1, :], ch="den%d" % par)
            vv3 = mx_v[vb].rearrange("p (c n) -> p c n", n=128)
            cxb = par
            last_pv_evac = None
            for rnd in range(4):
                b.wait_ev("tensor", ev_mv)
                if rnd > 0:
                    b.wait_ev("tensor", last_pv_evac)
                b.pe_bank_wait((cb,))
                evt = None
                for j in range(4):
                    tok = gm0 + rnd * 4 + j
                    evt = b.op("tensor", mm(bank(cb)[0:32, j * 128:(j + 1) * 128],
                                            mx_p[par][0:32,
                                                      (rnd * 4 + j) * 32:
                                                      (rnd * 4 + j + 1) * 32],
                                            vv3[0:32, tok - vbase, :]),
                               inc="eT" if j == 3 else None) or evt
                b.wait_ev("scalar", evt)
                if rnd == 0:
                    b.wait_ev("scalar", cxfree.get(cxb))
                last_pv_evac = b.op("scalar", lambda e,
                                    o=mx_cx[cxb][0:32, rnd * 512:(rnd + 1) * 512],
                                    i=bank(cb)[0:32, :]:
                                    e.activation(o, i, AF.Identity), inc="eA")
                b.bank_free[cb] = last_pv_evac
            pfree_t[par] = b.now("eT")
            b.wait_ev("sync", last_pv_evac)
            cxfree[cxb] = b.dma(
                "sync", ctx_tm[m0 + gm0:m0 + gm0 + 16, :]
                .rearrange("c (h d) -> h c d", d=128),
                mx_cx[cxb][0:32, :].rearrange("p (c n) -> p c n", n=128)[0:32],
                ch="ctm%d" % cxb)
            mixv_used[vb] = b.now("eT")
    while "done" not in dq_state:
        next(dq_gen)
    ev_dq_done = dq_state["done"]
    ev_ctx_all = [b.qnow("ctm0"), b.qnow("ctm1")]

    ev_mix_T = b.now("eT")
    ev_mix_A = b.now("eA")
    ev_mix_V = b.now("eV")
    for w in ("g", "u", "d"):
        b.wait_ev("sync", b.qnow("dqo" + w))
        b.wait_ev("vector", b.qnow("dqo" + w))
        b.wait_ev("scalar", b.qnow("dqo" + w))
    b.wait_ev("sync", ev_d0_done)
    b.wait_ev("sync", ev_mix_T)
    b.wait_ev("sync", ev_mix_A)
    b.wait_ev("sync", ev_mix_V)
    for i in range(2):
        b.wait_ev("sync", b.qnow("ctm%d" % i))
    b.wait_ev("scalar", ev_dq_done)
    b.wait_ev("tensor", ev_dq_done)
    ev_fx = None
    for kc in range(KC):
        for tb in range(4):
            ev_fx = b.dma("sync",
                          fx_full.rearrange("p (c n) -> p c n", n=SOWN)
                          [:, kc, tb * 512:(tb + 1) * 512],
                          xn_s[tb * 512:(tb + 1) * 512, kc * 128:(kc + 1) * 128],
                          transpose=True, ch="ffx")
    fxv = fx_full.rearrange("p (c n) -> p c n", n=SOWN)
    b.op("vector", lambda e: e.memset(oh32[:], 0.0))
    oh32v = oh32.rearrange("p (c n) -> p c n", n=32)
    ev_oh = None
    for kc in range(KC):
        ev_oh = b.op("vector", lambda e, o=oh32v[:, kc, kc:kc + 1]: e.memset(o, 1.0),
                     inc="eV" if kc == KC - 1 else None) or ev_oh
    b.wait_ev("tensor", ev_fx)
    b.wait_ev("tensor", ev_oh)
    ev_cs4 = None
    for tb in range(4):
        b.pe_bank_wait((6,))
        evt = None
        for kc in range(KC):
            evt = b.op("tensor", mm(bank(6)[0:32, :], oh32v[:, kc, :],
                                    fxv[:, kc, tb * 512:(tb + 1) * 512],
                                    kc == 0, kc == KC - 1),
                       inc="eT" if kc == KC - 1 else None) or evt
        b.wait_ev("scalar", evt)
        b.wait_ev("scalar", ev_cs4)
        ev_cs4 = b.op("scalar", lambda e, o=cs4[0:32, tb * 512:(tb + 1) * 512],
                      i=bank(6)[0:32, :]:
                      e.activation(o, i, AF.Identity, scale=-1.0), inc="eA")
        b.bank_free[6] = ev_cs4
    b.wait_ev("tensor", ev_cs4)
    f0_used = {}
    zs_used = {}
    sil_used = {}
    ev_ht_store = {}
    for nci in range(GN):
        wb2 = nci % 2
        b.wait_ev("sync", f0_used.get(wb2))
        evwg = b.dma("sync", f0_wg[wb2][:], Wg_t[nci].rearrange("p c n -> p (c n)"),
                     ch="f0w%d" % wb2)
        evwu = b.dma("sync", f0_wu[wb2][:], Wu_t[nci].rearrange("p c n -> p (c n)"),
                     ch="f0u%d" % wb2)
        b.wait_ev("sync", zs_used.get(wb2))
        b.dma("sync", zs4[wb2][0:32, :], zs_s["g"][:, nci * 128:(nci + 1) * 128],
              ch="f0z%d" % wb2)
        b.wait_ev("sync", zs_used.get(2 + wb2))
        evzu = b.dma("sync", zs4[2 + wb2][0:32, :],
                     zs_s["u"][:, nci * 128:(nci + 1) * 128], ch="f0z%d" % wb2)
        gv = f0_wg[wb2].rearrange("p (c n) -> p c n", n=128)
        uv = f0_wu[wb2].rearrange("p (c n) -> p c n", n=128)
        hb = nci % 2
        b.wait_ev("vector", ev_ht_store.get(hb))
        evt = None
        evv = None
        for tb in range(4):
            bkG, bkU = 2 * tb, 2 * tb + 1
            b.wait_ev("tensor", evwg)
            b.wait_ev("tensor", evwu)
            b.wait_ev("tensor", evzu)
            for bk_, wsl, zi in ((bkG, gv, wb2), (bkU, uv, 2 + wb2)):
                b.pe_bank_wait((bk_,))
                for kc in range(KC):
                    b.op("tensor", mm(bank(bk_), wsl[:, kc, :],
                                      fxv[:, kc, tb * 512:(tb + 1) * 512],
                                      kc == 0, False))
                evt = b.op("tensor", mm(bank(bk_), zs4[zi][0:32, :],
                                        cs4[0:32, tb * 512:(tb + 1) * 512],
                                        False, True), inc="eT")
                zs_used[zi] = evt
            sp = tb % 2
            b.wait_ev("scalar", evt)
            b.wait_ev("scalar", sil_used.get(sp))
            eva = b.op("scalar", lambda e, o=f0_sil[sp], i=bank(bkG):
                       e.activation(o, i, AF.Silu), inc="eA")
            b.bank_free[bkG] = eva
            b.wait_ev("vector", eva)
            evv = b.op("vector", lambda e, o=h_t[hb][:, tb * 512:(tb + 1) * 512],
                       i=f0_sil[sp], u=bank(bkU): e.tensor_tensor(o, i, u, ALU.mult),
                       inc="eV")
            b.bank_free[bkU] = evv
            sil_used[sp] = evv
        f0_used[wb2] = evt
        b.wait_ev("sync", evv)
        ev_ht_store[hb] = b.dma("sync", h_d[nci], h_t[hb][:], ch="hst%d" % hb)

    ev_f0_T = b.now("eT")
    b.wait_ev("sync", ev_f0_T)
    b.wait_ev("vector", ev_f0_T)
    for i in range(2):
        b.wait_ev("sync", ev_ht_store.get(i))
    b.op("vector", lambda e: e.memset(oh86[:], 0.0))
    oh86v = oh86.rearrange("p (c n) -> p c n", n=86)
    ev_oh86 = None
    for j in range(GN):
        ev_oh86 = b.op("vector", lambda e, o=oh86v[:, j, j:j + 1]: e.memset(o, 1.0),
                       inc="eV" if j == GN - 1 else None) or ev_oh86
    ev_g_store = {}
    f2h_used = None
    cs_free = None
    for mb in range(NFB):
        m0 = mb * FB
        b.wait_ev("sync", f2h_used)
        for nci in range(GN):
            b.dma("sync", f2_h.rearrange("p (c n) -> p c n", n=FB)[:, nci, :],
                  h_d[nci][:, m0:m0 + FB], ch="f2h%d" % (nci % 2))
        ev_h = [b.qnow("f2h0"), b.qnow("f2h1")]
        hv = f2_h.rearrange("p (c n) -> p c n", n=FB)
        for _e in ev_h:
            b.wait_ev("tensor", _e)
        b.wait_ev("tensor", ev_oh86)
        b.pe_bank_wait((7,))
        csh_evt = None
        for j in range(GN):
            csh_evt = b.op("tensor", mm(bank(7)[0:86, :], oh86v[:, j, :], hv[:, j, :],
                                        j == 0, j == GN - 1),
                           inc="eT" if j == GN - 1 else None) or csh_evt
        b.wait_ev("scalar", csh_evt)
        b.wait_ev("scalar", cs_free)
        ev_csh = b.op("scalar", lambda e, o=f2_cs[0:86, :], i=bank(7)[0:86, :]:
                      e.activation(o, i, AF.Identity, scale=-1.0), inc="eA")
        b.bank_free[7] = ev_csh
        for oc in range(OC):
            evwd = {0: [], 1: []}
            for hs in range(2):
                b.wait_ev("sync", f0_used.get(("wd", hs)))
                for qd, (c0, cn) in enumerate(((0, 22), (22, 21))):
                    evwd[hs].append(b.dma(
                        "sync", f2_wd[hs][:, c0 * 128:(c0 + cn) * 128],
                        Wd_t[oc, :, hs * 43 + c0:hs * 43 + c0 + cn, :]
                        .rearrange("p c n -> p (c n)"), ch="ffd%d" % (hs * 2 + qd)))
            b.wait_ev("sync", zs_used.get(oc % 2))
            evzd = b.dma("sync", zs4[oc % 2][0:86, :],
                         zs_s["d"][:, oc * 128:(oc + 1) * 128], ch="f0z%d" % (oc % 2))
            bk_ = 4 + oc % 2
            b.wait_ev("tensor", evzd)
            b.wait_ev("tensor", ev_csh)
            b.pe_bank_wait((bk_,))
            for nci in range(GN):
                hb2 = 0 if nci < 43 else 1
                if nci in (0, 43):
                    for _e in evwd[hb2]:
                        b.wait_ev("tensor", _e)
                dv = f2_wd[hb2].rearrange("p (c n) -> p c n", n=128)
                inc = "eT" if nci in (42, GN - 1) else None
                ev_ = b.op("tensor", mm(bank(bk_), dv[:, nci - 43 * hb2, :],
                                        hv[:, nci, :], nci == 0, False), inc=inc)
                if nci == 42:
                    f0_used[("wd", 0)] = ev_
                elif nci == GN - 1:
                    f0_used[("wd", 1)] = ev_
            evt = b.op("tensor", mm(bank(bk_), zs4[oc % 2][0:86, :], f2_cs[0:86, :],
                                    False, True), inc="eT")
            zs_used[oc % 2] = evt
            cs_free = evt
            f2h_used = evt
            sp = oc % 2
            b.wait_ev("scalar", evt)
            b.wait_ev("scalar", ev_g_store.get(sp))
            eva = b.op("scalar", lambda e, o=f2_g[sp], i=bank(bk_):
                       e.activation(o, i, AF.Silu), inc="eA")
            b.bank_free[bk_] = eva
            b.wait_ev("sync", eva)
            ev_g_store[sp] = b.dma("sync", g_s[oc * 128:(oc + 1) * 128,
                                              mb * FB:(mb + 1) * FB], f2_g[sp],
                                   ch="ffgs%d" % sp)
    ev_g_all = [b.qnow("ffgs0"), b.qnow("ffgs1")]

    cb_used = {}
    for _e in ev_ctx_all + ev_g_all:
        b.wait_ev("sync", _e)
    for i in range(2):
        b.wait_ev("sync", denfree.get(i))
    for sci in range(16):
        bi = sci % 2
        l0 = sci * 128
        b.wait_ev("sync", cb_used.get(bi))
        b.dma("sync", cb_x[bi], xd[l0:l0 + 128, :], ch="cbl%d" % bi)
        b.dma("sync", cb_c[bi], ctx_tm[l0:l0 + 128, :], ch="cbl%d" % bi)
        b.dma("sync", cb_dn[bi], den_d[l0:l0 + 128, :], ch="cbl%d" % bi)
        ev_ld = None
        for j in range(8):
            ev_ld = b.dma("sync", cb_g[bi].rearrange("p (c n) -> p c n", n=512)[:, j, :],
                          g_s[j * 512:(j + 1) * 512, l0:l0 + 128], transpose=True,
                          ch="cbl%d" % bi)
        b.wait_ev("vector", ev_ld)
        b.wait_ev("vector", cb_used.get(("o", bi)))
        b.op("vector", lambda e, o=cb_r[bi], i=cb_dn[bi]: e.reciprocal(o, i))
        b.opb("vector", lambda e, o=cb_t, c=cb_c[bi], g=cb_g[bi]:
              e.tensor_tensor(o, c, g, ALU.mult), inc="eV")
        b.opb("vector", lambda e, o=cb_t2.rearrange("p (h d) -> p h d", d=128),
              t=cb_t.rearrange("p (h d) -> p h d", d=128),
              r=cb_r[bi].rearrange("p (h o) -> p h o", o=1).broadcast_to([128, H, 128]):
              e.tensor_tensor(o, t, r, ALU.mult), inc="eV")
        evv = b.op("vector", lambda e, o=cb_o[bi], t=cb_t2, x=cb_x[bi]:
                   e.tensor_tensor(o, t, x, ALU.add), inc="eV")
        cb_used[bi] = evv
        b.wait_ev("sync", evv)
        cb_used[("o", bi)] = b.dma("sync", outd[l0:l0 + 128, :], cb_o[bi],
                                   ch="cbo%d" % bi)

    for name, (dd, src) in dump.items():
        b.dma("sync", dd, src, ch="dump")

    GP_CH = ("bt2", "castq", "castk", "castv", "d0s", "dqw0", "dqw1", "dqr",
             "dqog", "dqou", "dqod")
    for name in list(b.cnt):
        if name.startswith("q_"):
            eng = "gpsimd" if name[2:] in GP_CH else "sync"
            b.wait(eng, name, b.cnt[name])

    if not b.check_deadlock():
        raise RuntimeError("offline deadlock check failed")

    with nc.Block() as block:
        @block.sync
        def _(e):
            for f in b.ops["sync"]:
                f(e)

        @block.gpsimd
        def _(e):
            for f in b.ops["gpsimd"]:
                f(e)

        @block.tensor
        def _(e):
            for f in b.ops["tensor"]:
                f(e)

        @block.vector
        def _(e):
            for f in b.ops["vector"]:
                f(e)

        @block.scalar
        def _(e):
            for f in b.ops["scalar"]:
                f(e)

    return nc


def kernel(**inputs):
    f32 = np.float32
    x = np.ascontiguousarray(np.asarray(inputs["L0_attn_hidden_states"], dtype=f32))
    host = {}
    for w, kw in (("q", "L0_attn_wq"), ("k", "L0_attn_wk"), ("v", "L0_attn_wv")):
        host[f"w{w}T"] = np.ascontiguousarray(np.asarray(inputs[kw], dtype=f32).T)
    for w, kb in (("q", "L0_attn_bq"), ("k", "L0_attn_bk")):
        host[f"b{w}t"] = np.ascontiguousarray(
            np.asarray(inputs[kb], dtype=f32).reshape(32, 128).T)
    host["bv"] = np.ascontiguousarray(
        np.asarray(inputs["L0_attn_bv"], dtype=f32).reshape(1, D))
    for pre, nm in (("g", "gate"), ("u", "up"), ("d", "down")):
        host[pre + "qw"] = np.ascontiguousarray(
            np.asarray(inputs[f"L0_ffn_{nm}_qweight"], dtype=np.int32))
        sc = np.ascontiguousarray(np.asarray(inputs[f"L0_ffn_{nm}_scales"], dtype=f32))
        host[pre + "sc"] = sc
        host[pre + "qz"] = np.ascontiguousarray(
            np.asarray(inputs[f"L0_ffn_{nm}_qzeros"], dtype=np.int32))

    nc = build_program()
    in_maps = []
    for c in range(8):
        m = dict(host)
        h = c % 2
        m["x"] = np.ascontiguousarray(x[c // 2][h * SOWN:(h + 1) * SOWN])
        in_maps.append(m)
    res = run_bass_kernel_spmd(nc, in_maps, core_ids=list(range(8)))
    out = np.empty((B, S, D), dtype=f32)
    for c in range(8):
        h = c % 2
        out[c // 2, h * SOWN:(h + 1) * SOWN, :] = res.results[c]["out"]
    return out


if __name__ == "__main__":
    pass


# version 13
# speedup vs baseline: 1.4845x; 1.0397x over previous
import numpy as np
import ml_dtypes
import concourse.bass as bass
import concourse.mybir as mybir
from concourse.bass_utils import run_bass_kernel_spmd

F32 = mybir.dt.float32
BF16 = mybir.dt.bfloat16
I32 = mybir.dt.int32
U8 = mybir.dt.uint8
AF = mybir.ActivationFunctionType
ALU = mybir.AluOpType
AX = mybir.AxisListType
BF = ml_dtypes.bfloat16

D = 4096
S = 4096
B = 4
H = 32
I = 11008
SOWN = 2048
KC = 32
GN = 86
OC = 32
MT = 256
NMT = SOWN // MT
FB = 512
NFB = SOWN // FB
RD = float(1.0 / np.sqrt(128.0))
LNEPS = 1e-5
AWQ_L = [0, 2, 4, 6, 1, 3, 5, 7]

ENGINES = ("sync", "gpsimd", "tensor", "vector", "scalar")

DUMP = ()


def ceil_div(a, b):
    return (a + b - 1) // b


class Build:
    def __init__(self, nc):
        self.nc = nc
        self.ops = {e: [] for e in ENGINES}
        self.sem = {}
        self.cnt = {}
        for name in ("eT", "eA", "eV", "eP"):
            self.sem[name] = nc.alloc_semaphore(name)
            self.cnt[name] = 0
        self.bank_free = {}
        self.trace = {}
        self.chan_eng = {}

    def emit(self, eng, fn, meta=None):
        self.ops[eng].append(fn)
        self.trace.setdefault(eng, []).append(meta or ("op", None, 0))

    def wait(self, eng, semname, val):
        if val is None or val <= 0:
            return
        s = self.sem[semname]
        self.emit(eng, lambda e, s=s, v=val: e.wait_ge(s, v),
                  meta=("wait", semname, val))

    def wait_ev(self, eng, ev):
        if ev:
            self.wait(eng, ev[0], ev[1])

    def dma(self, eng, out, in_, ch="misc", **kw):
        semname = "q_" + ch
        if semname not in self.sem:
            self.sem[semname] = self.nc.alloc_semaphore(semname)
            self.cnt[semname] = 0
        self.chan_eng[semname] = eng
        s = self.sem[semname]
        self.cnt[semname] += 16
        self.emit(eng, lambda e, o=out, i=in_, s=s, kw=kw:
                  e.dma_start(out=o, in_=i, **kw).then_inc(s, 16),
                  meta=("inc", semname, 16))
        return (semname, self.cnt[semname])

    def qnow(self, ch):
        return ("q_" + ch, self.cnt.get("q_" + ch, 0))

    def op(self, eng, fn, inc=None):
        if inc:
            s = self.sem[inc]
            self.cnt[inc] += 1
            v = self.cnt[inc]
            self.emit(eng, lambda e, fn=fn, s=s: fn(e).then_inc(s, 1),
                      meta=("inc", inc, 1))
            return (inc, v)
        self.emit(eng, lambda e, fn=fn: fn(e))
        return None

    def opb(self, eng, fn, inc):
        ev = self.op(eng, fn, inc=inc)
        self.wait_ev(eng, ev)
        return ev

    def now(self, semname):
        return (semname, self.cnt[semname])

    def pe_bank_wait(self, banks):
        for bk in banks:
            ev = self.bank_free.get(bk)
            self.wait_ev("tensor", ev)
            self.bank_free[bk] = None

    def check_deadlock(self):
        import collections
        ptr = {e: 0 for e in ENGINES}
        cnt = collections.defaultdict(int)
        n = {e: len(self.trace.get(e, [])) for e in ENGINES}
        progress = True
        while progress:
            progress = False
            for e in ENGINES:
                while ptr[e] < n[e]:
                    kind, sem, val = self.trace[e][ptr[e]]
                    if kind == "wait" and cnt[sem] < val:
                        break
                    if kind == "inc":
                        cnt[sem] += val
                    ptr[e] += 1
                    progress = True
        stuck = {e: (ptr[e], n[e]) for e in ENGINES if ptr[e] < n[e]}
        if stuck:
            print("DEADLOCK! stuck engines:", stuck)
            for e, (p, _) in stuck.items():
                kind, sem, val = self.trace[e][p]
                print(f"  {e}@{p}: waiting {sem} >= {val}, have {cnt[sem]}")
            return False
        return True


def build_program():
    nc = bass.Bass("TRN2", target_bir_lowering=False)

    xd = nc.dram_tensor("x", [SOWN, D], F32, kind="ExternalInput").ap()
    wT = {w: nc.dram_tensor(f"w{w}T", [D, D], F32, kind="ExternalInput").ap()
          for w in ("q", "k", "v")}
    bT = {w: nc.dram_tensor(f"b{w}t", [128, 32], F32, kind="ExternalInput").ap()
          for w in ("q", "k")}
    bvd = nc.dram_tensor("bv", [1, D], F32, kind="ExternalInput").ap()
    qw_d = {"g": nc.dram_tensor("gqw", [D, 1376], I32, kind="ExternalInput").ap(),
            "u": nc.dram_tensor("uqw", [D, 1376], I32, kind="ExternalInput").ap(),
            "d": nc.dram_tensor("dqw", [I, 512], I32, kind="ExternalInput").ap()}
    sc_d = {"g": nc.dram_tensor("gsc", [32, I], F32, kind="ExternalInput").ap(),
            "u": nc.dram_tensor("usc", [32, I], F32, kind="ExternalInput").ap(),
            "d": nc.dram_tensor("dsc", [86, D], F32, kind="ExternalInput").ap()}
    qz_d = {"g": nc.dram_tensor("gqz", [32, 1376], I32, kind="ExternalInput").ap(),
            "u": nc.dram_tensor("uqz", [32, 1376], I32, kind="ExternalInput").ap(),
            "d": nc.dram_tensor("dqz", [86, 512], I32, kind="ExternalInput").ap()}
    outd = nc.dram_tensor("out", [SOWN, D], F32, kind="ExternalOutput").ap()

    xn_s = nc.dram_tensor("xn_s", [SOWN, D], BF16, kind="Internal").ap()
    wq7 = nc.dram_tensor("wq7", [32, 128, 32, 128], BF16, kind="Internal").ap()
    wk7 = nc.dram_tensor("wk7", [32, 128, 32, 128], BF16, kind="Internal").ap()
    wv7 = nc.dram_tensor("wv7", [16, 128, 32, 256], BF16, kind="Internal").ap()
    w7 = {"q": wq7, "k": wk7, "v": wv7}
    v_h8 = nc.dram_tensor("v_h8", [32, SOWN, 128], BF16, kind="Internal").ap()
    ctx_tm = nc.dram_tensor("ctx_tm", [SOWN, D], BF16, kind="Internal").ap()
    den_d = nc.dram_tensor("den_d", [SOWN, H], F32, kind="Internal").ap()
    g_s = nc.dram_tensor("g_s", [D, SOWN], BF16, kind="Internal").ap()
    Wg_t = nc.dram_tensor("Wg_t", [GN, 128, KC, 128], BF16, kind="Internal").ap()
    Wu_t = nc.dram_tensor("Wu_t", [GN, 128, KC, 128], BF16, kind="Internal").ap()
    Wd_t = nc.dram_tensor("Wd_t", [OC, 128, GN, 128], BF16, kind="Internal").ap()
    zs_s = {"g": nc.dram_tensor("zs_g", [32, I], BF16, kind="Internal").ap(),
            "u": nc.dram_tensor("zs_u", [32, I], BF16, kind="Internal").ap(),
            "d": nc.dram_tensor("zs_d", [86, D], BF16, kind="Internal").ap()}
    scb_d = {"g": nc.dram_tensor("scb_g", [32, I], BF16, kind="Internal").ap(),
             "u": nc.dram_tensor("scb_u", [32, I], BF16, kind="Internal").ap(),
             "d": nc.dram_tensor("scb_d", [86, D], BF16, kind="Internal").ap()}
    h_d = nc.dram_tensor("h_d", [GN, 128, SOWN], BF16, kind="Internal").ap()

    dump = {}
    for name, src in (("xn", xn_s), ("v", v_h8), ("ctx", ctx_tm), ("g", g_s),
                      ("wg", Wg_t), ("wu", Wu_t), ("wd", Wd_t)):
        if name in DUMP:
            dump[name] = (nc.dram_tensor("dump_" + name, list(src.shape), BF16,
                                         kind="ExternalOutput").ap(), src)

    ones_bf = nc.alloc_sbuf_tensor("ones_bf", [128, 128], BF16)
    bt_sb = {w: nc.alloc_sbuf_tensor(f"bt_{w}", [128, 32], F32) for w in ("q", "k")}
    bv_sb = nc.alloc_sbuf_tensor("bv_sb", [1, D], BF16)
    epsc = nc.alloc_sbuf_tensor("epsc", [128, 1], F32)
    ARENA_KB = 198
    arena = nc.alloc_sbuf_tensor("arena", [128, ARENA_KB * 1024], U8)

    def view(off_b, n, dtype):
        sz = {F32: 4, BF16: 2, I32: 4}[dtype]
        assert off_b + n * sz <= ARENA_KB * 1024, (off_b, n, dtype)
        return arena[:, off_b:off_b + n * sz].bitcast(dtype)

    K1 = 1024
    ln_x = [view(0, D, F32), view(16 * K1, D, F32)]
    ln_sq = view(32 * K1, D, BF16)
    ln_xn = [view(40 * K1, D, BF16), view(48 * K1, D, BF16)]
    ln_st = view(56 * K1, 16, F32)
    d0_qz = view(64 * K1, 1376, I32)
    d0_z = view(70 * K1, 1376, I32)
    d0_sc = view(76 * K1, 1376, F32)
    d0_zs = view(82 * K1, 1376, BF16)
    d0_scb = view(86 * K1, 1376, BF16)

    pj_xn = [view(0, KC * MT, BF16), view(16 * K1, KC * MT, BF16)]
    pj_w = [view(32 * K1, KC * 128, BF16), view(40 * K1, KC * 128, BF16)]
    pj_wv = [view(48 * K1, KC * 256, BF16), view(64 * K1, KC * 256, BF16)]
    q_sl = view(80 * K1, H * MT, BF16)
    k_sl = view(96 * K1, H * MT, BF16)
    mx_v = [view(112 * K1, 32 * 128, BF16), view(120 * K1, 32 * 128, BF16),
            view(188032, 32 * 128, BF16)]
    mx_p = [view(128 * K1, 512, BF16), view(128 * K1 + 1024, 512, BF16)]
    mx_cx = [view(130 * K1, 16 * 128, BF16), view(134 * K1, 16 * 128, BF16)]
    pj_ev = [view(138 * K1 + o * 1024, 512, BF16) for o in range(4)]
    den_sb = [view(142 * K1, 512, F32), view(144 * K1, 512, F32)]
    _o = 146 * K1
    dqoff = {}
    for nm, nb in (("qw0", 688 * 4), ("qw1", 688 * 4), ("sr0", 2752 * 2),
                   ("sr1", 2752 * 2), ("nib", 2752 * 4), ("wout", 5504 * 2)):
        dqoff[nm] = _o
        _o += nb
    assert _o <= ARENA_KB * K1, _o

    def dqv(name, n, dtype):
        sz = {F32: 4, BF16: 2, I32: 4}[dtype]
        return arena[:, dqoff[name]:dqoff[name] + n * sz].bitcast(dtype)

    dq_qw = [dqv("qw0", 688, I32), dqv("qw1", 688, I32)]
    dq_sr = [dqv("sr0", 2752, BF16), dqv("sr1", 2752, BF16)]
    dq_nib = dqv("nib", 2752, I32)
    dq_wout = dqv("wout", 5504, BF16)

    fx_full = view(0, KC * SOWN, BF16)
    f0_wg = [view(131072, KC * 128, BF16), view(139264, KC * 128, BF16)]
    f0_wu = [view(147456, KC * 128, BF16), view(155648, KC * 128, BF16)]
    h_t = [view(163840, SOWN, BF16), view(167936, SOWN, BF16)]
    f0_sil = [view(172032, 512, BF16), view(173056, 512, BF16)]
    cs4 = view(174080, 4 * 512, BF16)
    oh32 = view(178176, KC * 32, BF16)
    zs4 = [view(180224 + i * 256, 128, BF16) for i in range(4)]
    f2_h = view(0, GN * 512, BF16)
    f2_wd = [view(88064 + i * 11008, 43 * 128, BF16) for i in range(4)]
    oh86 = view(132096, 86 * 86, BF16)
    f2_g = [view(146888, 512, BF16), view(147912, 512, BF16)]
    f2_cs = view(148936, 512, BF16)
    cb_x = [view(0, D, F32), view(16 * K1, D, F32)]
    cb_c = [view(32 * K1, D, BF16), view(40 * K1, D, BF16)]
    cb_g = [view(48 * K1, D, BF16), view(56 * K1, D, BF16)]
    cb_t = view(64 * K1, D, F32)
    cb_o = [view(80 * K1, D, F32), view(96 * K1, D, F32)]
    cb_dn = [view(112 * K1, 32, F32), view(112 * K1 + 128, 32, F32)]
    cb_r = [view(112 * K1 + 256, 32, F32), view(112 * K1 + 384, 32, F32)]
    cb_t2 = view(116 * K1, D, F32)

    P = nc.alloc_psum_tensor("P", [128, 4096], F32)

    def bank(i, n=1):
        return P[:, i * 512:(i + n) * 512]

    b = Build(nc)

    def mm(out, lhsT, rhs, start=True, stop=True):
        return lambda e: e.matmul(out, lhsT, rhs, start=start, stop=stop)

    def lp(fn, *a, **kw):
        with nc.allow_low_precision(reason="bf16 operand rounding (within bf16-matmul precision)"):
            return fn(*a, **kw)

    b.op("vector", lambda e: e.memset(ones_bf[:], 1.0))
    ev_const = b.op("vector", lambda e: e.memset(epsc[:], LNEPS), inc="eV")
    b.wait_ev("scalar", ev_const)
    b.wait_ev("gpsimd", ev_const)
    b.wait_ev("tensor", ev_const)
    for w in ("q", "k"):
        ev_bt = b.dma("sync", bt_sb[w][:], bT[w], ch="bt")
    ev_bv = b.dma("gpsimd", bv_sb[:], bvd, ch="bt2")

    ev_wcast = {}
    for w in ("q", "k", "v"):
        nn = 256 if w == "v" else 128
        for kc in range(KC):
            ev_wcast[w] = b.dma(
                "gpsimd", w7[w][:, :, kc, :].rearrange("a p n -> p a n"),
                wT[w][kc * 128:(kc + 1) * 128, :].rearrange("p (a n) -> p a n", n=nn),
                ch="cast" + w)

    WCFG = {
        "g": dict(K=KC, N=I, pk=1376, G=32, qwd=qw_d["g"], scd=sc_d["g"], qzd=qz_d["g"]),
        "u": dict(K=KC, N=I, pk=1376, G=32, qwd=qw_d["u"], scd=sc_d["u"], qzd=qz_d["u"]),
        "d": dict(K=GN, N=D, pk=512, G=86, qwd=qw_d["d"], scd=sc_d["d"], qzd=qz_d["d"]),
    }
    ev_d0_v = None
    d0_scb_done = {}
    for w in ("g", "u", "d"):
        cfg = WCFG[w]
        G, N = cfg["G"], cfg["N"]
        nblk = 8
        blkn = N // nblk
        blkp = blkn // 8
        if ev_d0_v:
            b.wait_ev("sync", ev_d0_v)
        b.dma("sync", d0_qz[0:G, 0:blkp * nblk], cfg["qzd"], ch="d0l")
        for blk in range(nblk):
            if ev_d0_v:
                b.wait_ev("sync", ev_d0_v)
            ev_sc = b.dma("sync", d0_sc[0:G, 0:blkn],
                          cfg["scd"][:, blk * blkn:(blk + 1) * blkn], ch="d0l")
            b.wait_ev("vector", ev_sc)
            ev_scb = b.op("vector", lambda e, o=d0_scb[0:G, 0:blkn],
                          i=d0_sc[0:G, 0:blkn]: e.tensor_copy(o, i), inc="eV")
            for j in range(8):
                qzsl = d0_qz[0:G, blk * blkp:(blk + 1) * blkp]
                fn = (lambda e, o=d0_z[0:G, AWQ_L[j]:blkn:8], i=qzsl, sh=4 * j:
                      e.tensor_scalar(o, i, sh, 15, ALU.logical_shift_right,
                                      ALU.bitwise_and))
                if j == 7:
                    b.opb("vector", fn, inc="eV")
                else:
                    b.op("vector", fn)
            ev_d0_v = b.op("vector", lambda e, o=d0_zs[0:G, 0:blkn], z=d0_z[0:G, 0:blkn],
                           s=d0_sc[0:G, 0:blkn]: e.tensor_tensor(o, z, s, ALU.mult),
                           inc="eV")
            b.wait_ev("gpsimd", ev_scb)
            b.dma("gpsimd", scb_d[w][:, blk * blkn:(blk + 1) * blkn],
                  d0_scb[0:G, 0:blkn], ch="d0s")
            b.wait_ev("gpsimd", ev_d0_v)
            ev_zs = b.dma("gpsimd", zs_s[w][:, blk * blkn:(blk + 1) * blkn],
                          d0_zs[0:G, 0:blkn], ch="d0s")
            b.wait_ev("vector", ev_zs)
        d0_scb_done[w] = b.qnow("d0s")
    ev_d0_done = b.qnow("d0s")
    ev_d0_vec = b.now("eV")

    c_inv = 1.0 / D
    ev_ln_act = [None, None]
    for i in range(16):
        bi = i % 2
        if ev_ln_act[bi]:
            b.wait_ev("sync", ev_ln_act[bi])
        ev_x = b.dma("sync", ln_x[bi], xd[i * 128:(i + 1) * 128, :], ch="lnx%d" % bi)
        st = ln_st
        b.wait_ev("vector", ev_x)
        b.opb("vector", lambda e, o=st[:, 0:1], i2=ln_x[bi]:
              e.reduce_sum(o, i2, axis=AX.X), inc="eV")
        b.wait_ev("scalar", ev_x)
        eva = b.op("scalar", lambda e, o=ln_sq, i2=ln_x[bi], a=st[:, 1:2]:
                   e.activation(o, i2, AF.Square, accum_out=a), inc="eA")
        b.wait_ev("vector", eva)
        b.opb("vector", lambda e, o=st[:, 2:3], i2=st[:, 0:1]:
              e.tensor_scalar_mul(o, i2, c_inv), inc="eV")
        b.opb("vector", lambda e, o=st[:, 3:4], i2=st[:, 2:3]:
              e.tensor_tensor(o, i2, i2, ALU.mult), inc="eV")
        evv = b.opb("vector", lambda e, o=st[:, 4:5], i2=st[:, 1:2], m2=st[:, 3:4]:
                    e.scalar_tensor_tensor(o, i2, c_inv, m2, ALU.mult, ALU.subtract),
                    inc="eV")
        b.wait_ev("scalar", evv)
        eva = b.op("scalar", lambda e, o=st[:, 5:6], i2=st[:, 4:5]:
                   e.activation(o, i2, AF.Sqrt, bias=epsc[:]), inc="eA")
        b.wait_ev("vector", eva)
        b.opb("vector", lambda e, o=st[:, 6:7], i2=st[:, 5:6]:
              e.reciprocal(o, i2), inc="eV")
        evv = b.opb("vector", lambda e, o=st[:, 7:8], i2=st[:, 2:3], r=st[:, 6:7]:
                    e.scalar_tensor_tensor(o, i2, -1.0, r, ALU.mult, ALU.mult), inc="eV")
        b.wait_ev("scalar", evv)
        ev_ln_act[bi] = b.op("scalar", lambda e, o=ln_xn[bi], i2=ln_x[bi],
                             r=st[:, 6:7], m=st[:, 7:8]:
                             e.activation(o, i2, AF.Identity, bias=m, scale=r), inc="eA")
        b.wait_ev("sync", ev_ln_act[bi])
        b.dma("sync", xn_s[i * 128:(i + 1) * 128, :], ln_xn[bi], ch="lnst")
    ev_ln_all = b.qnow("lnst")

    dq_state = {}
    dq_halves = []
    for w in ("g", "u", "d"):
        for ch in range(WCFG[w]["K"]):
            for h2 in (0, 1):
                dq_halves.append((w, ch, h2))
    NQ = 2 * len(dq_halves)

    def dq_issue_qw(t):
        w, ch, h2 = dq_halves[t]
        hq = WCFG[w]["pk"] // 2
        qb = t % 2
        b.wait_ev("gpsimd", dq_state.get(("qwfree", qb)))
        dq_state[("qwld", t)] = b.dma(
            "gpsimd", dq_qw[qb][:, 0:hq],
            WCFG[w]["qwd"][ch * 128:(ch + 1) * 128, h2 * hq:(h2 + 1) * hq],
            ch="dqw%d" % qb)

    def dq_issue_sr(u):
        t, qih = divmod(u, 2)
        w, ch, h2 = dq_halves[t]
        qn = WCFG[w]["N"] // 4
        hf = h2 * 2 + qih
        sb_ = u % 2
        b.wait_ev("gpsimd", dq_state.get(("srfree", sb_)))
        b.wait_ev("gpsimd", d0_scb_done[w])
        dq_state[("srld", u)] = b.dma(
            "gpsimd", dq_sr[sb_][:, 0:qn],
            scb_d[w][ch:ch + 1, hf * qn:(hf + 1) * qn].partition_broadcast(128),
            ch="dqs%d" % sb_)

    def dequant_steps():
        for t0 in (0, 1):
            dq_issue_qw(t0)
        for u0 in (0, 1):
            dq_issue_sr(u0)
        for t, (w, ch, h2) in enumerate(dq_halves):
            cfg = WCFG[w]
            N, pk = cfg["N"], cfg["pk"]
            qn = N // 4
            qp = pk // 4
            nhalf = (N // 2) // 128
            qb = t % 2
            tgt = {"g": Wg_t, "u": Wu_t, "d": Wd_t}[w]
            evm = None
            for qih in range(2):
                u = 2 * t + qih
                sb_ = u % 2
                if qih == 0:
                    b.wait_ev("vector", dq_state[("qwld", t)])
                    b.wait_ev("vector", dq_state.get("woutfree"))
                for j in range(8):
                    src = dq_qw[qb][:, qih * qp:(qih + 1) * qp]
                    fn = (lambda e, o=dq_nib[:, AWQ_L[j]:qn:8], i=src, sh=4 * j:
                          e.tensor_scalar(o, i, sh, 15, ALU.logical_shift_right,
                                          ALU.bitwise_and))
                    if j == 7:
                        evup = b.opb("vector", fn, inc="eV")
                    else:
                        b.op("vector", fn)
                if qih == 1:
                    dq_state[("qwfree", qb)] = evup
                    if t + 2 < len(dq_halves):
                        dq_issue_qw(t + 2)
                b.wait_ev("vector", dq_state[("srld", u)])
                evm = b.op("vector", lambda e, o=dq_wout[:, qih * qn:(qih + 1) * qn],
                           i=dq_nib[:, 0:qn], s=dq_sr[sb_][:, 0:qn]:
                           e.tensor_tensor(o, i, s, ALU.mult), inc="eV")
                dq_state[("srfree", sb_)] = evm
                if u + 2 < NQ:
                    dq_issue_sr(u + 2)
                yield
            b.wait_ev("gpsimd", evm)
            dst = tgt[h2 * nhalf:(h2 + 1) * nhalf, :, ch, :]
            dq_state["woutfree"] = b.dma("gpsimd", dst.rearrange("a p n -> p a n"),
                                         dq_wout[:, 0:2 * qn]
                                         .rearrange("p (a n) -> p a n", n=128),
                                         ch="dqo" + w)
        dq_state["done"] = b.now("eV")
        while True:
            yield

    dq_gen = dequant_steps()

    ev_wslab = [None, None]
    ev_wvslab = [None, None]
    wslab_used = [None, None]
    wvslab_used = [None, None]
    xn_used = {}
    ev_xn_b = {}
    ev_ev_store = {}
    qk_ready = None
    mixv_used = [None, None, None]
    last_exp = {}
    pfree_t = {}
    denfree = {}
    cxfree = {}

    for mt in range(NMT):
        m0 = mt * MT
        bx = mt % 2
        if mt == 0:
            b.wait_ev("sync", ev_ln_all)
            b.wait_ev("sync", ev_d0_done)
            b.wait_ev("sync", ev_d0_vec)
            b.wait_ev("scalar", ev_d0_done)
            b.wait_ev("scalar", ev_d0_vec)
        b.wait_ev("sync", xn_used.get(bx))
        for kc in range(KC):
            ev_xn = b.dma("sync", pj_xn[bx].rearrange("p (c n) -> p c n", n=MT)[:, kc, :],
                          xn_s[m0:m0 + MT, kc * 128:(kc + 1) * 128], transpose=True,
                          ch="pjx%d" % bx)
        ev_xn_b[bx] = ev_xn
        xv = pj_xn[bx].rearrange("p (c n) -> p c n", n=MT)
        for w, dslab in (("q", q_sl), ("k", k_sl)):
            for ncn in range(H):
                wb2 = ncn % 2
                b.wait_ev("sync", ev_wcast[w])
                b.wait_ev("sync", wslab_used[wb2])
                ev_wslab[wb2] = [
                    b.dma("sync", pj_w[wb2][:, hf * 2048:(hf + 1) * 2048],
                          w7[w][ncn][:, hf * 16:(hf + 1) * 16, :]
                          .rearrange("p c n -> p (c n)"), ch="pjw%d%d" % (wb2, hf))
                    for hf in range(2)]
                pbank = ncn % 2
                b.wait_ev("tensor", ev_xn)
                for _e in ev_wslab[wb2]:
                    b.wait_ev("tensor", _e)
                wv_ = pj_w[wb2].rearrange("p (c n) -> p c n", n=128)
                b.pe_bank_wait((pbank,))
                evt = None
                for kc in range(KC):
                    evt = b.op("tensor", mm(bank(pbank)[:, 0:MT], wv_[:, kc, :],
                                            xv[:, kc, :], kc == 0, kc == KC - 1),
                               inc="eT" if kc == KC - 1 else None) or evt
                wslab_used[wb2] = evt
                xn_used[bx] = evt
                b.wait_ev("scalar", evt)
                if mt == 0 and w == "q" and ncn == 0:
                    b.wait_ev("scalar", ev_bt)
                eva = b.op("scalar", lambda e,
                           o=dslab.rearrange("p (c n) -> p c n", n=MT)[:, ncn, :],
                           i=bank(pbank)[:, 0:MT], bias=bt_sb[w][:, ncn:ncn + 1]:
                           e.activation(o, i, AF.Identity, bias=bias), inc="eA")
                b.bank_free[pbank] = eva
                qk_ready = eva
                next(dq_gen)
        for ns in range(16):
            wb2 = ns % 2
            b.wait_ev("sync", ev_wcast["v"])
            b.wait_ev("sync", wvslab_used[wb2])
            ev_wvslab[wb2] = [
                b.dma("sync", pj_wv[wb2][:, hf * 4096:(hf + 1) * 4096],
                      w7["v"][ns][:, hf * 16:(hf + 1) * 16, :]
                      .rearrange("p c n -> p (c n)"), ch="pjv%d%d" % (wb2, hf))
                for hf in range(2)]
            wvv = pj_wv[wb2].rearrange("p (c n) -> p c n", n=256)
            for mc in range(MT // 128):
                pbank = (ns * 2 + mc) % 2
                b.wait_ev("tensor", ev_xn)
                for _e in ev_wvslab[wb2]:
                    b.wait_ev("tensor", _e)
                if mt == 0 and ns == 0 and mc == 0:
                    b.wait_ev("tensor", ev_bv)
                b.pe_bank_wait((pbank,))
                for kc in range(KC):
                    b.op("tensor", mm(bank(pbank)[:, 0:256],
                                      xv[:, kc, mc * 128:(mc + 1) * 128],
                                      wvv[:, kc, :], kc == 0, False))
                evt = b.op("tensor", mm(bank(pbank)[:, 0:256], ones_bf[0:1, :],
                                        bv_sb[0:1, ns * 256:(ns + 1) * 256], False, True),
                           inc="eT")
                wvslab_used[wb2] = evt
                xn_used[bx] = evt
                sb_i = (ns * 2 + mc) % 4
                b.wait_ev("scalar", evt)
                b.wait_ev("scalar", ev_ev_store.get(sb_i))
                eva = b.op("scalar", lambda e, o=pj_ev[sb_i][:, 0:256],
                           i=bank(pbank)[:, 0:256]:
                           e.activation(o, i, AF.Identity), inc="eA")
                b.bank_free[pbank] = eva
                b.wait_ev("gpsimd", eva)
                ev_ev_store[sb_i] = b.dma(
                    "gpsimd", v_h8[ns * 2:(ns + 1) * 2, m0 + mc * 128:m0 + (mc + 1) * 128, :]
                    .rearrange("t m d -> m t d"),
                    pj_ev[sb_i][:, 0:256].rearrange("p (t d) -> p t d", d=128),
                    ch="pjev%d" % sb_i)
            next(dq_gen)
        ev_vtm = [b.qnow("pjev%d" % i) for i in range(4)]
        qv3 = q_sl.rearrange("p (c n) -> p c n", n=MT)
        kv3 = k_sl.rearrange("p (c n) -> p c n", n=MT)
        ev_mv_b = {}
        for g16 in range(MT // 16):
            gm0 = g16 * 16
            par = g16 % 2
            if gm0 % 32 == 0:
                vb = (gm0 // 32) % 3
                for _e in ev_vtm:
                    b.wait_ev("sync", _e)
                b.wait_ev("sync", mixv_used[vb])
                ev_mv_b[vb] = b.dma(
                    "sync", mx_v[vb].rearrange("p (c n) -> p c n", n=128)[0:32],
                    v_h8[:, m0 + gm0:m0 + gm0 + 32, :], ch="mixv%d" % vb)
            vb = (gm0 // 32) % 3
            vbase = (gm0 // 32) * 32
            ev_mv = ev_mv_b[vb]
            scb = 4 + par
            b.wait_ev("tensor", qk_ready)
            b.wait_ev("tensor", last_exp.get(scb))
            b.pe_bank_wait((scb,))
            evt = None
            for j in range(16):
                mloc = gm0 + j
                evt = b.op("tensor", mm(bank(scb)[0:32, j * 32:(j + 1) * 32],
                                        kv3[:, :, mloc], qv3[:, :, mloc]),
                           inc="eT" if j == 15 else None) or evt
            b.wait_ev("scalar", evt)
            b.wait_ev("scalar", pfree_t.get(par))
            eva = b.op("scalar", lambda e, o=mx_p[par][0:32, :], i=bank(scb)[0:32, :]:
                       e.activation(o, i, AF.Exp, scale=RD), inc="eA")
            last_exp[scb] = eva
            cb = 6 + par
            b.wait_ev("tensor", eva)
            b.pe_bank_wait((cb,))
            evt_d = b.op("tensor", mm(bank(cb)[0:1, :], ones_bf[0:32, 0:1],
                                      mx_p[par][0:32, :]), inc="eT")
            b.wait_ev("scalar", evt_d)
            b.wait_ev("scalar", denfree.get(par))
            ev_den = b.op("scalar", lambda e, o=den_sb[par][0:1, :], i=bank(cb)[0:1, :]:
                          e.activation(o, i, AF.Identity), inc="eA")
            b.bank_free[cb] = ev_den
            b.wait_ev("gpsimd", ev_den)
            gidx = (m0 + gm0) // 16
            denfree[par] = b.dma(
                "gpsimd", den_d.rearrange("(a m) h -> a (m h)", m=16)[gidx:gidx + 1, :],
                den_sb[par][0:1, :], ch="den%d" % par)
            vv3 = mx_v[vb].rearrange("p (c n) -> p c n", n=128)
            cxb = par
            last_pv_evac = None
            for rnd in range(4):
                b.wait_ev("tensor", ev_mv)
                if rnd > 0:
                    b.wait_ev("tensor", last_pv_evac)
                b.pe_bank_wait((cb,))
                evt = None
                for j in range(4):
                    tok = gm0 + rnd * 4 + j
                    evt = b.op("tensor", mm(bank(cb)[0:32, j * 128:(j + 1) * 128],
                                            mx_p[par][0:32,
                                                      (rnd * 4 + j) * 32:
                                                      (rnd * 4 + j + 1) * 32],
                                            vv3[0:32, tok - vbase, :]),
                               inc="eT" if j == 3 else None) or evt
                b.wait_ev("scalar", evt)
                if rnd == 0:
                    b.wait_ev("scalar", cxfree.get(cxb))
                last_pv_evac = b.op("scalar", lambda e,
                                    o=mx_cx[cxb][0:32, rnd * 512:(rnd + 1) * 512],
                                    i=bank(cb)[0:32, :]:
                                    e.activation(o, i, AF.Identity), inc="eA")
                b.bank_free[cb] = last_pv_evac
            pfree_t[par] = b.now("eT")
            b.wait_ev("gpsimd", last_pv_evac)
            cxfree[cxb] = b.dma(
                "gpsimd", ctx_tm[m0 + gm0:m0 + gm0 + 16, :]
                .rearrange("c (h d) -> h c d", d=128),
                mx_cx[cxb][0:32, :].rearrange("p (c n) -> p c n", n=128)[0:32],
                ch="ctm%d" % cxb)
            mixv_used[vb] = b.now("eT")
    while "done" not in dq_state:
        next(dq_gen)
    ev_dq_done = dq_state["done"]
    ev_ctx_all = [b.qnow("ctm0"), b.qnow("ctm1")]

    ev_mix_T = b.now("eT")
    ev_mix_A = b.now("eA")
    ev_mix_V = b.now("eV")
    for w in ("g", "u", "d"):
        b.wait_ev("sync", b.qnow("dqo" + w))
        b.wait_ev("vector", b.qnow("dqo" + w))
        b.wait_ev("scalar", b.qnow("dqo" + w))
    b.wait_ev("sync", ev_d0_done)
    b.wait_ev("sync", ev_mix_T)
    b.wait_ev("sync", ev_mix_A)
    b.wait_ev("sync", ev_mix_V)
    for i in range(2):
        b.wait_ev("sync", b.qnow("ctm%d" % i))
    b.wait_ev("scalar", ev_dq_done)
    b.wait_ev("tensor", ev_dq_done)
    ev_fx = None
    for kc in range(KC):
        for tb in range(4):
            ev_fx = b.dma("sync",
                          fx_full.rearrange("p (c n) -> p c n", n=SOWN)
                          [:, kc, tb * 512:(tb + 1) * 512],
                          xn_s[tb * 512:(tb + 1) * 512, kc * 128:(kc + 1) * 128],
                          transpose=True, ch="ffx")
    fxv = fx_full.rearrange("p (c n) -> p c n", n=SOWN)
    b.op("vector", lambda e: e.memset(oh32[:], 0.0))
    oh32v = oh32.rearrange("p (c n) -> p c n", n=32)
    ev_oh = None
    for kc in range(KC):
        ev_oh = b.op("vector", lambda e, o=oh32v[:, kc, kc:kc + 1]: e.memset(o, 1.0),
                     inc="eV" if kc == KC - 1 else None) or ev_oh
    b.wait_ev("tensor", ev_fx)
    b.wait_ev("tensor", ev_oh)
    ev_cs4 = None
    for tb in range(4):
        b.pe_bank_wait((6,))
        evt = None
        for kc in range(KC):
            evt = b.op("tensor", mm(bank(6)[0:32, :], oh32v[:, kc, :],
                                    fxv[:, kc, tb * 512:(tb + 1) * 512],
                                    kc == 0, kc == KC - 1),
                       inc="eT" if kc == KC - 1 else None) or evt
        b.wait_ev("scalar", evt)
        b.wait_ev("scalar", ev_cs4)
        ev_cs4 = b.op("scalar", lambda e, o=cs4[0:32, tb * 512:(tb + 1) * 512],
                      i=bank(6)[0:32, :]:
                      e.activation(o, i, AF.Identity, scale=-1.0), inc="eA")
        b.bank_free[6] = ev_cs4
    b.wait_ev("tensor", ev_cs4)
    f0_used = {}
    zs_used = {}
    sil_used = {}
    ev_ht_store = {}
    for nci in range(GN):
        wb2 = nci % 2
        b.wait_ev("sync", f0_used.get(wb2))
        evwg = b.dma("sync", f0_wg[wb2][:], Wg_t[nci].rearrange("p c n -> p (c n)"),
                     ch="f0w%d" % wb2)
        evwu = b.dma("sync", f0_wu[wb2][:], Wu_t[nci].rearrange("p c n -> p (c n)"),
                     ch="f0u%d" % wb2)
        b.wait_ev("sync", zs_used.get(wb2))
        b.dma("sync", zs4[wb2][0:32, :], zs_s["g"][:, nci * 128:(nci + 1) * 128],
              ch="f0z%d" % wb2)
        b.wait_ev("sync", zs_used.get(2 + wb2))
        evzu = b.dma("sync", zs4[2 + wb2][0:32, :],
                     zs_s["u"][:, nci * 128:(nci + 1) * 128], ch="f0z%d" % wb2)
        gv = f0_wg[wb2].rearrange("p (c n) -> p c n", n=128)
        uv = f0_wu[wb2].rearrange("p (c n) -> p c n", n=128)
        hb = nci % 2
        b.wait_ev("vector", ev_ht_store.get(hb))
        evt = None
        evv = None
        for tb in range(4):
            bkG, bkU = 2 * tb, 2 * tb + 1
            b.wait_ev("tensor", evwg)
            b.wait_ev("tensor", evwu)
            b.wait_ev("tensor", evzu)
            for bk_, wsl, zi in ((bkG, gv, wb2), (bkU, uv, 2 + wb2)):
                b.pe_bank_wait((bk_,))
                for kc in range(KC):
                    b.op("tensor", mm(bank(bk_), wsl[:, kc, :],
                                      fxv[:, kc, tb * 512:(tb + 1) * 512],
                                      kc == 0, False))
                evt = b.op("tensor", mm(bank(bk_), zs4[zi][0:32, :],
                                        cs4[0:32, tb * 512:(tb + 1) * 512],
                                        False, True), inc="eT")
                zs_used[zi] = evt
            sp = tb % 2
            b.wait_ev("scalar", evt)
            b.wait_ev("scalar", sil_used.get(sp))
            eva = b.op("scalar", lambda e, o=f0_sil[sp], i=bank(bkG):
                       e.activation(o, i, AF.Silu), inc="eA")
            b.bank_free[bkG] = eva
            b.wait_ev("vector", eva)
            evv = b.op("vector", lambda e, o=h_t[hb][:, tb * 512:(tb + 1) * 512],
                       i=f0_sil[sp], u=bank(bkU): e.tensor_tensor(o, i, u, ALU.mult),
                       inc="eV")
            b.bank_free[bkU] = evv
            sil_used[sp] = evv
        f0_used[wb2] = evt
        b.wait_ev("gpsimd", evv)
        ev_ht_store[hb] = b.dma("gpsimd", h_d[nci], h_t[hb][:], ch="hst%d" % hb)

    ev_f0_T = b.now("eT")
    b.wait_ev("sync", ev_f0_T)
    b.wait_ev("vector", ev_f0_T)
    for i in range(2):
        b.wait_ev("sync", ev_ht_store.get(i))
    b.op("vector", lambda e: e.memset(oh86[:], 0.0))
    oh86v = oh86.rearrange("p (c n) -> p c n", n=86)
    ev_oh86 = None
    for j in range(GN):
        ev_oh86 = b.op("vector", lambda e, o=oh86v[:, j, j:j + 1]: e.memset(o, 1.0),
                       inc="eV" if j == GN - 1 else None) or ev_oh86
    ev_g_store = {}
    f2h_used = None
    cs_free = None
    wd_used = {}
    for mb in range(NFB):
        m0 = mb * FB
        b.wait_ev("sync", f2h_used)
        for nci in range(GN):
            b.dma("sync", f2_h.rearrange("p (c n) -> p c n", n=FB)[:, nci, :],
                  h_d[nci][:, m0:m0 + FB], ch="f2h%d" % (nci % 2))
        ev_h = [b.qnow("f2h0"), b.qnow("f2h1")]
        hv = f2_h.rearrange("p (c n) -> p c n", n=FB)
        for _e in ev_h:
            b.wait_ev("tensor", _e)
        b.wait_ev("tensor", ev_oh86)
        b.pe_bank_wait((7,))
        csh_evt = None
        for j in range(GN):
            csh_evt = b.op("tensor", mm(bank(7)[0:86, :], oh86v[:, j, :], hv[:, j, :],
                                        j == 0, j == GN - 1),
                           inc="eT" if j == GN - 1 else None) or csh_evt
        b.wait_ev("scalar", csh_evt)
        b.wait_ev("scalar", cs_free)
        ev_csh = b.op("scalar", lambda e, o=f2_cs[0:86, :], i=bank(7)[0:86, :]:
                      e.activation(o, i, AF.Identity, scale=-1.0), inc="eA")
        b.bank_free[7] = ev_csh
        wd_ev = {}

        def load_wd(oc2):
            sl = oc2 % 2
            evs = {0: [], 1: []}
            for hs in range(2):
                b.wait_ev("sync", wd_used.get((sl, hs)))
                for qd, (c0, cn) in enumerate(((0, 22), (22, 21))):
                    evs[hs].append(b.dma(
                        "sync", f2_wd[sl * 2 + hs][:, c0 * 128:(c0 + cn) * 128],
                        Wd_t[oc2, :, hs * 43 + c0:hs * 43 + c0 + cn, :]
                        .rearrange("p c n -> p (c n)"), ch="ffd%d" % (hs * 2 + qd)))
            wd_ev[oc2] = evs

        load_wd(0)
        load_wd(1)
        for oc in range(OC):
            sl = oc % 2
            evwd = wd_ev.pop(oc)
            b.wait_ev("sync", zs_used.get(oc % 2))
            evzd = b.dma("sync", zs4[oc % 2][0:86, :],
                         zs_s["d"][:, oc * 128:(oc + 1) * 128], ch="f0z%d" % (oc % 2))
            bk_ = 4 + oc % 2
            b.wait_ev("tensor", evzd)
            b.wait_ev("tensor", ev_csh)
            b.pe_bank_wait((bk_,))
            for nci in range(GN):
                hb2 = 0 if nci < 43 else 1
                if nci in (0, 43):
                    for _e in evwd[hb2]:
                        b.wait_ev("tensor", _e)
                dv = f2_wd[sl * 2 + hb2].rearrange("p (c n) -> p c n", n=128)
                inc = "eT" if nci in (42, GN - 1) else None
                ev_ = b.op("tensor", mm(bank(bk_), dv[:, nci - 43 * hb2, :],
                                        hv[:, nci, :], nci == 0, False), inc=inc)
                if nci == 42:
                    wd_used[(sl, 0)] = ev_
                elif nci == GN - 1:
                    wd_used[(sl, 1)] = ev_
            evt = b.op("tensor", mm(bank(bk_), zs4[oc % 2][0:86, :], f2_cs[0:86, :],
                                    False, True), inc="eT")
            zs_used[oc % 2] = evt
            cs_free = evt
            f2h_used = evt
            if oc + 2 < OC:
                load_wd(oc + 2)
            elif mb + 1 < NFB and oc + 2 - OC < 2:
                pass
            sp = oc % 2
            b.wait_ev("scalar", evt)
            b.wait_ev("scalar", ev_g_store.get(sp))
            eva = b.op("scalar", lambda e, o=f2_g[sp], i=bank(bk_):
                       e.activation(o, i, AF.Silu), inc="eA")
            b.bank_free[bk_] = eva
            b.wait_ev("gpsimd", eva)
            ev_g_store[sp] = b.dma("gpsimd", g_s[oc * 128:(oc + 1) * 128,
                                                 mb * FB:(mb + 1) * FB], f2_g[sp],
                                   ch="ffgs%d" % sp)
    ev_g_all = [b.qnow("ffgs0"), b.qnow("ffgs1")]

    cb_used = {}
    for _e in ev_ctx_all + ev_g_all:
        b.wait_ev("sync", _e)
    for i in range(2):
        b.wait_ev("sync", denfree.get(i))
    for sci in range(16):
        bi = sci % 2
        l0 = sci * 128
        b.wait_ev("sync", cb_used.get(bi))
        ev_lds = [b.dma("sync", cb_x[bi], xd[l0:l0 + 128, :], ch="cbx%d" % bi),
                  b.dma("sync", cb_c[bi], ctx_tm[l0:l0 + 128, :], ch="cbc%d" % bi),
                  b.dma("sync", cb_dn[bi], den_d[l0:l0 + 128, :], ch="cbc%d" % bi)]
        for j in range(8):
            ev_ld = b.dma("sync", cb_g[bi].rearrange("p (c n) -> p c n", n=512)[:, j, :],
                          g_s[j * 512:(j + 1) * 512, l0:l0 + 128], transpose=True,
                          ch="cbg%d" % ((sci * 8 + j) % 4))
        ev_lds.append(b.qnow("cbg%d" % ((sci * 8 + 7) % 4)))
        for j in range(3):
            ev_lds.append(b.qnow("cbg%d" % j))
        for _e in ev_lds:
            b.wait_ev("vector", _e)
        b.wait_ev("vector", cb_used.get(("o", bi)))
        b.op("vector", lambda e, o=cb_r[bi], i=cb_dn[bi]: e.reciprocal(o, i))
        b.opb("vector", lambda e, o=cb_t, c=cb_c[bi], g=cb_g[bi]:
              e.tensor_tensor(o, c, g, ALU.mult), inc="eV")
        b.opb("vector", lambda e, o=cb_t2.rearrange("p (h d) -> p h d", d=128),
              t=cb_t.rearrange("p (h d) -> p h d", d=128),
              r=cb_r[bi].rearrange("p (h o) -> p h o", o=1).broadcast_to([128, H, 128]):
              e.tensor_tensor(o, t, r, ALU.mult), inc="eV")
        evv = b.op("vector", lambda e, o=cb_o[bi], t=cb_t2, x=cb_x[bi]:
                   e.tensor_tensor(o, t, x, ALU.add), inc="eV")
        cb_used[bi] = evv
        b.wait_ev("gpsimd", evv)
        cb_used[("o", bi)] = b.dma("gpsimd", outd[l0:l0 + 128, :], cb_o[bi],
                                   ch="cbo%d" % bi)

    for name, (dd, src) in dump.items():
        b.dma("sync", dd, src, ch="dump")

    for name in list(b.cnt):
        if name.startswith("q_"):
            eng = b.chan_eng.get(name, "sync")
            b.wait(eng, name, b.cnt[name])

    if not b.check_deadlock():
        raise RuntimeError("offline deadlock check failed")

    with nc.Block() as block:
        @block.sync
        def _(e):
            for f in b.ops["sync"]:
                f(e)

        @block.gpsimd
        def _(e):
            for f in b.ops["gpsimd"]:
                f(e)

        @block.tensor
        def _(e):
            for f in b.ops["tensor"]:
                f(e)

        @block.vector
        def _(e):
            for f in b.ops["vector"]:
                f(e)

        @block.scalar
        def _(e):
            for f in b.ops["scalar"]:
                f(e)

    return nc


def kernel(**inputs):
    f32 = np.float32
    x = np.ascontiguousarray(np.asarray(inputs["L0_attn_hidden_states"], dtype=f32))
    host = {}
    for w, kw in (("q", "L0_attn_wq"), ("k", "L0_attn_wk"), ("v", "L0_attn_wv")):
        host[f"w{w}T"] = np.ascontiguousarray(np.asarray(inputs[kw], dtype=f32).T)
    for w, kb in (("q", "L0_attn_bq"), ("k", "L0_attn_bk")):
        host[f"b{w}t"] = np.ascontiguousarray(
            np.asarray(inputs[kb], dtype=f32).reshape(32, 128).T)
    host["bv"] = np.ascontiguousarray(
        np.asarray(inputs["L0_attn_bv"], dtype=f32).reshape(1, D))
    for pre, nm in (("g", "gate"), ("u", "up"), ("d", "down")):
        host[pre + "qw"] = np.ascontiguousarray(
            np.asarray(inputs[f"L0_ffn_{nm}_qweight"], dtype=np.int32))
        sc = np.ascontiguousarray(np.asarray(inputs[f"L0_ffn_{nm}_scales"], dtype=f32))
        host[pre + "sc"] = sc
        host[pre + "qz"] = np.ascontiguousarray(
            np.asarray(inputs[f"L0_ffn_{nm}_qzeros"], dtype=np.int32))

    nc = build_program()
    in_maps = []
    for c in range(8):
        m = dict(host)
        h = c % 2
        m["x"] = np.ascontiguousarray(x[c // 2][h * SOWN:(h + 1) * SOWN])
        in_maps.append(m)
    res = run_bass_kernel_spmd(nc, in_maps, core_ids=list(range(8)))
    out = np.empty((B, S, D), dtype=f32)
    for c in range(8):
        h = c % 2
        out[c // 2, h * SOWN:(h + 1) * SOWN, :] = res.results[c]["out"]
    return out


if __name__ == "__main__":
    pass


# version 15 (final)
# speedup vs baseline: 1.6977x; 1.1436x over previous
import numpy as np
import ml_dtypes
import concourse.bass as bass
import concourse.mybir as mybir
from concourse.bass_utils import run_bass_kernel_spmd

F32 = mybir.dt.float32
BF16 = mybir.dt.bfloat16
I32 = mybir.dt.int32
U8 = mybir.dt.uint8
AF = mybir.ActivationFunctionType
ALU = mybir.AluOpType
AX = mybir.AxisListType
BF = ml_dtypes.bfloat16

D = 4096
S = 4096
B = 4
H = 32
I = 11008
SOWN = 2048
KC = 32
GN = 86
OC = 32
MT = 512
NMT = SOWN // MT
FB = 512
NFB = SOWN // FB
RD = float(1.0 / np.sqrt(128.0))
LNEPS = 1e-5
AWQ_L = [0, 2, 4, 6, 1, 3, 5, 7]

ENGINES = ("sync", "gpsimd", "tensor", "vector", "scalar")

DUMP = ()


def ceil_div(a, b):
    return (a + b - 1) // b


class Build:
    def __init__(self, nc):
        self.nc = nc
        self.ops = {e: [] for e in ENGINES}
        self.sem = {}
        self.cnt = {}
        for name in ("eT", "eA", "eV", "eP"):
            self.sem[name] = nc.alloc_semaphore(name)
            self.cnt[name] = 0
        self.bank_free = {}
        self.trace = {}
        self.chan_eng = {}

    def emit(self, eng, fn, meta=None):
        self.ops[eng].append(fn)
        self.trace.setdefault(eng, []).append(meta or ("op", None, 0))

    def wait(self, eng, semname, val):
        if val is None or val <= 0:
            return
        s = self.sem[semname]
        self.emit(eng, lambda e, s=s, v=val: e.wait_ge(s, v),
                  meta=("wait", semname, val))

    def wait_ev(self, eng, ev):
        if ev:
            self.wait(eng, ev[0], ev[1])

    def dma(self, eng, out, in_, ch="misc", **kw):
        semname = "q_" + ch
        if semname not in self.sem:
            self.sem[semname] = self.nc.alloc_semaphore(semname)
            self.cnt[semname] = 0
        self.chan_eng[semname] = eng
        s = self.sem[semname]
        self.cnt[semname] += 16
        self.emit(eng, lambda e, o=out, i=in_, s=s, kw=kw:
                  e.dma_start(out=o, in_=i, **kw).then_inc(s, 16),
                  meta=("inc", semname, 16))
        return (semname, self.cnt[semname])

    def qnow(self, ch):
        return ("q_" + ch, self.cnt.get("q_" + ch, 0))

    def op(self, eng, fn, inc=None):
        if inc:
            s = self.sem[inc]
            self.cnt[inc] += 1
            v = self.cnt[inc]
            self.emit(eng, lambda e, fn=fn, s=s: fn(e).then_inc(s, 1),
                      meta=("inc", inc, 1))
            return (inc, v)
        self.emit(eng, lambda e, fn=fn: fn(e))
        return None

    def opb(self, eng, fn, inc):
        ev = self.op(eng, fn, inc=inc)
        self.wait_ev(eng, ev)
        return ev

    def now(self, semname):
        return (semname, self.cnt[semname])

    def pe_bank_wait(self, banks):
        for bk in banks:
            ev = self.bank_free.get(bk)
            self.wait_ev("tensor", ev)
            self.bank_free[bk] = None

    def check_deadlock(self):
        import collections
        ptr = {e: 0 for e in ENGINES}
        cnt = collections.defaultdict(int)
        n = {e: len(self.trace.get(e, [])) for e in ENGINES}
        progress = True
        while progress:
            progress = False
            for e in ENGINES:
                while ptr[e] < n[e]:
                    kind, sem, val = self.trace[e][ptr[e]]
                    if kind == "wait" and cnt[sem] < val:
                        break
                    if kind == "inc":
                        cnt[sem] += val
                    ptr[e] += 1
                    progress = True
        stuck = {e: (ptr[e], n[e]) for e in ENGINES if ptr[e] < n[e]}
        if stuck:
            print("DEADLOCK! stuck engines:", stuck)
            for e, (p, _) in stuck.items():
                kind, sem, val = self.trace[e][p]
                print(f"  {e}@{p}: waiting {sem} >= {val}, have {cnt[sem]}")
            return False
        return True


def build_program():
    nc = bass.Bass("TRN2", target_bir_lowering=False)

    xd = nc.dram_tensor("x", [SOWN, D], F32, kind="ExternalInput").ap()
    wT = {w: nc.dram_tensor(f"w{w}T", [D, D], F32, kind="ExternalInput").ap()
          for w in ("q", "k", "v")}
    bT = {w: nc.dram_tensor(f"b{w}t", [128, 32], F32, kind="ExternalInput").ap()
          for w in ("q", "k", "v")}
    identd = nc.dram_tensor("identd", [128, 128], F32, kind="ExternalInput").ap()
    qw_d = {"g": nc.dram_tensor("gqw", [D, 1376], I32, kind="ExternalInput").ap(),
            "u": nc.dram_tensor("uqw", [D, 1376], I32, kind="ExternalInput").ap(),
            "d": nc.dram_tensor("dqw", [I, 512], I32, kind="ExternalInput").ap()}
    sc_d = {"g": nc.dram_tensor("gsc", [32, I], F32, kind="ExternalInput").ap(),
            "u": nc.dram_tensor("usc", [32, I], F32, kind="ExternalInput").ap(),
            "d": nc.dram_tensor("dsc", [86, D], F32, kind="ExternalInput").ap()}
    qz_d = {"g": nc.dram_tensor("gqz", [32, 1376], I32, kind="ExternalInput").ap(),
            "u": nc.dram_tensor("uqz", [32, 1376], I32, kind="ExternalInput").ap(),
            "d": nc.dram_tensor("dqz", [86, 512], I32, kind="ExternalInput").ap()}
    outd = nc.dram_tensor("out", [SOWN, D], F32, kind="ExternalOutput").ap()

    xn_s = nc.dram_tensor("xn_s", [SOWN, D], BF16, kind="Internal").ap()
    wq7 = nc.dram_tensor("wq7", [32, 128, 32, 128], BF16, kind="Internal").ap()
    wk7 = nc.dram_tensor("wk7", [32, 128, 32, 128], BF16, kind="Internal").ap()
    wv7 = nc.dram_tensor("wv7", [32, 128, 32, 128], BF16, kind="Internal").ap()
    w7 = {"q": wq7, "k": wk7, "v": wv7}
    q_d = nc.dram_tensor("q_d", [NMT, 128, H, MT], BF16, kind="Internal").ap()
    k_d = nc.dram_tensor("k_d", [NMT, 128, H, MT], BF16, kind="Internal").ap()
    v_h8 = nc.dram_tensor("v_h8", [32, SOWN, 128], BF16, kind="Internal").ap()
    ctx_tm = nc.dram_tensor("ctx_tm", [SOWN, D], BF16, kind="Internal").ap()
    den_d = nc.dram_tensor("den_d", [SOWN, H], F32, kind="Internal").ap()
    g_s = nc.dram_tensor("g_s", [D, SOWN], BF16, kind="Internal").ap()
    Wg_t = nc.dram_tensor("Wg_t", [GN, 128, KC, 128], BF16, kind="Internal").ap()
    Wu_t = nc.dram_tensor("Wu_t", [GN, 128, KC, 128], BF16, kind="Internal").ap()
    Wd_t = nc.dram_tensor("Wd_t", [OC, 128, GN, 128], BF16, kind="Internal").ap()
    zs_s = {"g": nc.dram_tensor("zs_g", [32, I], BF16, kind="Internal").ap(),
            "u": nc.dram_tensor("zs_u", [32, I], BF16, kind="Internal").ap(),
            "d": nc.dram_tensor("zs_d", [86, D], BF16, kind="Internal").ap()}
    scb_d = {"g": nc.dram_tensor("scb_g", [32, I], BF16, kind="Internal").ap(),
             "u": nc.dram_tensor("scb_u", [32, I], BF16, kind="Internal").ap(),
             "d": nc.dram_tensor("scb_d", [86, D], BF16, kind="Internal").ap()}
    h_d = nc.dram_tensor("h_d", [GN, 128, SOWN], BF16, kind="Internal").ap()

    dump = {}
    for name, src in (("xn", xn_s), ("v", v_h8), ("ctx", ctx_tm), ("g", g_s),
                      ("wg", Wg_t), ("wu", Wu_t), ("wd", Wd_t)):
        if name in DUMP:
            dump[name] = (nc.dram_tensor("dump_" + name, list(src.shape), BF16,
                                         kind="ExternalOutput").ap(), src)

    ones_bf = nc.alloc_sbuf_tensor("ones_bf", [128, 128], BF16)
    ident = nc.alloc_sbuf_tensor("ident", [128, 128], F32)
    bt_sb = {w: nc.alloc_sbuf_tensor(f"bt_{w}", [128, 32], F32)
             for w in ("q", "k", "v")}
    epsc = nc.alloc_sbuf_tensor("epsc", [128, 1], F32)
    ARENA_KB = 198
    arena = nc.alloc_sbuf_tensor("arena", [128, ARENA_KB * 1024], U8)

    def view(off_b, n, dtype):
        sz = {F32: 4, BF16: 2, I32: 4}[dtype]
        assert off_b + n * sz <= ARENA_KB * 1024, (off_b, n, dtype)
        return arena[:, off_b:off_b + n * sz].bitcast(dtype)

    K1 = 1024
    ln_x = [view(0, D, F32), view(16 * K1, D, F32)]
    ln_sq = view(32 * K1, D, BF16)
    ln_xn = [view(40 * K1, D, BF16), view(48 * K1, D, BF16)]
    ln_st = view(56 * K1, 16, F32)
    d0_qz = view(64 * K1, 1376, I32)
    d0_z = view(70 * K1, 1376, I32)
    d0_sc = view(76 * K1, 1376, F32)
    d0_zs = view(82 * K1, 1376, BF16)
    d0_scb = view(86 * K1, 1376, BF16)

    pj_w = [view(131072, KC * 128, BF16), view(139264, KC * 128, BF16)]
    pvev = [view(147456, 512, BF16), view(148480, 512, BF16)]
    pvevf = [view(196224, 512, F32), view(198272, 512, F32)]
    vtev = [view(200320, 512, BF16), view(201344, 512, BF16)]
    q_sl = view(0, H * MT, BF16)
    k_sl = view(32 * K1, H * MT, BF16)
    mx_v = [view(64 * K1 + i * 8192, 32 * 128, BF16) for i in range(3)]
    mx_p = [view(88 * K1, 512, BF16), view(88 * K1 + 1024, 512, BF16)]
    mx_cx = [view(90 * K1, 16 * 128, BF16), view(94 * K1, 16 * 128, BF16)]
    den_sb = [view(98 * K1, 512, F32), view(100 * K1, 512, F32)]
    _o = 146 * K1
    dqoff = {}
    for nm, nb in (("qw0", 688 * 4), ("qw1", 688 * 4), ("sr0", 2752 * 2),
                   ("sr1", 2752 * 2), ("nib", 2752 * 4), ("wout", 5504 * 2)):
        dqoff[nm] = _o
        _o += nb
    assert _o <= ARENA_KB * K1, _o

    def dqv(name, n, dtype):
        sz = {F32: 4, BF16: 2, I32: 4}[dtype]
        return arena[:, dqoff[name]:dqoff[name] + n * sz].bitcast(dtype)

    dq_qw = [dqv("qw0", 688, I32), dqv("qw1", 688, I32)]
    dq_sr = [dqv("sr0", 2752, BF16), dqv("sr1", 2752, BF16)]
    dq_nib = dqv("nib", 2752, I32)
    dq_wout = dqv("wout", 5504, BF16)

    fx_full = view(0, KC * SOWN, BF16)
    f0_wg = [view(131072, KC * 128, BF16), view(139264, KC * 128, BF16)]
    f0_wu = [view(147456, KC * 128, BF16), view(155648, KC * 128, BF16)]
    h_t = [view(163840, SOWN, BF16), view(167936, SOWN, BF16)]
    f0_sil = [view(172032, 512, BF16), view(173056, 512, BF16)]
    cs4 = view(174080, 4 * 512, BF16)
    oh32 = view(178176, KC * 32, BF16)
    zs4 = [view(180224 + i * 256, 128, BF16) for i in range(4)]
    f2_h = view(0, GN * 512, BF16)
    f2_wd = [view(88064 + i * 11008, 43 * 128, BF16) for i in range(4)]
    oh86 = view(132096, 86 * 86, BF16)
    f2_g = [view(146888, 512, BF16), view(147912, 512, BF16)]
    f2_cs = view(148936, 512, BF16)
    cb_x = [view(0, D, F32), view(16 * K1, D, F32)]
    cb_c = [view(32 * K1, D, BF16), view(40 * K1, D, BF16)]
    cb_g = [view(48 * K1, D, BF16), view(56 * K1, D, BF16)]
    cb_t = view(64 * K1, D, F32)
    cb_o = [view(80 * K1, D, F32), view(96 * K1, D, F32)]
    cb_dn = [view(112 * K1, 32, F32), view(112 * K1 + 128, 32, F32)]
    cb_r = [view(112 * K1 + 256, 32, F32), view(112 * K1 + 384, 32, F32)]
    cb_t2 = view(116 * K1, D, F32)

    P = nc.alloc_psum_tensor("P", [128, 4096], F32)

    def bank(i, n=1):
        return P[:, i * 512:(i + n) * 512]

    b = Build(nc)

    def mm(out, lhsT, rhs, start=True, stop=True):
        return lambda e: e.matmul(out, lhsT, rhs, start=start, stop=stop)

    def lp(fn, *a, **kw):
        with nc.allow_low_precision(reason="bf16 operand rounding (within bf16-matmul precision)"):
            return fn(*a, **kw)

    b.op("vector", lambda e: e.memset(ones_bf[:], 1.0))
    ev_const = b.op("vector", lambda e: e.memset(epsc[:], LNEPS), inc="eV")
    b.wait_ev("scalar", ev_const)
    b.wait_ev("gpsimd", ev_const)
    b.wait_ev("tensor", ev_const)
    for w in ("q", "k", "v"):
        ev_bt = b.dma("sync", bt_sb[w][:], bT[w], ch="bt")

    ev_wcast = {}
    for w in ("q", "k", "v"):
        for kc in range(KC):
            ev_wcast[w] = b.dma(
                "gpsimd", w7[w][:, :, kc, :].rearrange("a p n -> p a n"),
                wT[w][kc * 128:(kc + 1) * 128, :].rearrange("p (a n) -> p a n", n=128),
                ch="cast" + w)

    WCFG = {
        "g": dict(K=KC, N=I, pk=1376, G=32, qwd=qw_d["g"], scd=sc_d["g"], qzd=qz_d["g"]),
        "u": dict(K=KC, N=I, pk=1376, G=32, qwd=qw_d["u"], scd=sc_d["u"], qzd=qz_d["u"]),
        "d": dict(K=GN, N=D, pk=512, G=86, qwd=qw_d["d"], scd=sc_d["d"], qzd=qz_d["d"]),
    }
    ev_d0_v = None
    d0_scb_done = {}
    for w in ("g", "u", "d"):
        cfg = WCFG[w]
        G, N = cfg["G"], cfg["N"]
        nblk = 8
        blkn = N // nblk
        blkp = blkn // 8
        if ev_d0_v:
            b.wait_ev("sync", ev_d0_v)
        b.dma("sync", d0_qz[0:G, 0:blkp * nblk], cfg["qzd"], ch="d0l")
        for blk in range(nblk):
            if ev_d0_v:
                b.wait_ev("sync", ev_d0_v)
            ev_sc = b.dma("sync", d0_sc[0:G, 0:blkn],
                          cfg["scd"][:, blk * blkn:(blk + 1) * blkn], ch="d0l")
            b.wait_ev("vector", ev_sc)
            ev_scb = b.op("vector", lambda e, o=d0_scb[0:G, 0:blkn],
                          i=d0_sc[0:G, 0:blkn]: e.tensor_copy(o, i), inc="eV")
            for j in range(8):
                qzsl = d0_qz[0:G, blk * blkp:(blk + 1) * blkp]
                fn = (lambda e, o=d0_z[0:G, AWQ_L[j]:blkn:8], i=qzsl, sh=4 * j:
                      e.tensor_scalar(o, i, sh, 15, ALU.logical_shift_right,
                                      ALU.bitwise_and))
                if j == 7:
                    b.opb("vector", fn, inc="eV")
                else:
                    b.op("vector", fn)
            ev_d0_v = b.op("vector", lambda e, o=d0_zs[0:G, 0:blkn], z=d0_z[0:G, 0:blkn],
                           s=d0_sc[0:G, 0:blkn]: e.tensor_tensor(o, z, s, ALU.mult),
                           inc="eV")
            b.wait_ev("gpsimd", ev_scb)
            b.dma("gpsimd", scb_d[w][:, blk * blkn:(blk + 1) * blkn],
                  d0_scb[0:G, 0:blkn], ch="d0s")
            b.wait_ev("gpsimd", ev_d0_v)
            ev_zs = b.dma("gpsimd", zs_s[w][:, blk * blkn:(blk + 1) * blkn],
                          d0_zs[0:G, 0:blkn], ch="d0s")
            b.wait_ev("vector", ev_zs)
        d0_scb_done[w] = b.qnow("d0s")
    ev_d0_done = b.qnow("d0s")
    ev_d0_vec = b.now("eV")

    c_inv = 1.0 / D
    ev_ln_act = [None, None]
    for i in range(16):
        bi = i % 2
        if ev_ln_act[bi]:
            b.wait_ev("sync", ev_ln_act[bi])
        ev_x = b.dma("sync", ln_x[bi], xd[i * 128:(i + 1) * 128, :], ch="lnx%d" % bi)
        st = ln_st
        b.wait_ev("vector", ev_x)
        b.opb("vector", lambda e, o=st[:, 0:1], i2=ln_x[bi]:
              e.reduce_sum(o, i2, axis=AX.X), inc="eV")
        b.wait_ev("scalar", ev_x)
        eva = b.op("scalar", lambda e, o=ln_sq, i2=ln_x[bi], a=st[:, 1:2]:
                   e.activation(o, i2, AF.Square, accum_out=a), inc="eA")
        b.wait_ev("vector", eva)
        b.opb("vector", lambda e, o=st[:, 2:3], i2=st[:, 0:1]:
              e.tensor_scalar_mul(o, i2, c_inv), inc="eV")
        b.opb("vector", lambda e, o=st[:, 3:4], i2=st[:, 2:3]:
              e.tensor_tensor(o, i2, i2, ALU.mult), inc="eV")
        evv = b.opb("vector", lambda e, o=st[:, 4:5], i2=st[:, 1:2], m2=st[:, 3:4]:
                    e.scalar_tensor_tensor(o, i2, c_inv, m2, ALU.mult, ALU.subtract),
                    inc="eV")
        b.wait_ev("scalar", evv)
        eva = b.op("scalar", lambda e, o=st[:, 5:6], i2=st[:, 4:5]:
                   e.activation(o, i2, AF.Sqrt, bias=epsc[:]), inc="eA")
        b.wait_ev("vector", eva)
        b.opb("vector", lambda e, o=st[:, 6:7], i2=st[:, 5:6]:
              e.reciprocal(o, i2), inc="eV")
        evv = b.opb("vector", lambda e, o=st[:, 7:8], i2=st[:, 2:3], r=st[:, 6:7]:
                    e.scalar_tensor_tensor(o, i2, -1.0, r, ALU.mult, ALU.mult), inc="eV")
        b.wait_ev("scalar", evv)
        ev_ln_act[bi] = b.op("scalar", lambda e, o=ln_xn[bi], i2=ln_x[bi],
                             r=st[:, 6:7], m=st[:, 7:8]:
                             e.activation(o, i2, AF.Identity, bias=m, scale=r), inc="eA")
        b.wait_ev("sync", ev_ln_act[bi])
        b.dma("sync", xn_s[i * 128:(i + 1) * 128, :], ln_xn[bi], ch="lnst")
    ev_ln_all = b.qnow("lnst")

    dq_state = {}
    dq_halves = []
    for w in ("g", "u", "d"):
        for ch in range(WCFG[w]["K"]):
            for h2 in (0, 1):
                dq_halves.append((w, ch, h2))
    NQ = 2 * len(dq_halves)

    def dq_issue_qw(t):
        w, ch, h2 = dq_halves[t]
        hq = WCFG[w]["pk"] // 2
        qb = t % 2
        b.wait_ev("gpsimd", dq_state.get(("qwfree", qb)))
        dq_state[("qwld", t)] = b.dma(
            "gpsimd", dq_qw[qb][:, 0:hq],
            WCFG[w]["qwd"][ch * 128:(ch + 1) * 128, h2 * hq:(h2 + 1) * hq],
            ch="dqw%d" % qb)

    def dq_issue_sr(u):
        t, qih = divmod(u, 2)
        w, ch, h2 = dq_halves[t]
        qn = WCFG[w]["N"] // 4
        hf = h2 * 2 + qih
        sb_ = u % 2
        b.wait_ev("gpsimd", dq_state.get(("srfree", sb_)))
        b.wait_ev("gpsimd", d0_scb_done[w])
        dq_state[("srld", u)] = b.dma(
            "gpsimd", dq_sr[sb_][:, 0:qn],
            scb_d[w][ch:ch + 1, hf * qn:(hf + 1) * qn].partition_broadcast(128),
            ch="dqs%d" % sb_)

    def dequant_steps():
        for t0 in (0, 1):
            dq_issue_qw(t0)
        for u0 in (0, 1):
            dq_issue_sr(u0)
        for t, (w, ch, h2) in enumerate(dq_halves):
            cfg = WCFG[w]
            N, pk = cfg["N"], cfg["pk"]
            qn = N // 4
            qp = pk // 4
            nhalf = (N // 2) // 128
            qb = t % 2
            tgt = {"g": Wg_t, "u": Wu_t, "d": Wd_t}[w]
            evm = None
            for qih in range(2):
                u = 2 * t + qih
                sb_ = u % 2
                if qih == 0:
                    b.wait_ev("vector", dq_state[("qwld", t)])
                    b.wait_ev("vector", dq_state.get("woutfree"))
                for j in range(8):
                    src = dq_qw[qb][:, qih * qp:(qih + 1) * qp]
                    fn = (lambda e, o=dq_nib[:, AWQ_L[j]:qn:8], i=src, sh=4 * j:
                          e.tensor_scalar(o, i, sh, 15, ALU.logical_shift_right,
                                          ALU.bitwise_and))
                    if j == 7:
                        evup = b.opb("vector", fn, inc="eV")
                    else:
                        b.op("vector", fn)
                if qih == 1:
                    dq_state[("qwfree", qb)] = evup
                    if t + 2 < len(dq_halves):
                        dq_issue_qw(t + 2)
                b.wait_ev("vector", dq_state[("srld", u)])
                evm = b.op("vector", lambda e, o=dq_wout[:, qih * qn:(qih + 1) * qn],
                           i=dq_nib[:, 0:qn], s=dq_sr[sb_][:, 0:qn]:
                           e.tensor_tensor(o, i, s, ALU.mult), inc="eV")
                dq_state[("srfree", sb_)] = evm
                if u + 2 < NQ:
                    dq_issue_sr(u + 2)
                yield
            b.wait_ev("gpsimd", evm)
            dst = tgt[h2 * nhalf:(h2 + 1) * nhalf, :, ch, :]
            dq_state["woutfree"] = b.dma("gpsimd", dst.rearrange("a p n -> p a n"),
                                         dq_wout[:, 0:2 * qn]
                                         .rearrange("p (a n) -> p a n", n=128),
                                         ch="dqo" + w)
        dq_state["done"] = b.now("eV")
        while True:
            yield

    dq_gen = dequant_steps()

    b.wait_ev("sync", ev_ln_all)
    b.wait_ev("sync", ev_d0_done)
    b.wait_ev("sync", ev_d0_vec)
    b.wait_ev("scalar", ev_d0_done)
    b.wait_ev("scalar", ev_d0_vec)
    for kc in range(KC):
        for tb in range(4):
            b.dma("sync",
                  fx_full.rearrange("p (c n) -> p c n", n=SOWN)
                  [:, kc, tb * 512:(tb + 1) * 512],
                  xn_s[tb * 512:(tb + 1) * 512, kc * 128:(kc + 1) * 128],
                  transpose=True, ch="pjx%d" % (tb % 2))
    ev_xt_all = [b.qnow("pjx0"), b.qnow("pjx1")]
    xnv = fx_full.rearrange("p (c n) -> p c n", n=SOWN)
    evI = b.dma("sync", ident[:], identd, ch="bt")
    wslab_used = [None, None]
    ev_wslab = [None, None]
    pv_free = {}
    vt_free = {}
    qkd = {"q": q_d, "k": k_d}
    first_mm = True
    for w in ("q", "k", "v"):
        for ncn in range(H):
            wb2 = ncn % 2
            b.wait_ev("sync", ev_wcast[w])
            b.wait_ev("sync", wslab_used[wb2])
            ev_wslab[wb2] = [
                b.dma("sync", pj_w[wb2][:, hf * 2048:(hf + 1) * 2048],
                      w7[w][ncn][:, hf * 16:(hf + 1) * 16, :]
                      .rearrange("p c n -> p (c n)"), ch="pjw%d%d" % (wb2, hf))
                for hf in range(2)]
            wv_ = pj_w[wb2].rearrange("p (c n) -> p c n", n=128)
            evt = None
            for tb in range(4):
                bk_ = tb
                if first_mm:
                    for _e in ev_xt_all:
                        b.wait_ev("tensor", _e)
                    first_mm = False
                for _e in ev_wslab[wb2]:
                    b.wait_ev("tensor", _e)
                b.pe_bank_wait((bk_,))
                for kc in range(KC):
                    evt = b.op("tensor", mm(bank(bk_), wv_[:, kc, :],
                                            xnv[:, kc, tb * 512:(tb + 1) * 512],
                                            kc == 0, kc == KC - 1),
                               inc="eT" if kc == KC - 1 else None) or evt
                sp = tb % 2
                b.wait_ev("scalar", evt)
                if w == "q" and ncn == 0 and tb == 0:
                    b.wait_ev("scalar", ev_bt)
                if w in ("q", "k"):
                    b.wait_ev("scalar", pv_free.get(sp))
                    eva = b.op("scalar", lambda e, o=pvev[sp], i=bank(bk_),
                               bias=bt_sb[w][:, ncn:ncn + 1]:
                               e.activation(o, i, AF.Identity, bias=bias), inc="eA")
                    b.bank_free[bk_] = eva
                    pv_free[sp] = b.dma("scalar", qkd[w][tb, :, ncn, :], pvev[sp][:],
                                        ch="pqs%d" % sp)
                else:
                    b.wait_ev("scalar", pv_free.get(("vf", sp)))
                    eva = b.op("scalar", lambda e, o=pvevf[sp], i=bank(bk_),
                               bias=bt_sb["v"][:, ncn:ncn + 1]:
                               e.activation(o, i, AF.Identity, bias=bias), inc="eA")
                    b.bank_free[bk_] = eva
                    tbk = 4 + sp
                    b.wait_ev("tensor", eva)
                    b.wait_ev("tensor", evI)
                    b.pe_bank_wait((tbk,))
                    evt2 = None
                    for j in range(4):
                        evt2 = b.op("tensor", lambda e,
                                    o=bank(tbk)[:, j * 128:(j + 1) * 128],
                                    i=pvevf[sp][:, j * 128:(j + 1) * 128]:
                                    e.transpose(o, i, ident[:]),
                                    inc="eT" if j == 3 else None) or evt2
                    pv_free[("vf", sp)] = evt2
                    b.wait_ev("scalar", evt2)
                    b.wait_ev("scalar", vt_free.get(sp))
                    eva2 = b.op("scalar", lambda e, o=vtev[sp], i=bank(tbk):
                                e.activation(o, i, AF.Identity), inc="eA")
                    b.bank_free[tbk] = eva2
                    for j in range(4):
                        vt_free[sp] = b.dma(
                            "scalar",
                            v_h8[ncn, tb * 512 + j * 128:tb * 512 + (j + 1) * 128, :],
                            vtev[sp].rearrange("p (j d) -> p j d", d=128)[:, j, :],
                            ch="pvs%d" % sp)
            wslab_used[wb2] = evt
            next(dq_gen)
            next(dq_gen)
    while "done" not in dq_state:
        next(dq_gen)
    ev_dq_done = dq_state["done"]
    ev_pj_T = b.now("eT")

    ev_vtm = [b.qnow("pvs0"), b.qnow("pvs1")]
    mixv_used = [None, None, None]
    last_exp = {}
    pfree_t = {}
    denfree = {}
    cxfree = {}
    qk_used = None
    b.wait_ev("sync", ev_pj_T)
    for chn in ("pqs0", "pqs1", "pvs0", "pvs1"):
        b.wait_ev("sync", b.qnow(chn))
    for mt in range(NMT):
        m0 = mt * MT
        b.wait_ev("sync", qk_used)
        qk_ld = []
        for w, dst in (("q", q_sl), ("k", k_sl)):
            for hf in range(2):
                qk_ld.append(b.dma(
                    "sync", dst[:, hf * 16 * MT:(hf + 1) * 16 * MT],
                    qkd[w][mt][:, hf * 16:(hf + 1) * 16, :]
                    .rearrange("p h t -> p (h t)"), ch="qk%d" % (len(qk_ld) % 4)))
        qv3 = q_sl.rearrange("p (c n) -> p c n", n=MT)
        kv3 = k_sl.rearrange("p (c n) -> p c n", n=MT)
        ev_mv_b = {}
        for g16 in range(MT // 16):
            gm0 = g16 * 16
            par = g16 % 2
            if gm0 % 32 == 0:
                vb = (gm0 // 32) % 3
                for _e in ev_vtm:
                    b.wait_ev("sync", _e)
                b.wait_ev("sync", mixv_used[vb])
                ev_mv_b[vb] = b.dma(
                    "sync", mx_v[vb].rearrange("p (c n) -> p c n", n=128)[0:32],
                    v_h8[:, m0 + gm0:m0 + gm0 + 32, :], ch="mixv%d" % vb)
            vb = (gm0 // 32) % 3
            vbase = (gm0 // 32) * 32
            ev_mv = ev_mv_b[vb]
            scb = 4 + par
            for _e in qk_ld:
                b.wait_ev("tensor", _e)
            b.wait_ev("tensor", last_exp.get(scb))
            b.pe_bank_wait((scb,))
            evt = None
            for j in range(16):
                mloc = gm0 + j
                evt = b.op("tensor", mm(bank(scb)[0:32, j * 32:(j + 1) * 32],
                                        kv3[:, :, mloc], qv3[:, :, mloc]),
                           inc="eT" if j == 15 else None) or evt
            b.wait_ev("scalar", evt)
            b.wait_ev("scalar", pfree_t.get(par))
            eva = b.op("scalar", lambda e, o=mx_p[par][0:32, :], i=bank(scb)[0:32, :]:
                       e.activation(o, i, AF.Exp, scale=RD), inc="eA")
            last_exp[scb] = eva
            cb = 6 + par
            b.wait_ev("tensor", eva)
            b.pe_bank_wait((cb,))
            evt_d = b.op("tensor", mm(bank(cb)[0:1, :], ones_bf[0:32, 0:1],
                                      mx_p[par][0:32, :]), inc="eT")
            b.wait_ev("scalar", evt_d)
            b.wait_ev("scalar", denfree.get(par))
            ev_den = b.op("scalar", lambda e, o=den_sb[par][0:1, :], i=bank(cb)[0:1, :]:
                          e.activation(o, i, AF.Identity), inc="eA")
            b.bank_free[cb] = ev_den
            gidx = (m0 + gm0) // 16
            denfree[par] = b.dma(
                "scalar", den_d.rearrange("(a m) h -> a (m h)", m=16)[gidx:gidx + 1, :],
                den_sb[par][0:1, :], ch="den%d" % par)
            vv3 = mx_v[vb].rearrange("p (c n) -> p c n", n=128)
            cxb = par
            last_pv_evac = None
            for rnd in range(4):
                b.wait_ev("tensor", ev_mv)
                if rnd > 0:
                    b.wait_ev("tensor", last_pv_evac)
                b.pe_bank_wait((cb,))
                evt = None
                for j in range(4):
                    tok = gm0 + rnd * 4 + j
                    evt = b.op("tensor", mm(bank(cb)[0:32, j * 128:(j + 1) * 128],
                                            mx_p[par][0:32,
                                                      (rnd * 4 + j) * 32:
                                                      (rnd * 4 + j + 1) * 32],
                                            vv3[0:32, tok - vbase, :]),
                               inc="eT" if j == 3 else None) or evt
                b.wait_ev("scalar", evt)
                if rnd == 0:
                    b.wait_ev("scalar", cxfree.get(cxb))
                last_pv_evac = b.op("scalar", lambda e,
                                    o=mx_cx[cxb][0:32, rnd * 512:(rnd + 1) * 512],
                                    i=bank(cb)[0:32, :]:
                                    e.activation(o, i, AF.Identity), inc="eA")
                b.bank_free[cb] = last_pv_evac
            pfree_t[par] = b.now("eT")
            cxfree[cxb] = b.dma(
                "scalar", ctx_tm[m0 + gm0:m0 + gm0 + 16, :]
                .rearrange("c (h d) -> h c d", d=128),
                mx_cx[cxb][0:32, :].rearrange("p (c n) -> p c n", n=128)[0:32],
                ch="ctm%d" % cxb)
            mixv_used[vb] = b.now("eT")
        qk_used = b.now("eT")
    ev_ctx_all = [b.qnow("ctm0"), b.qnow("ctm1")]

    ev_mix_T = b.now("eT")
    ev_mix_A = b.now("eA")
    ev_mix_V = b.now("eV")
    for w in ("g", "u", "d"):
        b.wait_ev("sync", b.qnow("dqo" + w))
        b.wait_ev("vector", b.qnow("dqo" + w))
        b.wait_ev("scalar", b.qnow("dqo" + w))
    b.wait_ev("sync", ev_d0_done)
    b.wait_ev("sync", ev_mix_T)
    b.wait_ev("sync", ev_mix_A)
    b.wait_ev("sync", ev_mix_V)
    for i in range(2):
        b.wait_ev("sync", b.qnow("ctm%d" % i))
    b.wait_ev("scalar", ev_dq_done)
    b.wait_ev("tensor", ev_dq_done)
    ev_fx = None
    for kc in range(KC):
        for tb in range(4):
            ev_fx = b.dma("sync",
                          fx_full.rearrange("p (c n) -> p c n", n=SOWN)
                          [:, kc, tb * 512:(tb + 1) * 512],
                          xn_s[tb * 512:(tb + 1) * 512, kc * 128:(kc + 1) * 128],
                          transpose=True, ch="ffx")
    fxv = fx_full.rearrange("p (c n) -> p c n", n=SOWN)
    b.op("vector", lambda e: e.memset(oh32[:], 0.0))
    oh32v = oh32.rearrange("p (c n) -> p c n", n=32)
    ev_oh = None
    for kc in range(KC):
        ev_oh = b.op("vector", lambda e, o=oh32v[:, kc, kc:kc + 1]: e.memset(o, 1.0),
                     inc="eV" if kc == KC - 1 else None) or ev_oh
    b.wait_ev("tensor", ev_fx)
    b.wait_ev("tensor", ev_oh)
    ev_cs4 = None
    for tb in range(4):
        b.pe_bank_wait((6,))
        evt = None
        for kc in range(KC):
            evt = b.op("tensor", mm(bank(6)[0:32, :], oh32v[:, kc, :],
                                    fxv[:, kc, tb * 512:(tb + 1) * 512],
                                    kc == 0, kc == KC - 1),
                       inc="eT" if kc == KC - 1 else None) or evt
        b.wait_ev("scalar", evt)
        b.wait_ev("scalar", ev_cs4)
        ev_cs4 = b.op("scalar", lambda e, o=cs4[0:32, tb * 512:(tb + 1) * 512],
                      i=bank(6)[0:32, :]:
                      e.activation(o, i, AF.Identity, scale=-1.0), inc="eA")
        b.bank_free[6] = ev_cs4
    b.wait_ev("tensor", ev_cs4)
    f0_used = {}
    zs_used = {}
    sil_used = {}
    ev_ht_store = {}
    for nci in range(GN):
        wb2 = nci % 2
        b.wait_ev("sync", f0_used.get(wb2))
        evwg = b.dma("sync", f0_wg[wb2][:], Wg_t[nci].rearrange("p c n -> p (c n)"),
                     ch="f0w%d" % wb2)
        evwu = b.dma("sync", f0_wu[wb2][:], Wu_t[nci].rearrange("p c n -> p (c n)"),
                     ch="f0u%d" % wb2)
        b.wait_ev("sync", zs_used.get(wb2))
        b.dma("sync", zs4[wb2][0:32, :], zs_s["g"][:, nci * 128:(nci + 1) * 128],
              ch="f0z%d" % wb2)
        b.wait_ev("sync", zs_used.get(2 + wb2))
        evzu = b.dma("sync", zs4[2 + wb2][0:32, :],
                     zs_s["u"][:, nci * 128:(nci + 1) * 128], ch="f0z%d" % wb2)
        gv = f0_wg[wb2].rearrange("p (c n) -> p c n", n=128)
        uv = f0_wu[wb2].rearrange("p (c n) -> p c n", n=128)
        hb = nci % 2
        b.wait_ev("vector", ev_ht_store.get(hb))
        evt = None
        evv = None
        for tb in range(4):
            bkG, bkU = 2 * tb, 2 * tb + 1
            b.wait_ev("tensor", evwg)
            b.wait_ev("tensor", evwu)
            b.wait_ev("tensor", evzu)
            for bk_, wsl, zi in ((bkG, gv, wb2), (bkU, uv, 2 + wb2)):
                b.pe_bank_wait((bk_,))
                for kc in range(KC):
                    b.op("tensor", mm(bank(bk_), wsl[:, kc, :],
                                      fxv[:, kc, tb * 512:(tb + 1) * 512],
                                      kc == 0, False))
                evt = b.op("tensor", mm(bank(bk_), zs4[zi][0:32, :],
                                        cs4[0:32, tb * 512:(tb + 1) * 512],
                                        False, True), inc="eT")
                zs_used[zi] = evt
            sp = tb % 2
            b.wait_ev("scalar", evt)
            b.wait_ev("scalar", sil_used.get(sp))
            eva = b.op("scalar", lambda e, o=f0_sil[sp], i=bank(bkG):
                       e.activation(o, i, AF.Silu), inc="eA")
            b.bank_free[bkG] = eva
            b.wait_ev("vector", eva)
            evv = b.op("vector", lambda e, o=h_t[hb][:, tb * 512:(tb + 1) * 512],
                       i=f0_sil[sp], u=bank(bkU): e.tensor_tensor(o, i, u, ALU.mult),
                       inc="eV")
            b.bank_free[bkU] = evv
            sil_used[sp] = evv
        f0_used[wb2] = evt
        b.wait_ev("scalar", evv)
        ev_ht_store[hb] = b.dma("scalar", h_d[nci], h_t[hb][:], ch="hst%d" % hb)

    ev_f0_T = b.now("eT")
    b.wait_ev("sync", ev_f0_T)
    b.wait_ev("vector", ev_f0_T)
    for i in range(2):
        b.wait_ev("sync", ev_ht_store.get(i))
    b.op("vector", lambda e: e.memset(oh86[:], 0.0))
    oh86v = oh86.rearrange("p (c n) -> p c n", n=86)
    ev_oh86 = None
    for j in range(GN):
        ev_oh86 = b.op("vector", lambda e, o=oh86v[:, j, j:j + 1]: e.memset(o, 1.0),
                       inc="eV" if j == GN - 1 else None) or ev_oh86
    ev_g_store = {}
    f2h_used = None
    cs_free = None
    wd_used = {}
    for mb in range(NFB):
        m0 = mb * FB
        b.wait_ev("sync", f2h_used)
        for nci in range(GN):
            b.dma("sync", f2_h.rearrange("p (c n) -> p c n", n=FB)[:, nci, :],
                  h_d[nci][:, m0:m0 + FB], ch="f2h%d" % (nci % 4))
        ev_h = [b.qnow("f2h%d" % i) for i in range(4)]
        hv = f2_h.rearrange("p (c n) -> p c n", n=FB)
        for _e in ev_h:
            b.wait_ev("tensor", _e)
        b.wait_ev("tensor", ev_oh86)
        b.pe_bank_wait((7,))
        csh_evt = None
        for j in range(GN):
            csh_evt = b.op("tensor", mm(bank(7)[0:86, :], oh86v[:, j, :], hv[:, j, :],
                                        j == 0, j == GN - 1),
                           inc="eT" if j == GN - 1 else None) or csh_evt
        b.wait_ev("scalar", csh_evt)
        b.wait_ev("scalar", cs_free)
        ev_csh = b.op("scalar", lambda e, o=f2_cs[0:86, :], i=bank(7)[0:86, :]:
                      e.activation(o, i, AF.Identity, scale=-1.0), inc="eA")
        b.bank_free[7] = ev_csh
        wd_ev = {}

        def load_wd(oc2):
            sl = oc2 % 2
            evs = {0: [], 1: []}
            for hs in range(2):
                b.wait_ev("sync", wd_used.get((sl, hs)))
                for qd, (c0, cn) in enumerate(((0, 22), (22, 21))):
                    evs[hs].append(b.dma(
                        "sync", f2_wd[sl * 2 + hs][:, c0 * 128:(c0 + cn) * 128],
                        Wd_t[oc2, :, hs * 43 + c0:hs * 43 + c0 + cn, :]
                        .rearrange("p c n -> p (c n)"), ch="ffd%d" % (hs * 2 + qd)))
            wd_ev[oc2] = evs

        load_wd(0)
        load_wd(1)
        for oc in range(OC):
            sl = oc % 2
            evwd = wd_ev.pop(oc)
            b.wait_ev("sync", zs_used.get(oc % 2))
            evzd = b.dma("sync", zs4[oc % 2][0:86, :],
                         zs_s["d"][:, oc * 128:(oc + 1) * 128], ch="f0z%d" % (oc % 2))
            bk_ = 4 + oc % 2
            b.wait_ev("tensor", evzd)
            b.wait_ev("tensor", ev_csh)
            b.pe_bank_wait((bk_,))
            for nci in range(GN):
                hb2 = 0 if nci < 43 else 1
                if nci in (0, 43):
                    for _e in evwd[hb2]:
                        b.wait_ev("tensor", _e)
                dv = f2_wd[sl * 2 + hb2].rearrange("p (c n) -> p c n", n=128)
                inc = "eT" if nci in (42, GN - 1) else None
                ev_ = b.op("tensor", mm(bank(bk_), dv[:, nci - 43 * hb2, :],
                                        hv[:, nci, :], nci == 0, False), inc=inc)
                if nci == 42:
                    wd_used[(sl, 0)] = ev_
                elif nci == GN - 1:
                    wd_used[(sl, 1)] = ev_
            evt = b.op("tensor", mm(bank(bk_), zs4[oc % 2][0:86, :], f2_cs[0:86, :],
                                    False, True), inc="eT")
            zs_used[oc % 2] = evt
            cs_free = evt
            f2h_used = evt
            if oc + 2 < OC:
                load_wd(oc + 2)
            elif mb + 1 < NFB and oc + 2 - OC < 2:
                pass
            sp = oc % 2
            b.wait_ev("scalar", evt)
            b.wait_ev("scalar", ev_g_store.get(sp))
            eva = b.op("scalar", lambda e, o=f2_g[sp], i=bank(bk_):
                       e.activation(o, i, AF.Silu), inc="eA")
            b.bank_free[bk_] = eva
            ev_g_store[sp] = b.dma("scalar", g_s[oc * 128:(oc + 1) * 128,
                                                 mb * FB:(mb + 1) * FB], f2_g[sp],
                                   ch="ffgs%d" % sp)
    ev_g_all = [b.qnow("ffgs0"), b.qnow("ffgs1")]

    cb_used = {}
    for _e in ev_ctx_all + ev_g_all:
        b.wait_ev("sync", _e)
    for i in range(2):
        b.wait_ev("sync", denfree.get(i))
    for sci in range(16):
        bi = sci % 2
        l0 = sci * 128
        b.wait_ev("sync", cb_used.get(bi))
        ev_lds = [b.dma("sync", cb_x[bi], xd[l0:l0 + 128, :], ch="cbx%d" % bi),
                  b.dma("sync", cb_c[bi], ctx_tm[l0:l0 + 128, :], ch="cbc%d" % bi),
                  b.dma("sync", cb_dn[bi], den_d[l0:l0 + 128, :], ch="cbc%d" % bi)]
        for j in range(8):
            ev_ld = b.dma("sync", cb_g[bi].rearrange("p (c n) -> p c n", n=512)[:, j, :],
                          g_s[j * 512:(j + 1) * 512, l0:l0 + 128], transpose=True,
                          ch="cbg%d" % ((sci * 8 + j) % 4))
        ev_lds.append(b.qnow("cbg%d" % ((sci * 8 + 7) % 4)))
        for j in range(3):
            ev_lds.append(b.qnow("cbg%d" % j))
        for _e in ev_lds:
            b.wait_ev("vector", _e)
        b.wait_ev("vector", cb_used.get(("o", bi)))
        b.op("vector", lambda e, o=cb_r[bi], i=cb_dn[bi]: e.reciprocal(o, i))
        b.opb("vector", lambda e, o=cb_t, c=cb_c[bi], g=cb_g[bi]:
              e.tensor_tensor(o, c, g, ALU.mult), inc="eV")
        b.opb("vector", lambda e, o=cb_t2.rearrange("p (h d) -> p h d", d=128),
              t=cb_t.rearrange("p (h d) -> p h d", d=128),
              r=cb_r[bi].rearrange("p (h o) -> p h o", o=1).broadcast_to([128, H, 128]):
              e.tensor_tensor(o, t, r, ALU.mult), inc="eV")
        evv = b.op("vector", lambda e, o=cb_o[bi], t=cb_t2, x=cb_x[bi]:
                   e.tensor_tensor(o, t, x, ALU.add), inc="eV")
        cb_used[bi] = evv
        b.wait_ev("scalar", evv)
        cb_used[("o", bi)] = b.dma("scalar", outd[l0:l0 + 128, :], cb_o[bi],
                                   ch="cbo%d" % bi)

    for name, (dd, src) in dump.items():
        b.dma("sync", dd, src, ch="dump")

    for name in list(b.cnt):
        if name.startswith("q_"):
            eng = b.chan_eng.get(name, "sync")
            b.wait(eng, name, b.cnt[name])

    if not b.check_deadlock():
        raise RuntimeError("offline deadlock check failed")

    with nc.Block() as block:
        @block.sync
        def _(e):
            for f in b.ops["sync"]:
                f(e)

        @block.gpsimd
        def _(e):
            for f in b.ops["gpsimd"]:
                f(e)

        @block.tensor
        def _(e):
            for f in b.ops["tensor"]:
                f(e)

        @block.vector
        def _(e):
            for f in b.ops["vector"]:
                f(e)

        @block.scalar
        def _(e):
            for f in b.ops["scalar"]:
                f(e)

    return nc


def kernel(**inputs):
    f32 = np.float32
    x = np.ascontiguousarray(np.asarray(inputs["L0_attn_hidden_states"], dtype=f32))
    host = {}
    for w, kw in (("q", "L0_attn_wq"), ("k", "L0_attn_wk"), ("v", "L0_attn_wv")):
        host[f"w{w}T"] = np.ascontiguousarray(np.asarray(inputs[kw], dtype=f32).T)
    for w, kb in (("q", "L0_attn_bq"), ("k", "L0_attn_bk"), ("v", "L0_attn_bv")):
        host[f"b{w}t"] = np.ascontiguousarray(
            np.asarray(inputs[kb], dtype=f32).reshape(32, 128).T)
    host["identd"] = np.eye(128, dtype=f32)
    for pre, nm in (("g", "gate"), ("u", "up"), ("d", "down")):
        host[pre + "qw"] = np.ascontiguousarray(
            np.asarray(inputs[f"L0_ffn_{nm}_qweight"], dtype=np.int32))
        sc = np.ascontiguousarray(np.asarray(inputs[f"L0_ffn_{nm}_scales"], dtype=f32))
        host[pre + "sc"] = sc
        host[pre + "qz"] = np.ascontiguousarray(
            np.asarray(inputs[f"L0_ffn_{nm}_qzeros"], dtype=np.int32))

    nc = build_program()
    in_maps = []
    for c in range(8):
        m = dict(host)
        h = c % 2
        m["x"] = np.ascontiguousarray(x[c // 2][h * SOWN:(h + 1) * SOWN])
        in_maps.append(m)
    res = run_bass_kernel_spmd(nc, in_maps, core_ids=list(range(8)))
    out = np.empty((B, S, D), dtype=f32)
    for c in range(8):
        h = c % 2
        out[c // 2, h * SOWN:(h + 1) * SOWN, :] = res.results[c]["out"]
    return out


if __name__ == "__main__":
    pass


# version 19
# speedup vs baseline: 1.7279x; 1.0178x over previous
import numpy as np
import ml_dtypes
import concourse.bass as bass
import concourse.mybir as mybir
from concourse.bass_utils import run_bass_kernel_spmd

F32 = mybir.dt.float32
BF16 = mybir.dt.bfloat16
I32 = mybir.dt.int32
U8 = mybir.dt.uint8
AF = mybir.ActivationFunctionType
ALU = mybir.AluOpType
AX = mybir.AxisListType
BF = ml_dtypes.bfloat16

D = 4096
S = 4096
B = 4
H = 32
I = 11008
SOWN = 2048
KC = 32
GN = 86
OC = 32
MT = 512
NMT = SOWN // MT
FB = 512
NFB = SOWN // FB
RD = float(1.0 / np.sqrt(128.0))
LNEPS = 1e-5
AWQ_L = [0, 2, 4, 6, 1, 3, 5, 7]

ENGINES = ("sync", "gpsimd", "tensor", "vector", "scalar")

DUMP = ()


def ceil_div(a, b):
    return (a + b - 1) // b


class Build:
    def __init__(self, nc):
        self.nc = nc
        self.ops = {e: [] for e in ENGINES}
        self.sem = {}
        self.cnt = {}
        for name in ("eT", "eA", "eV", "eP"):
            self.sem[name] = nc.alloc_semaphore(name)
            self.cnt[name] = 0
        self.bank_free = {}
        self.trace = {}
        self.chan_eng = {}

    def emit(self, eng, fn, meta=None):
        self.ops[eng].append(fn)
        self.trace.setdefault(eng, []).append(meta or ("op", None, 0))

    def wait(self, eng, semname, val):
        if val is None or val <= 0:
            return
        s = self.sem[semname]
        self.emit(eng, lambda e, s=s, v=val: e.wait_ge(s, v),
                  meta=("wait", semname, val))

    def wait_ev(self, eng, ev):
        if ev:
            self.wait(eng, ev[0], ev[1])

    def dma(self, eng, out, in_, ch="misc", **kw):
        semname = "q_" + ch
        if semname not in self.sem:
            self.sem[semname] = self.nc.alloc_semaphore(semname)
            self.cnt[semname] = 0
        self.chan_eng[semname] = eng
        s = self.sem[semname]
        self.cnt[semname] += 16
        self.emit(eng, lambda e, o=out, i=in_, s=s, kw=kw:
                  e.dma_start(out=o, in_=i, **kw).then_inc(s, 16),
                  meta=("inc", semname, 16))
        return (semname, self.cnt[semname])

    def qnow(self, ch):
        return ("q_" + ch, self.cnt.get("q_" + ch, 0))

    def op(self, eng, fn, inc=None):
        if inc:
            s = self.sem[inc]
            self.cnt[inc] += 1
            v = self.cnt[inc]
            self.emit(eng, lambda e, fn=fn, s=s: fn(e).then_inc(s, 1),
                      meta=("inc", inc, 1))
            return (inc, v)
        self.emit(eng, lambda e, fn=fn: fn(e))
        return None

    def opb(self, eng, fn, inc):
        ev = self.op(eng, fn, inc=inc)
        self.wait_ev(eng, ev)
        return ev

    def now(self, semname):
        return (semname, self.cnt[semname])

    def pe_bank_wait(self, banks):
        for bk in banks:
            ev = self.bank_free.get(bk)
            self.wait_ev("tensor", ev)
            self.bank_free[bk] = None

    def check_deadlock(self):
        import collections
        ptr = {e: 0 for e in ENGINES}
        cnt = collections.defaultdict(int)
        n = {e: len(self.trace.get(e, [])) for e in ENGINES}
        progress = True
        while progress:
            progress = False
            for e in ENGINES:
                while ptr[e] < n[e]:
                    kind, sem, val = self.trace[e][ptr[e]]
                    if kind == "wait" and cnt[sem] < val:
                        break
                    if kind == "inc":
                        cnt[sem] += val
                    ptr[e] += 1
                    progress = True
        stuck = {e: (ptr[e], n[e]) for e in ENGINES if ptr[e] < n[e]}
        if stuck:
            print("DEADLOCK! stuck engines:", stuck)
            for e, (p, _) in stuck.items():
                kind, sem, val = self.trace[e][p]
                print(f"  {e}@{p}: waiting {sem} >= {val}, have {cnt[sem]}")
            return False
        return True


def build_program():
    nc = bass.Bass("TRN2", target_bir_lowering=False)

    xd = nc.dram_tensor("x", [SOWN, D], F32, kind="ExternalInput").ap()
    wT = {w: nc.dram_tensor(f"w{w}T", [D, D], F32, kind="ExternalInput").ap()
          for w in ("q", "k", "v")}
    bT = {w: nc.dram_tensor(f"b{w}t", [128, 32], F32, kind="ExternalInput").ap()
          for w in ("q", "k", "v")}
    identd = nc.dram_tensor("identd", [128, 128], F32, kind="ExternalInput").ap()
    qw_d = {"g": nc.dram_tensor("gqw", [D, 1376], I32, kind="ExternalInput").ap(),
            "u": nc.dram_tensor("uqw", [D, 1376], I32, kind="ExternalInput").ap(),
            "d": nc.dram_tensor("dqw", [I, 512], I32, kind="ExternalInput").ap()}
    sc_d = {"g": nc.dram_tensor("gsc", [32, I], F32, kind="ExternalInput").ap(),
            "u": nc.dram_tensor("usc", [32, I], F32, kind="ExternalInput").ap(),
            "d": nc.dram_tensor("dsc", [86, D], F32, kind="ExternalInput").ap()}
    qz_d = {"g": nc.dram_tensor("gqz", [32, 1376], I32, kind="ExternalInput").ap(),
            "u": nc.dram_tensor("uqz", [32, 1376], I32, kind="ExternalInput").ap(),
            "d": nc.dram_tensor("dqz", [86, 512], I32, kind="ExternalInput").ap()}
    outd = nc.dram_tensor("out", [SOWN, D], F32, kind="ExternalOutput").ap()

    xn_s = nc.dram_tensor("xn_s", [SOWN, D], BF16, kind="Internal").ap()
    wq7 = nc.dram_tensor("wq7", [32, 128, 32, 128], BF16, kind="Internal").ap()
    wk7 = nc.dram_tensor("wk7", [32, 128, 32, 128], BF16, kind="Internal").ap()
    wv7 = nc.dram_tensor("wv7", [32, 128, 32, 128], BF16, kind="Internal").ap()
    w7 = {"q": wq7, "k": wk7, "v": wv7}
    q_d = nc.dram_tensor("q_d", [NMT, 128, H, MT], BF16, kind="Internal").ap()
    k_d = nc.dram_tensor("k_d", [NMT, 128, H, MT], BF16, kind="Internal").ap()
    v_h8 = nc.dram_tensor("v_h8", [32, SOWN, 128], BF16, kind="Internal").ap()
    ctx_tm = nc.dram_tensor("ctx_tm", [SOWN, D], BF16, kind="Internal").ap()
    den_d = nc.dram_tensor("den_d", [SOWN, H], F32, kind="Internal").ap()
    g_s = nc.dram_tensor("g_s", [D, SOWN], BF16, kind="Internal").ap()
    Wg_t = nc.dram_tensor("Wg_t", [GN, 128, KC, 128], BF16, kind="Internal").ap()
    Wu_t = nc.dram_tensor("Wu_t", [GN, 128, KC, 128], BF16, kind="Internal").ap()
    Wd_t = nc.dram_tensor("Wd_t", [OC, 128, GN, 128], BF16, kind="Internal").ap()
    zs_s = {"g": nc.dram_tensor("zs_g", [32, I], BF16, kind="Internal").ap(),
            "u": nc.dram_tensor("zs_u", [32, I], BF16, kind="Internal").ap(),
            "d": nc.dram_tensor("zs_d", [86, D], BF16, kind="Internal").ap()}
    scb_d = {"g": nc.dram_tensor("scb_g", [32, I], BF16, kind="Internal").ap(),
             "u": nc.dram_tensor("scb_u", [32, I], BF16, kind="Internal").ap(),
             "d": nc.dram_tensor("scb_d", [86, D], BF16, kind="Internal").ap()}
    h_d = nc.dram_tensor("h_d", [GN, 128, SOWN], BF16, kind="Internal").ap()

    dump = {}
    for name, src in (("xn", xn_s), ("v", v_h8), ("ctx", ctx_tm), ("g", g_s),
                      ("wg", Wg_t), ("wu", Wu_t), ("wd", Wd_t)):
        if name in DUMP:
            dump[name] = (nc.dram_tensor("dump_" + name, list(src.shape), BF16,
                                         kind="ExternalOutput").ap(), src)

    ones_bf = nc.alloc_sbuf_tensor("ones_bf", [128, 128], BF16)
    ident = nc.alloc_sbuf_tensor("ident", [128, 128], F32)
    bt_sb = {w: nc.alloc_sbuf_tensor(f"bt_{w}", [128, 32], F32)
             for w in ("q", "k", "v")}
    epsc = nc.alloc_sbuf_tensor("epsc", [128, 1], F32)
    ARENA_KB = 198
    arena = nc.alloc_sbuf_tensor("arena", [128, ARENA_KB * 1024], U8)

    def view(off_b, n, dtype):
        sz = {F32: 4, BF16: 2, I32: 4}[dtype]
        assert off_b + n * sz <= ARENA_KB * 1024, (off_b, n, dtype)
        return arena[:, off_b:off_b + n * sz].bitcast(dtype)

    K1 = 1024
    ln_x = [view(0, D, F32), view(16 * K1, D, F32)]
    ln_sq = view(32 * K1, D, BF16)
    ln_xn = [view(40 * K1, D, BF16), view(48 * K1, D, BF16)]
    ln_st = view(56 * K1, 16, F32)
    d0_qz = view(64 * K1, 1376, I32)
    d0_z = view(70 * K1, 1376, I32)
    d0_sc = view(76 * K1, 1376, F32)
    d0_zs = view(82 * K1, 1376, BF16)
    d0_scb = view(86 * K1, 1376, BF16)

    pj_w = [view(131072, KC * 128, BF16), view(139264, KC * 128, BF16)]
    pvev = [view(147456, 512, BF16), view(148480, 512, BF16)]
    pvevf = [view(196224, 512, F32), view(198272, 512, F32)]
    vtev = [view(200320, 512, BF16), view(201344, 512, BF16)]
    q_sl = view(0, H * MT, BF16)
    k_sl = view(32 * K1, H * MT, BF16)
    mx_v = [view(64 * K1 + i * 8192, 32 * 128, BF16) for i in range(3)]
    mx_p = [view(88 * K1, 512, BF16), view(88 * K1 + 1024, 512, BF16)]
    mx_cx = [view(90 * K1, 16 * 128, BF16), view(94 * K1, 16 * 128, BF16)]
    den_sb = [view(98 * K1, 512, F32), view(100 * K1, 512, F32)]
    _o = 146 * K1
    dqoff = {}
    for nm, nb in (("qw0", 688 * 4), ("qw1", 688 * 4), ("sr0", 2752 * 2),
                   ("sr1", 2752 * 2), ("nib", 2752 * 4), ("wout", 5504 * 2)):
        dqoff[nm] = _o
        _o += nb
    assert _o <= ARENA_KB * K1, _o

    def dqv(name, n, dtype):
        sz = {F32: 4, BF16: 2, I32: 4}[dtype]
        return arena[:, dqoff[name]:dqoff[name] + n * sz].bitcast(dtype)

    dq_qw = [dqv("qw0", 688, I32), dqv("qw1", 688, I32)]
    dq_sr = [dqv("sr0", 2752, BF16), dqv("sr1", 2752, BF16)]
    dq_nib = dqv("nib", 2752, I32)
    dq_wout = dqv("wout", 5504, BF16)

    fx_full = view(0, KC * SOWN, BF16)
    f0_wg = [view(131072, KC * 128, BF16), view(139264, KC * 128, BF16)]
    f0_wu = [view(147456, KC * 128, BF16), view(155648, KC * 128, BF16)]
    h_t = [view(163840, SOWN, BF16), view(167936, SOWN, BF16)]
    f0_sil = [view(172032, 512, BF16), view(173056, 512, BF16)]
    cs4 = view(174080, 4 * 512, BF16)
    oh32 = view(178176, KC * 32, BF16)
    zs4 = [view(180224 + i * 256, 128, BF16) for i in range(4)]
    f2_h = view(0, GN * 512, BF16)
    f2_wd = [view(88064 + i * 11008, 43 * 128, BF16) for i in range(4)]
    oh86 = view(132096, 86 * 86, BF16)
    f2_g = [view(146888, 512, BF16), view(147912, 512, BF16)]
    f2_cs = view(148936, 512, BF16)
    cb_x = [view(0, D, F32), view(16 * K1, D, F32)]
    cb_c = [view(32 * K1, D, BF16), view(40 * K1, D, BF16)]
    cb_g = [view(48 * K1, D, BF16), view(56 * K1, D, BF16)]
    cb_t = view(64 * K1, D, F32)
    cb_o = [view(80 * K1, D, F32), view(96 * K1, D, F32)]
    cb_dn = [view(112 * K1, 32, F32), view(112 * K1 + 128, 32, F32)]
    cb_r = [view(112 * K1 + 256, 32, F32), view(112 * K1 + 384, 32, F32)]
    cb_t2 = view(116 * K1, D, F32)

    P = nc.alloc_psum_tensor("P", [128, 4096], F32)

    def bank(i, n=1):
        return P[:, i * 512:(i + n) * 512]

    b = Build(nc)

    def mm(out, lhsT, rhs, start=True, stop=True):
        return lambda e: e.matmul(out, lhsT, rhs, start=start, stop=stop)

    def lp(fn, *a, **kw):
        with nc.allow_low_precision(reason="bf16 operand rounding (within bf16-matmul precision)"):
            return fn(*a, **kw)

    b.op("vector", lambda e: e.memset(ones_bf[:], 1.0))
    ev_const = b.op("vector", lambda e: e.memset(epsc[:], LNEPS), inc="eV")
    b.wait_ev("scalar", ev_const)
    b.wait_ev("gpsimd", ev_const)
    b.wait_ev("tensor", ev_const)
    for w in ("q", "k", "v"):
        ev_bt = b.dma("sync", bt_sb[w][:], bT[w], ch="bt")

    ev_wcast = {}
    for w in ("q", "k", "v"):
        for kc in range(KC):
            ev_wcast[w] = b.dma(
                "gpsimd", w7[w][:, :, kc, :].rearrange("a p n -> p a n"),
                wT[w][kc * 128:(kc + 1) * 128, :].rearrange("p (a n) -> p a n", n=128),
                ch="cast" + w)

    WCFG = {
        "g": dict(K=KC, N=I, pk=1376, G=32, qwd=qw_d["g"], scd=sc_d["g"], qzd=qz_d["g"]),
        "u": dict(K=KC, N=I, pk=1376, G=32, qwd=qw_d["u"], scd=sc_d["u"], qzd=qz_d["u"]),
        "d": dict(K=GN, N=D, pk=512, G=86, qwd=qw_d["d"], scd=sc_d["d"], qzd=qz_d["d"]),
    }
    ev_d0_v = None
    d0_scb_done = {}
    for w in ("g", "u", "d"):
        cfg = WCFG[w]
        G, N = cfg["G"], cfg["N"]
        nblk = 8
        blkn = N // nblk
        blkp = blkn // 8
        if ev_d0_v:
            b.wait_ev("sync", ev_d0_v)
        b.dma("sync", d0_qz[0:G, 0:blkp * nblk], cfg["qzd"], ch="d0l")
        for blk in range(nblk):
            if ev_d0_v:
                b.wait_ev("sync", ev_d0_v)
            ev_sc = b.dma("sync", d0_sc[0:G, 0:blkn],
                          cfg["scd"][:, blk * blkn:(blk + 1) * blkn], ch="d0l")
            b.wait_ev("vector", ev_sc)
            ev_scb = b.op("vector", lambda e, o=d0_scb[0:G, 0:blkn],
                          i=d0_sc[0:G, 0:blkn]: e.tensor_copy(o, i), inc="eV")
            for j in range(8):
                qzsl = d0_qz[0:G, blk * blkp:(blk + 1) * blkp]
                fn = (lambda e, o=d0_z[0:G, AWQ_L[j]:blkn:8], i=qzsl, sh=4 * j:
                      e.tensor_scalar(o, i, sh, 15, ALU.logical_shift_right,
                                      ALU.bitwise_and))
                if j == 7:
                    b.opb("vector", fn, inc="eV")
                else:
                    b.op("vector", fn)
            ev_d0_v = b.op("vector", lambda e, o=d0_zs[0:G, 0:blkn], z=d0_z[0:G, 0:blkn],
                           s=d0_sc[0:G, 0:blkn]: e.tensor_tensor(o, z, s, ALU.mult),
                           inc="eV")
            b.wait_ev("gpsimd", ev_scb)
            b.dma("gpsimd", scb_d[w][:, blk * blkn:(blk + 1) * blkn],
                  d0_scb[0:G, 0:blkn], ch="d0s")
            b.wait_ev("gpsimd", ev_d0_v)
            ev_zs = b.dma("gpsimd", zs_s[w][:, blk * blkn:(blk + 1) * blkn],
                          d0_zs[0:G, 0:blkn], ch="d0s")
            b.wait_ev("vector", ev_zs)
        d0_scb_done[w] = b.qnow("d0s")
    ev_d0_done = b.qnow("d0s")
    ev_d0_vec = b.now("eV")

    c_inv = 1.0 / D
    ev_ln_act = [None, None]
    ev_ln_t = []
    for i in range(16):
        bi = i % 2
        if ev_ln_act[bi]:
            b.wait_ev("sync", ev_ln_act[bi])
        ev_x = b.dma("sync", ln_x[bi], xd[i * 128:(i + 1) * 128, :], ch="lnx%d" % bi)
        st = ln_st
        b.wait_ev("vector", ev_x)
        b.opb("vector", lambda e, o=st[:, 0:1], i2=ln_x[bi]:
              e.reduce_sum(o, i2, axis=AX.X), inc="eV")
        b.wait_ev("scalar", ev_x)
        eva = b.op("scalar", lambda e, o=ln_sq, i2=ln_x[bi], a=st[:, 1:2]:
                   e.activation(o, i2, AF.Square, accum_out=a), inc="eA")
        b.wait_ev("vector", eva)
        b.opb("vector", lambda e, o=st[:, 2:3], i2=st[:, 0:1]:
              e.tensor_scalar_mul(o, i2, c_inv), inc="eV")
        b.opb("vector", lambda e, o=st[:, 3:4], i2=st[:, 2:3]:
              e.tensor_tensor(o, i2, i2, ALU.mult), inc="eV")
        evv = b.opb("vector", lambda e, o=st[:, 4:5], i2=st[:, 1:2], m2=st[:, 3:4]:
                    e.scalar_tensor_tensor(o, i2, c_inv, m2, ALU.mult, ALU.subtract),
                    inc="eV")
        b.wait_ev("scalar", evv)
        eva = b.op("scalar", lambda e, o=st[:, 5:6], i2=st[:, 4:5]:
                   e.activation(o, i2, AF.Sqrt, bias=epsc[:]), inc="eA")
        b.wait_ev("vector", eva)
        b.opb("vector", lambda e, o=st[:, 6:7], i2=st[:, 5:6]:
              e.reciprocal(o, i2), inc="eV")
        evv = b.opb("vector", lambda e, o=st[:, 7:8], i2=st[:, 2:3], r=st[:, 6:7]:
                    e.scalar_tensor_tensor(o, i2, -1.0, r, ALU.mult, ALU.mult), inc="eV")
        b.wait_ev("scalar", evv)
        if len(ev_ln_t) >= 2:
            b.wait_ev("scalar", ev_ln_t[-2])
        ev_ln_act[bi] = b.op("scalar", lambda e, o=ln_xn[bi], i2=ln_x[bi],
                             r=st[:, 6:7], m=st[:, 7:8]:
                             e.activation(o, i2, AF.Identity, bias=m, scale=r), inc="eA")
        b.wait_ev("sync", ev_ln_act[bi])
        b.dma("sync", xn_s[i * 128:(i + 1) * 128, :], ln_xn[bi], ch="lnst")
        ev_ln_t.append(b.qnow("lnst"))
    ev_ln_all = b.qnow("lnst")

    dq_state = {}
    dq_halves = []
    for w in ("g", "u", "d"):
        for ch in range(WCFG[w]["K"]):
            for h2 in (0, 1):
                dq_halves.append((w, ch, h2))
    NQ = 2 * len(dq_halves)

    def dq_issue_qw(t):
        w, ch, h2 = dq_halves[t]
        hq = WCFG[w]["pk"] // 2
        qb = t % 2
        b.wait_ev("gpsimd", dq_state.get(("qwfree", qb)))
        dq_state[("qwld", t)] = b.dma(
            "gpsimd", dq_qw[qb][:, 0:hq],
            WCFG[w]["qwd"][ch * 128:(ch + 1) * 128, h2 * hq:(h2 + 1) * hq],
            ch="dqw%d" % qb)

    def dq_issue_sr(u):
        t, qih = divmod(u, 2)
        w, ch, h2 = dq_halves[t]
        qn = WCFG[w]["N"] // 4
        hf = h2 * 2 + qih
        sb_ = u % 2
        b.wait_ev("gpsimd", dq_state.get(("srfree", sb_)))
        b.wait_ev("gpsimd", d0_scb_done[w])
        dq_state[("srld", u)] = b.dma(
            "gpsimd", dq_sr[sb_][:, 0:qn],
            scb_d[w][ch:ch + 1, hf * qn:(hf + 1) * qn].partition_broadcast(128),
            ch="dqs%d" % sb_)

    def dequant_steps():
        for t0 in (0, 1):
            dq_issue_qw(t0)
        for u0 in (0, 1):
            dq_issue_sr(u0)
        for t, (w, ch, h2) in enumerate(dq_halves):
            cfg = WCFG[w]
            N, pk = cfg["N"], cfg["pk"]
            qn = N // 4
            qp = pk // 4
            nhalf = (N // 2) // 128
            qb = t % 2
            tgt = {"g": Wg_t, "u": Wu_t, "d": Wd_t}[w]
            evm = None
            for qih in range(2):
                u = 2 * t + qih
                sb_ = u % 2
                if qih == 0:
                    b.wait_ev("vector", dq_state[("qwld", t)])
                    b.wait_ev("vector", dq_state.get("woutfree"))
                for j in range(8):
                    src = dq_qw[qb][:, qih * qp:(qih + 1) * qp]
                    fn = (lambda e, o=dq_nib[:, AWQ_L[j]:qn:8], i=src, sh=4 * j:
                          e.tensor_scalar(o, i, sh, 15, ALU.logical_shift_right,
                                          ALU.bitwise_and))
                    if j == 7:
                        evup = b.opb("vector", fn, inc="eV")
                    else:
                        b.op("vector", fn)
                if qih == 1:
                    dq_state[("qwfree", qb)] = evup
                    if t + 2 < len(dq_halves):
                        dq_issue_qw(t + 2)
                b.wait_ev("vector", dq_state[("srld", u)])
                evm = b.op("vector", lambda e, o=dq_wout[:, qih * qn:(qih + 1) * qn],
                           i=dq_nib[:, 0:qn], s=dq_sr[sb_][:, 0:qn]:
                           e.tensor_tensor(o, i, s, ALU.mult), inc="eV")
                dq_state[("srfree", sb_)] = evm
                if u + 2 < NQ:
                    dq_issue_sr(u + 2)
                yield
            b.wait_ev("gpsimd", evm)
            dst = tgt[h2 * nhalf:(h2 + 1) * nhalf, :, ch, :]
            dq_state["woutfree"] = b.dma("gpsimd", dst.rearrange("a p n -> p a n"),
                                         dq_wout[:, 0:2 * qn]
                                         .rearrange("p (a n) -> p a n", n=128),
                                         ch="dqo" + w)
        dq_state["done"] = b.now("eV")
        while True:
            yield

    dq_gen = dequant_steps()

    b.wait_ev("sync", ev_ln_all)
    b.wait_ev("sync", ev_d0_done)
    b.wait_ev("sync", ev_d0_vec)
    b.wait_ev("scalar", ev_d0_done)
    b.wait_ev("scalar", ev_d0_vec)
    ev_xt_tb = []
    for tb in range(4):
        for kc in range(KC):
            b.dma("sync",
                  fx_full.rearrange("p (c n) -> p c n", n=SOWN)
                  [:, kc, tb * 512:(tb + 1) * 512],
                  xn_s[tb * 512:(tb + 1) * 512, kc * 128:(kc + 1) * 128],
                  transpose=True, ch="pjx%d" % (tb % 2))
        ev_xt_tb.append(b.qnow("pjx%d" % (tb % 2)))
    xnv = fx_full.rearrange("p (c n) -> p c n", n=SOWN)
    evI = b.dma("sync", ident[:], identd, ch="bt")
    wslab_used = [None, None]
    ev_wslab = [None, None]
    pv_free = {}
    vt_free = {}
    qkd = {"q": q_d, "k": k_d}
    xt_waited = [False] * 4
    for w in ("q", "k", "v"):
        for ncn in range(H):
            wb2 = ncn % 2
            b.wait_ev("sync", ev_wcast[w])
            b.wait_ev("sync", wslab_used[wb2])
            ev_wslab[wb2] = [
                b.dma("sync", pj_w[wb2][:, hf * 2048:(hf + 1) * 2048],
                      w7[w][ncn][:, hf * 16:(hf + 1) * 16, :]
                      .rearrange("p c n -> p (c n)"), ch="pjw%d%d" % (wb2, hf))
                for hf in range(2)]
            wv_ = pj_w[wb2].rearrange("p (c n) -> p c n", n=128)
            evt = None
            for tb in range(4):
                bk_ = tb
                if not xt_waited[tb]:
                    b.wait_ev("tensor", ev_xt_tb[tb])
                    xt_waited[tb] = True
                for _e in ev_wslab[wb2]:
                    b.wait_ev("tensor", _e)
                b.pe_bank_wait((bk_,))
                for kc in range(KC):
                    evt = b.op("tensor", mm(bank(bk_), wv_[:, kc, :],
                                            xnv[:, kc, tb * 512:(tb + 1) * 512],
                                            kc == 0, kc == KC - 1),
                               inc="eT" if kc == KC - 1 else None) or evt
                sp = tb % 2
                b.wait_ev("scalar", evt)
                if w == "q" and ncn == 0 and tb == 0:
                    b.wait_ev("scalar", ev_bt)
                if w in ("q", "k"):
                    b.wait_ev("scalar", pv_free.get(sp))
                    eva = b.op("scalar", lambda e, o=pvev[sp], i=bank(bk_),
                               bias=bt_sb[w][:, ncn:ncn + 1]:
                               e.activation(o, i, AF.Identity, bias=bias), inc="eA")
                    b.bank_free[bk_] = eva
                    pv_free[sp] = b.dma("scalar", qkd[w][tb, :, ncn, :], pvev[sp][:],
                                        ch="pqs%d" % sp)
                else:
                    b.wait_ev("scalar", pv_free.get(("vf", sp)))
                    eva = b.op("scalar", lambda e, o=pvevf[sp], i=bank(bk_),
                               bias=bt_sb["v"][:, ncn:ncn + 1]:
                               e.activation(o, i, AF.Identity, bias=bias), inc="eA")
                    b.bank_free[bk_] = eva
                    tbk = 4 + sp
                    b.wait_ev("tensor", eva)
                    b.wait_ev("tensor", evI)
                    b.pe_bank_wait((tbk,))
                    evt2 = None
                    for j in range(4):
                        evt2 = b.op("tensor", lambda e,
                                    o=bank(tbk)[:, j * 128:(j + 1) * 128],
                                    i=pvevf[sp][:, j * 128:(j + 1) * 128]:
                                    e.transpose(o, i, ident[:]),
                                    inc="eT" if j == 3 else None) or evt2
                    pv_free[("vf", sp)] = evt2
                    b.wait_ev("scalar", evt2)
                    b.wait_ev("scalar", vt_free.get(sp))
                    eva2 = b.op("scalar", lambda e, o=vtev[sp], i=bank(tbk):
                                e.activation(o, i, AF.Identity), inc="eA")
                    b.bank_free[tbk] = eva2
                    for j in range(4):
                        vt_free[sp] = b.dma(
                            "scalar",
                            v_h8[ncn, tb * 512 + j * 128:tb * 512 + (j + 1) * 128, :],
                            vtev[sp].rearrange("p (j d) -> p j d", d=128)[:, j, :],
                            ch="pvs%d" % sp)
            wslab_used[wb2] = evt
            next(dq_gen)
            next(dq_gen)
    while "done" not in dq_state:
        next(dq_gen)
    ev_dq_done = dq_state["done"]
    ev_pj_T = b.now("eT")

    ev_vtm = [b.qnow("pvs0"), b.qnow("pvs1")]
    mixv_used = [None, None, None]
    last_exp = {}
    pfree_t = {}
    denfree = {}
    cxfree = {}
    qk_used = None
    b.wait_ev("sync", ev_pj_T)
    for chn in ("pqs0", "pqs1", "pvs0", "pvs1"):
        b.wait_ev("sync", b.qnow(chn))
    for mt in range(NMT):
        m0 = mt * MT
        b.wait_ev("sync", qk_used)
        qk_ld = []
        for w, dst in (("q", q_sl), ("k", k_sl)):
            for hf in range(2):
                qk_ld.append(b.dma(
                    "sync", dst[:, hf * 16 * MT:(hf + 1) * 16 * MT],
                    qkd[w][mt][:, hf * 16:(hf + 1) * 16, :]
                    .rearrange("p h t -> p (h t)"), ch="qk%d" % (len(qk_ld) % 4)))
        qv3 = q_sl.rearrange("p (c n) -> p c n", n=MT)
        kv3 = k_sl.rearrange("p (c n) -> p c n", n=MT)
        ev_mv_b = {}
        for g16 in range(MT // 16):
            gm0 = g16 * 16
            par = g16 % 2
            if gm0 % 32 == 0:
                vb = (gm0 // 32) % 3
                for _e in ev_vtm:
                    b.wait_ev("sync", _e)
                b.wait_ev("sync", mixv_used[vb])
                ev_mv_b[vb] = b.dma(
                    "sync", mx_v[vb].rearrange("p (c n) -> p c n", n=128)[0:32],
                    v_h8[:, m0 + gm0:m0 + gm0 + 32, :], ch="mixv%d" % vb)
            vb = (gm0 // 32) % 3
            vbase = (gm0 // 32) * 32
            ev_mv = ev_mv_b[vb]
            scb = 4 + par
            for _e in qk_ld:
                b.wait_ev("tensor", _e)
            b.wait_ev("tensor", last_exp.get(scb))
            b.pe_bank_wait((scb,))
            evt = None
            for j in range(16):
                mloc = gm0 + j
                evt = b.op("tensor", mm(bank(scb)[0:32, j * 32:(j + 1) * 32],
                                        kv3[:, :, mloc], qv3[:, :, mloc]),
                           inc="eT" if j == 15 else None) or evt
            b.wait_ev("scalar", evt)
            b.wait_ev("scalar", pfree_t.get(par))
            eva = b.op("scalar", lambda e, o=mx_p[par][0:32, :], i=bank(scb)[0:32, :]:
                       e.activation(o, i, AF.Exp, scale=RD), inc="eA")
            last_exp[scb] = eva
            cb = 6 + par
            b.wait_ev("tensor", eva)
            b.pe_bank_wait((cb,))
            evt_d = b.op("tensor", mm(bank(cb)[0:1, :], ones_bf[0:32, 0:1],
                                      mx_p[par][0:32, :]), inc="eT")
            b.wait_ev("scalar", evt_d)
            b.wait_ev("scalar", denfree.get(par))
            ev_den = b.op("scalar", lambda e, o=den_sb[par][0:1, :], i=bank(cb)[0:1, :]:
                          e.activation(o, i, AF.Identity), inc="eA")
            b.bank_free[cb] = ev_den
            gidx = (m0 + gm0) // 16
            denfree[par] = b.dma(
                "scalar", den_d.rearrange("(a m) h -> a (m h)", m=16)[gidx:gidx + 1, :],
                den_sb[par][0:1, :], ch="den%d" % par)
            vv3 = mx_v[vb].rearrange("p (c n) -> p c n", n=128)
            cxb = par
            last_pv_evac = None
            for rnd in range(4):
                b.wait_ev("tensor", ev_mv)
                if rnd > 0:
                    b.wait_ev("tensor", last_pv_evac)
                b.pe_bank_wait((cb,))
                evt = None
                for j in range(4):
                    tok = gm0 + rnd * 4 + j
                    evt = b.op("tensor", mm(bank(cb)[0:32, j * 128:(j + 1) * 128],
                                            mx_p[par][0:32,
                                                      (rnd * 4 + j) * 32:
                                                      (rnd * 4 + j + 1) * 32],
                                            vv3[0:32, tok - vbase, :]),
                               inc="eT" if j == 3 else None) or evt
                b.wait_ev("scalar", evt)
                if rnd == 0:
                    b.wait_ev("scalar", cxfree.get(cxb))
                last_pv_evac = b.op("scalar", lambda e,
                                    o=mx_cx[cxb][0:32, rnd * 512:(rnd + 1) * 512],
                                    i=bank(cb)[0:32, :]:
                                    e.activation(o, i, AF.Identity), inc="eA")
                b.bank_free[cb] = last_pv_evac
            pfree_t[par] = b.now("eT")
            cxfree[cxb] = b.dma(
                "scalar", ctx_tm[m0 + gm0:m0 + gm0 + 16, :]
                .rearrange("c (h d) -> h c d", d=128),
                mx_cx[cxb][0:32, :].rearrange("p (c n) -> p c n", n=128)[0:32],
                ch="ctm%d" % cxb)
            mixv_used[vb] = b.now("eT")
        qk_used = b.now("eT")
    ev_ctx_all = [b.qnow("ctm0"), b.qnow("ctm1")]

    ev_mix_T = b.now("eT")
    ev_mix_A = b.now("eA")
    ev_mix_V = b.now("eV")
    for w in ("g", "u", "d"):
        b.wait_ev("sync", b.qnow("dqo" + w))
        b.wait_ev("vector", b.qnow("dqo" + w))
        b.wait_ev("scalar", b.qnow("dqo" + w))
    b.wait_ev("sync", ev_d0_done)
    b.wait_ev("sync", ev_mix_T)
    b.wait_ev("sync", ev_mix_A)
    b.wait_ev("sync", ev_mix_V)
    for i in range(2):
        b.wait_ev("sync", b.qnow("ctm%d" % i))
    b.wait_ev("scalar", ev_dq_done)
    b.wait_ev("tensor", ev_dq_done)
    ev_fx = None
    for kc in range(KC):
        for tb in range(4):
            ev_fx = b.dma("sync",
                          fx_full.rearrange("p (c n) -> p c n", n=SOWN)
                          [:, kc, tb * 512:(tb + 1) * 512],
                          xn_s[tb * 512:(tb + 1) * 512, kc * 128:(kc + 1) * 128],
                          transpose=True, ch="ffx")
    fxv = fx_full.rearrange("p (c n) -> p c n", n=SOWN)
    b.op("vector", lambda e: e.memset(oh32[:], 0.0))
    oh32v = oh32.rearrange("p (c n) -> p c n", n=32)
    ev_oh = None
    for kc in range(KC):
        ev_oh = b.op("vector", lambda e, o=oh32v[:, kc, kc:kc + 1]: e.memset(o, 1.0),
                     inc="eV" if kc == KC - 1 else None) or ev_oh
    b.wait_ev("tensor", ev_fx)
    b.wait_ev("tensor", ev_oh)
    ev_cs4 = None
    for tb in range(4):
        b.pe_bank_wait((6,))
        evt = None
        for kc in range(KC):
            evt = b.op("tensor", mm(bank(6)[0:32, :], oh32v[:, kc, :],
                                    fxv[:, kc, tb * 512:(tb + 1) * 512],
                                    kc == 0, kc == KC - 1),
                       inc="eT" if kc == KC - 1 else None) or evt
        b.wait_ev("scalar", evt)
        b.wait_ev("scalar", ev_cs4)
        ev_cs4 = b.op("scalar", lambda e, o=cs4[0:32, tb * 512:(tb + 1) * 512],
                      i=bank(6)[0:32, :]:
                      e.activation(o, i, AF.Identity, scale=-1.0), inc="eA")
        b.bank_free[6] = ev_cs4
    b.wait_ev("tensor", ev_cs4)
    f0_used = {}
    zs_used = {}
    sil_used = {}
    ev_ht_store = {}
    for nci in range(GN):
        wb2 = nci % 2
        b.wait_ev("sync", f0_used.get(wb2))
        evwg = b.dma("sync", f0_wg[wb2][:], Wg_t[nci].rearrange("p c n -> p (c n)"),
                     ch="f0w%d" % wb2)
        evwu = b.dma("sync", f0_wu[wb2][:], Wu_t[nci].rearrange("p c n -> p (c n)"),
                     ch="f0u%d" % wb2)
        b.wait_ev("sync", zs_used.get(wb2))
        b.dma("sync", zs4[wb2][0:32, :], zs_s["g"][:, nci * 128:(nci + 1) * 128],
              ch="f0z%d" % wb2)
        b.wait_ev("sync", zs_used.get(2 + wb2))
        evzu = b.dma("sync", zs4[2 + wb2][0:32, :],
                     zs_s["u"][:, nci * 128:(nci + 1) * 128], ch="f0z%d" % wb2)
        gv = f0_wg[wb2].rearrange("p (c n) -> p c n", n=128)
        uv = f0_wu[wb2].rearrange("p (c n) -> p c n", n=128)
        hb = nci % 2
        b.wait_ev("vector", ev_ht_store.get(hb))
        evt = None
        evv = None
        for tb in range(4):
            bkG, bkU = 2 * tb, 2 * tb + 1
            b.wait_ev("tensor", evwg)
            b.wait_ev("tensor", evwu)
            b.wait_ev("tensor", evzu)
            for bk_, wsl, zi in ((bkG, gv, wb2), (bkU, uv, 2 + wb2)):
                b.pe_bank_wait((bk_,))
                for kc in range(KC):
                    b.op("tensor", mm(bank(bk_), wsl[:, kc, :],
                                      fxv[:, kc, tb * 512:(tb + 1) * 512],
                                      kc == 0, False))
                evt = b.op("tensor", mm(bank(bk_), zs4[zi][0:32, :],
                                        cs4[0:32, tb * 512:(tb + 1) * 512],
                                        False, True), inc="eT")
                zs_used[zi] = evt
            sp = tb % 2
            b.wait_ev("scalar", evt)
            b.wait_ev("scalar", sil_used.get(sp))
            eva = b.op("scalar", lambda e, o=f0_sil[sp], i=bank(bkG):
                       e.activation(o, i, AF.Silu), inc="eA")
            b.bank_free[bkG] = eva
            b.wait_ev("vector", eva)
            evv = b.op("vector", lambda e, o=h_t[hb][:, tb * 512:(tb + 1) * 512],
                       i=f0_sil[sp], u=bank(bkU): e.tensor_tensor(o, i, u, ALU.mult),
                       inc="eV")
            b.bank_free[bkU] = evv
            sil_used[sp] = evv
        f0_used[wb2] = evt
        b.wait_ev("scalar", evv)
        ev_ht_store[hb] = b.dma("scalar", h_d[nci], h_t[hb][:], ch="hst%d" % hb)

    ev_f0_T = b.now("eT")
    b.wait_ev("sync", ev_f0_T)
    b.wait_ev("vector", ev_f0_T)
    for i in range(2):
        b.wait_ev("sync", ev_ht_store.get(i))
    b.op("vector", lambda e: e.memset(oh86[:], 0.0))
    oh86v = oh86.rearrange("p (c n) -> p c n", n=86)
    ev_oh86 = None
    for j in range(GN):
        ev_oh86 = b.op("vector", lambda e, o=oh86v[:, j, j:j + 1]: e.memset(o, 1.0),
                       inc="eV" if j == GN - 1 else None) or ev_oh86
    ev_g_store = {}
    f2h_used = None
    cs_free = None
    wd_used = {}
    for mb in range(NFB):
        m0 = mb * FB
        b.wait_ev("sync", f2h_used)
        for nci in range(GN):
            b.dma("sync", f2_h.rearrange("p (c n) -> p c n", n=FB)[:, nci, :],
                  h_d[nci][:, m0:m0 + FB], ch="f2h%d" % (nci % 4))
        ev_h = [b.qnow("f2h%d" % i) for i in range(4)]
        hv = f2_h.rearrange("p (c n) -> p c n", n=FB)
        for _e in ev_h:
            b.wait_ev("tensor", _e)
        b.wait_ev("tensor", ev_oh86)
        b.pe_bank_wait((7,))
        csh_evt = None
        for j in range(GN):
            csh_evt = b.op("tensor", mm(bank(7)[0:86, :], oh86v[:, j, :], hv[:, j, :],
                                        j == 0, j == GN - 1),
                           inc="eT" if j == GN - 1 else None) or csh_evt
        b.wait_ev("scalar", csh_evt)
        b.wait_ev("scalar", cs_free)
        ev_csh = b.op("scalar", lambda e, o=f2_cs[0:86, :], i=bank(7)[0:86, :]:
                      e.activation(o, i, AF.Identity, scale=-1.0), inc="eA")
        b.bank_free[7] = ev_csh
        wd_ev = {}

        def load_wd(oc2):
            sl = oc2 % 2
            evs = {0: [], 1: []}
            for hs in range(2):
                b.wait_ev("sync", wd_used.get((sl, hs)))
                for qd, (c0, cn) in enumerate(((0, 22), (22, 21))):
                    evs[hs].append(b.dma(
                        "sync", f2_wd[sl * 2 + hs][:, c0 * 128:(c0 + cn) * 128],
                        Wd_t[oc2, :, hs * 43 + c0:hs * 43 + c0 + cn, :]
                        .rearrange("p c n -> p (c n)"), ch="ffd%d" % (hs * 2 + qd)))
            wd_ev[oc2] = evs

        load_wd(0)
        load_wd(1)
        for oc in range(OC):
            sl = oc % 2
            evwd = wd_ev.pop(oc)
            b.wait_ev("sync", zs_used.get(oc % 2))
            evzd = b.dma("sync", zs4[oc % 2][0:86, :],
                         zs_s["d"][:, oc * 128:(oc + 1) * 128], ch="f0z%d" % (oc % 2))
            bk_ = 4 + oc % 2
            b.wait_ev("tensor", evzd)
            b.wait_ev("tensor", ev_csh)
            b.pe_bank_wait((bk_,))
            for nci in range(GN):
                hb2 = 0 if nci < 43 else 1
                if nci in (0, 43):
                    for _e in evwd[hb2]:
                        b.wait_ev("tensor", _e)
                dv = f2_wd[sl * 2 + hb2].rearrange("p (c n) -> p c n", n=128)
                inc = "eT" if nci in (42, GN - 1) else None
                ev_ = b.op("tensor", mm(bank(bk_), dv[:, nci - 43 * hb2, :],
                                        hv[:, nci, :], nci == 0, False), inc=inc)
                if nci == 42:
                    wd_used[(sl, 0)] = ev_
                elif nci == GN - 1:
                    wd_used[(sl, 1)] = ev_
            evt = b.op("tensor", mm(bank(bk_), zs4[oc % 2][0:86, :], f2_cs[0:86, :],
                                    False, True), inc="eT")
            zs_used[oc % 2] = evt
            cs_free = evt
            f2h_used = evt
            if oc + 2 < OC:
                load_wd(oc + 2)
            elif mb + 1 < NFB and oc + 2 - OC < 2:
                pass
            sp = oc % 2
            b.wait_ev("scalar", evt)
            b.wait_ev("scalar", ev_g_store.get(sp))
            eva = b.op("scalar", lambda e, o=f2_g[sp], i=bank(bk_):
                       e.activation(o, i, AF.Silu), inc="eA")
            b.bank_free[bk_] = eva
            ev_g_store[sp] = b.dma("scalar", g_s[oc * 128:(oc + 1) * 128,
                                                 mb * FB:(mb + 1) * FB], f2_g[sp],
                                   ch="ffgs%d" % sp)
    ev_g_all = [b.qnow("ffgs0"), b.qnow("ffgs1")]

    cb_used = {}
    for _e in ev_ctx_all + ev_g_all:
        b.wait_ev("sync", _e)
    for i in range(2):
        b.wait_ev("sync", denfree.get(i))
    for sci in range(16):
        bi = sci % 2
        l0 = sci * 128
        b.wait_ev("sync", cb_used.get(bi))
        ev_lds = [b.dma("sync", cb_x[bi], xd[l0:l0 + 128, :], ch="cbx%d" % bi),
                  b.dma("sync", cb_c[bi], ctx_tm[l0:l0 + 128, :], ch="cbc%d" % bi),
                  b.dma("sync", cb_dn[bi], den_d[l0:l0 + 128, :], ch="cbc%d" % bi)]
        for j in range(8):
            ev_ld = b.dma("sync", cb_g[bi].rearrange("p (c n) -> p c n", n=512)[:, j, :],
                          g_s[j * 512:(j + 1) * 512, l0:l0 + 128], transpose=True,
                          ch="cbg%d" % ((sci * 8 + j) % 4))
        ev_lds.append(b.qnow("cbg%d" % ((sci * 8 + 7) % 4)))
        for j in range(3):
            ev_lds.append(b.qnow("cbg%d" % j))
        for _e in ev_lds:
            b.wait_ev("vector", _e)
        b.wait_ev("vector", cb_used.get(("o", bi)))
        b.op("vector", lambda e, o=cb_r[bi], i=cb_dn[bi]: e.reciprocal(o, i))
        b.opb("vector", lambda e, o=cb_t, c=cb_c[bi], g=cb_g[bi]:
              e.tensor_tensor(o, c, g, ALU.mult), inc="eV")
        b.opb("vector", lambda e, o=cb_t2.rearrange("p (h d) -> p h d", d=128),
              t=cb_t.rearrange("p (h d) -> p h d", d=128),
              r=cb_r[bi].rearrange("p (h o) -> p h o", o=1).broadcast_to([128, H, 128]):
              e.tensor_tensor(o, t, r, ALU.mult), inc="eV")
        evv = b.op("vector", lambda e, o=cb_o[bi], t=cb_t2, x=cb_x[bi]:
                   e.tensor_tensor(o, t, x, ALU.add), inc="eV")
        cb_used[bi] = evv
        b.wait_ev("scalar", evv)
        cb_used[("o", bi)] = b.dma("scalar", outd[l0:l0 + 128, :], cb_o[bi],
                                   ch="cbo%d" % bi)

    for name, (dd, src) in dump.items():
        b.dma("sync", dd, src, ch="dump")

    for name in list(b.cnt):
        if name.startswith("q_"):
            eng = b.chan_eng.get(name, "sync")
            b.wait(eng, name, b.cnt[name])

    if not b.check_deadlock():
        raise RuntimeError("offline deadlock check failed")

    with nc.Block() as block:
        @block.sync
        def _(e):
            for f in b.ops["sync"]:
                f(e)

        @block.gpsimd
        def _(e):
            for f in b.ops["gpsimd"]:
                f(e)

        @block.tensor
        def _(e):
            for f in b.ops["tensor"]:
                f(e)

        @block.vector
        def _(e):
            for f in b.ops["vector"]:
                f(e)

        @block.scalar
        def _(e):
            for f in b.ops["scalar"]:
                f(e)

    return nc


def kernel(**inputs):
    f32 = np.float32
    x = np.ascontiguousarray(np.asarray(inputs["L0_attn_hidden_states"], dtype=f32))
    host = {}
    for w, kw in (("q", "L0_attn_wq"), ("k", "L0_attn_wk"), ("v", "L0_attn_wv")):
        host[f"w{w}T"] = np.ascontiguousarray(np.asarray(inputs[kw], dtype=f32).T)
    for w, kb in (("q", "L0_attn_bq"), ("k", "L0_attn_bk"), ("v", "L0_attn_bv")):
        host[f"b{w}t"] = np.ascontiguousarray(
            np.asarray(inputs[kb], dtype=f32).reshape(32, 128).T)
    host["identd"] = np.eye(128, dtype=f32)
    for pre, nm in (("g", "gate"), ("u", "up"), ("d", "down")):
        host[pre + "qw"] = np.ascontiguousarray(
            np.asarray(inputs[f"L0_ffn_{nm}_qweight"], dtype=np.int32))
        sc = np.ascontiguousarray(np.asarray(inputs[f"L0_ffn_{nm}_scales"], dtype=f32))
        host[pre + "sc"] = sc
        host[pre + "qz"] = np.ascontiguousarray(
            np.asarray(inputs[f"L0_ffn_{nm}_qzeros"], dtype=np.int32))

    nc = build_program()
    in_maps = []
    for c in range(8):
        m = dict(host)
        h = c % 2
        m["x"] = np.ascontiguousarray(x[c // 2][h * SOWN:(h + 1) * SOWN])
        in_maps.append(m)
    res = run_bass_kernel_spmd(nc, in_maps, core_ids=list(range(8)))
    out = np.empty((B, S, D), dtype=f32)
    for c in range(8):
        h = c % 2
        out[c // 2, h * SOWN:(h + 1) * SOWN, :] = res.results[c]["out"]
    return out


if __name__ == "__main__":
    pass
